# version 03
# baseline (speedup 1.0000x reference)
import contextlib
import os
import subprocess
import sys
import tempfile

import numpy as np

import concourse.bass as bass
import concourse.mybir as mybir
from concourse.bass import AP
from concourse.bass_utils import run_bass_kernel_spmd

F32 = mybir.dt.float32
AF = mybir.ActivationFunctionType
ALU = mybir.AluOpType
AX = mybir.AxisListType

S = 2048
D = 1024
NT = S // 128
NCH = D // 128
N_SUB = 12

_JAX_TABLE_SCRIPT = r"""
import sys, numpy as np
import jax
jax.config.update('jax_platforms', 'cpu')
jax.config.update('jax_enable_x64', True)
import jax.numpy as jnp
S, d, H = 2048, 64, 16
inv0 = 1.0 / 10000.0 ** (jnp.arange(0, d, 2, dtype=jnp.float32) / d)
t32 = jnp.arange(S, dtype=jnp.float32)
fr0 = jnp.einsum('i,j->ij', t32, inv0)
emb0 = jnp.concatenate([fr0, fr0], axis=-1)
cos0 = np.asarray(jnp.cos(emb0))
sin0 = np.asarray(jnp.sin(emb0))
gam = 1.0 - jnp.exp2(-5.0 - jnp.arange(H, dtype=jnp.float32))
pow0 = np.asarray(jnp.power(np.asarray(gam)[:, None], jnp.arange(S, dtype=jnp.float32)[None, :]))
np.save(sys.argv[1] + "c.npy", cos0)
np.save(sys.argv[1] + "s.npy", sin0)
np.save(sys.argv[1] + "p.npy", pow0)
"""

_CONST_CACHE = {}


def _xla_tables():
    if "t" in _CONST_CACHE:
        return _CONST_CACHE["t"]
    out = None
    try:
        with tempfile.TemporaryDirectory() as td:
            p = os.path.join(td, "t")
            env = {**os.environ, "JAX_PLATFORMS": "cpu"}
            r = subprocess.run([sys.executable, "-c", _JAX_TABLE_SCRIPT, p],
                               env=env, capture_output=True, timeout=900)
            if r.returncode == 0 and os.path.exists(p + "c.npy"):
                out = {"cos0": np.load(p + "c.npy"), "sin0": np.load(p + "s.npy"),
                       "pow0": np.load(p + "p.npy")}
    except Exception:
        out = None
    if out is None:
        d = 64
        inv0 = (1.0 / 10000.0 ** (np.arange(0, d, 2, dtype=np.float64) / d)).astype(np.float32)
        fr0 = (np.arange(S, dtype=np.float32)[:, None] * inv0[None, :])
        emb0 = np.concatenate([fr0, fr0], -1)
        cos0 = np.cos(emb0.astype(np.float64)).astype(np.float32)
        sin0 = np.sin(emb0.astype(np.float64)).astype(np.float32)
        gam = (1.0 - np.exp2(-5.0 - np.arange(16, dtype=np.float64)))
        pow0 = np.power(gam[:, None], np.arange(S, dtype=np.float64)[None, :]).astype(np.float32)
        out = dict(cos0=cos0, sin0=sin0, pow0=pow0)
    _CONST_CACHE["t"] = out
    return out


def host_constants():
    if "c" in _CONST_CACHE:
        return _CONST_CACHE["c"]
    t = _xla_tables()
    cos0, sin0, pow0 = t["cos0"], t["sin0"], t["pow0"]
    c = {}
    cos0q = (cos0.astype(np.float64) * 0.125).astype(np.float32)
    sin0q = (sin0.astype(np.float64) * 0.125).astype(np.float32)
    c["C_T0"] = np.concatenate([cos0q.T, sin0q.T], axis=0).astype(np.float32)
    c["C_T1"] = np.concatenate([cos0.T, sin0.T], axis=0).astype(np.float32)
    u = np.arange(128)[:, None]
    q = np.arange(128)[None, :]
    decT = np.zeros((16, 19, 128, 128), np.float32)
    for dd in range(19):
        idx = (dd - 3) * 128 + q - u
        valid = (idx >= 0) & (idx < S)
        ii = np.clip(idx, 0, S - 1)
        for hh in range(16):
            decT[hh, dd] = np.where(valid, pow0[hh][ii], np.float32(0.0))
    c["C_decTx"] = decT.reshape(16 * 19 * 128, 128)
    d = 64
    inv1 = (1.0 / 10000.0 ** (np.arange(0, d, 2, dtype=np.float64) / d)).astype(np.float32)
    fr1 = np.arange(S, dtype=np.float32)[:, None] * inv1[None, :]
    emb1 = np.concatenate([fr1, fr1], -1).astype(np.float32)
    c64 = np.cos(emb1.astype(np.float64))
    s64 = np.sin(emb1.astype(np.float64))
    ch = c64.astype(np.float32)
    cl = (c64 - ch.astype(np.float64)).astype(np.float32)
    sh = s64.astype(np.float32)
    sl = (s64 - sh.astype(np.float64)).astype(np.float32)
    cqh = (c64 * 0.125).astype(np.float32)
    cql = (c64 * 0.125 - cqh.astype(np.float64)).astype(np.float32)
    sqh = (s64 * 0.125).astype(np.float32)
    sql = (s64 * 0.125 - sqh.astype(np.float64)).astype(np.float32)
    c["C_U0h"] = np.concatenate([cqh.T, sqh.T], axis=0).astype(np.float32)
    c["C_U0l"] = np.concatenate([cql.T, sql.T], axis=0).astype(np.float32)
    c["C_U1h"] = np.concatenate([ch.T, sh.T], axis=0).astype(np.float32)
    c["C_U1l"] = np.concatenate([cl.T, sl.T], axis=0).astype(np.float32)
    _CONST_CACHE["c"] = c
    return c



class Bld:
    SEMS = ["pe", "dv", "ac", "gp", "wd", "xd", "st", "pd"]

    def __init__(self, nc, ctx):
        self.nc = nc
        self.ctx = ctx
        self.blk = ctx.enter_context(nc.Block())
        self.sem = {s: ctx.enter_context(nc.semaphore(s)) for s in self.SEMS}
        self.cnt = {s: 0 for s in self.SEMS}
        self.waited = {}
        self.x_dv = 0
        self.x_xd = 0

    def sec(self, eng, fn):
        getattr(self.blk, eng)(lambda e, fn=fn, eng=eng: fn(_E(self, eng, e)))

    def mil(self, s):
        return self.cnt[s]

    def expect(self, s, v):
        assert self.cnt[s] == v, f"sem {s}: ledger {self.cnt[s]} != expected {v}"

    def pe(self, fn):
        self.sec("tensor", fn)

    def dv(self, fn):
        self.sec("vector", fn)

    def ac(self, fn):
        self.sec("scalar", fn)

    def gp(self, fn):
        self.sec("gpsimd", fn)

    def sy(self, fn):
        self.sec("sync", fn)


class _E:
    def __init__(self, b, name, e):
        self.b = b
        self.name = name
        self.e = e

    def wait(self, s, v):
        if v is None or v <= 0:
            return
        k = (self.name, s)
        if v <= self.b.waited.get(k, 0):
            return
        self.e.wait_ge(self.b.sem[s], v)
        self.b.waited[k] = v

    def inc(self, ins, s, k=1):
        ins.then_inc(self.b.sem[s], k)
        self.b.cnt[s] += k
        return self.b.cnt[s]

    OWN = {"tensor": "pe", "vector": "dv", "scalar": "ac", "gpsimd": "gp"}

    def link(self, ins):
        own = self.OWN[self.name]
        v = self.inc(ins, own)
        self.wait(own, v)
        return v

    def __getattr__(self, a):
        return getattr(self.e, a)



def build_program(n_sub=12):
    nc = bass.Bass("TRN2", target_bir_lowering=False)
    cst = host_constants()
    dram = {}

    def din(name, shape):
        dram[name] = nc.dram_tensor(name, list(shape), F32, kind="ExternalInput").ap()

    din("x_in", (S, D))
    for k, v in cst.items():
        din(k, v.shape)
    for nm, shp in [
        ("L0_attn_wq", (D, D)), ("L0_attn_wk", (D, D)), ("L0_attn_wv", (D, D)),
        ("L0_attn_wg", (D, D)), ("L0_attn_wo", (D, D)),
        ("L0_ffn_W", (D, D)), ("L0_ffn_b", (1, D)),
        ("L0_ffn_lora_down", (4, 100)), ("L0_ffn_lora_up", (50, 4)),
        ("L0_ffn_down_aux", (100, D)), ("L0_ffn_up_aux", (D, 50)),
        ("L1_attn_Wq", (D, D)), ("L1_attn_Wkv", (D, D)), ("L1_attn_Wout", (D, D)),
        ("L1_attn_bout", (1, D)),
        ("L1_ffn_W_exp", (D, 4096)), ("L1_ffn_b_exp", (1, 4096)),
        ("L1_ffn_up_bias", (1, 4096)),
        ("L1_ffn_W_con", (4096, D)), ("L1_ffn_b_con", (1, D)), ("L1_ffn_down_bias", (1, D)),
        ("L2_attn_Wq", (D, D)), ("L2_attn_Wk", (D, D)), ("L2_attn_Wv", (D, D)),
        ("L2_attn_Wf", (D, D)), ("L2_attn_Wb", (D, 16)), ("L2_attn_A_log", (1, 16)),
        ("L2_attn_dt_bias", (1, D)), ("L2_attn_Wg", (D, D)), ("L2_attn_bg", (1, D)),
        ("L2_attn_Wo", (D, D)), ("L2_attn_norm_w", (1, 64)),
        ("L2_ffn_U1", (D, 256)), ("L2_ffn_V1", (256, 4096)), ("L2_ffn_b1", (1, 4096)),
        ("L2_ffn_U2", (4096, 256)), ("L2_ffn_V2", (256, D)), ("L2_ffn_b2", (1, D)),
        ("L3_attn_Wqkv", (D, D)), ("L3_attn_temp", (16, 1)),
        ("L3_attn_Wout", (D, D)), ("L3_attn_bout", (1, D)),
        ("L3_ffn_w_gate", (2816, D)), ("L3_ffn_w_up", (2816, D)), ("L3_ffn_w_down", (D, 2816)),
        ("BF_up_self", (12, 4096)), ("BF_up_cross", (12, 4096)),
        ("BF_dn_self", (12, 4096)), ("BF_dn_cross", (12, 4096)),
    ]:
        din(nm, shp)
    y_out = nc.dram_tensor("y", [S, D], F32, kind="ExternalOutput").ap()

    with contextlib.ExitStack() as ctx:
        def sb(name, shape):
            return ctx.enter_context(nc.sbuf_tensor(name, list(shape), F32))

        def psb(name, shape):
            return ctx.enter_context(nc.psum_tensor(name, list(shape), F32))

        xres = sb("xres", (128, NT * D))
        SA = sb("SA", (128, 16 * 1024))
        SB = sb("SB", (128, 8 * 1024))
        SC = sb("SC", (128, 5 * 1024))
        ident = sb("ident", (128, 128))
        iof = sb("iof", (128, 128))
        iop = sb("iop", (128, 1))
        onesr = sb("onesr", (1, 512))
        onesc = sb("onesc", (128, 1))
        stats = sb("stats", (128, 10 * 16))
        scrA = sb("scrA", (128, 1024))
        smalls = sb("smalls", (128, 256))
        SD = sb("SD", (128, 4480))
        halfr = sb("halfr", (1, 128))
        c6 = sb("c6", (128, 1))
        ps = [psb(f"ps{i}", (128, 512)) for i in range(8)]

        b = Bld(nc, ctx)

        def pre_gp(g):
            g.iota(iof[:], pattern=[[1, 128]], base=0, channel_multiplier=0,
                   allow_small_or_imprecise_dtypes=True)
            g.inc(g.iota(iop[:], pattern=[[0, 1]], base=0, channel_multiplier=1,
                         allow_small_or_imprecise_dtypes=True), "gp")
        b.gp(pre_gp)

        def pre_dv(v):
            v.wait("gp", 1)
            v.memset(onesr[0:1, :], 1.0)
            v.memset(halfr[0:1, :], 0.5)
            v.memset(onesc[:], 1.0)
            v.link(v.memset(scrA[:, 0:1], 6.0))
            v.link(v.reciprocal(c6[:], scrA[:, 0:1]))
            v.inc(v.tensor_scalar(ident[:], iof[:], iop[:], None, ALU.is_equal), "dv")
        b.dv(pre_dv)
        b.IDENT = b.mil("dv")
        b.x_dv = b.mil("dv")

        def pre_sy(sy):
            for i in range(NT):
                sy.inc(sy.dma_start(out=xres[:, i * D:(i + 1) * D],
                                    in_=dram["x_in"][i * 128:(i + 1) * 128, :]), "xd", 16)
        b.sy(pre_sy)
        b.x_xd = b.mil("xd")

        h = Helpers(b, nc, dram, dict(
            xres=xres, SA=SA, SB=SB, SC=SC, SD=SD, ident=ident, iof=iof, iop=iop,
            onesr=onesr, onesc=onesc, halfr=halfr, c6=c6, stats=stats, scrA=scrA,
            smalls=smalls, ps=ps))

        if globals().get("_STOP_STAGE") == "pre":
            n_sub = 0
        subs = [emit_l0ffn, emit_l0attn, emit_l0ffn,
                emit_l1ffn, emit_l1attn, emit_l1ffn,
                emit_l2ffn, emit_l2attn, emit_l2ffn,
                emit_l3ffn, emit_l3attn, emit_l3ffn]
        for i in range(min(n_sub, 12)):
            subs[i](h, i)

        if globals().get("_DUMP_STATS"):
            def dump2_sy(sy):
                sy.wait("dv", b.mil("dv"))
                sy.wait("ac", b.mil("ac"))
                sy.inc(sy.dma_start(out=y_out[0:128, 0:160], in_=stats[:, :]), "st", 16)
                sy.wait("st", b.cnt["st"])
            b.sy(dump2_sy)
            return nc
        if globals().get("_DUMP_SA"):
            def dump_sy(sy):
                sy.wait("dv", b.mil("dv"))
                sy.wait("ac", b.mil("ac"))
                sy.wait("pe", b.mil("pe"))
                for i in range(NT):
                    sy.inc(sy.dma_start(out=y_out[i * 128:(i + 1) * 128, :],
                                        in_=SA[:, i * 1024:(i + 1) * 1024]), "st", 16)
                sy.wait("st", b.cnt["st"])
            b.sy(dump_sy)
            return nc
        dv_dep, xd_dep = b.x_dv, b.x_xd

        def out_sy(sy):
            sy.wait("dv", dv_dep)
            sy.wait("xd", xd_dep)
            for i in range(NT):
                sy.inc(sy.dma_start(out=y_out[i * 128:(i + 1) * 128, :],
                                    in_=xres[:, i * D:(i + 1) * D]), "st", 16)
            sy.wait("st", b.cnt["st"])
        b.sy(out_sy)
    return nc


class Helpers:
    def __init__(self, b, nc, dram, env):
        self.b = b
        self.nc = nc
        self.dram = dram
        for k, v in env.items():
            setattr(self, k, v)

    def ln_stats(self):
        b, st, xres, scrA = self.b, self.stats, self.xres, self.scrA
        xdep, dvdep = b.x_xd, b.x_dv
        s1, s2 = st[:, 0:16], st[:, 16:32]
        m, rs = st[:, 32:48], st[:, 48:64]
        ex2, msq = st[:, 64:80], st[:, 80:96]
        var, veps, sq = st[:, 96:112], st[:, 112:128], st[:, 128:144]
        ac0 = b.mil("ac")

        def ac1(a):
            a.wait("xd", xdep)
            a.wait("dv", dvdep)
            for i in range(NT):
                if i >= 1:
                    a.wait("ac", ac0 + i)
                a.inc(a.activation(scrA[:, 0:1024], xres[:, i * D:(i + 1) * D],
                                   AF.Square, accum_out=s2[:, i:i + 1]), "ac")
        b.ac(ac1)
        ac_m = b.mil("ac")

        def dv1(v):
            v.wait("xd", xdep)
            v.wait("dv", dvdep)
            ins = None
            for i in range(NT):
                ins = v.reduce_sum(s1[:, i:i + 1], xres[:, i * D:(i + 1) * D], axis=AX.X)
            v.link(ins)
            v.wait("ac", ac_m)
            v.link(v.tensor_scalar(m[:], s1[:], 1.0 / 1024.0, None, ALU.mult))
            v.link(v.tensor_scalar(ex2[:], s2[:], 1.0 / 1024.0, None, ALU.mult))
            v.link(v.tensor_tensor(msq[:], m[:], m[:], ALU.mult))
            v.link(v.tensor_tensor(var[:], ex2[:], msq[:], ALU.subtract))
            v.inc(v.tensor_scalar(veps[:], var[:], 1e-5, None, ALU.add), "dv")
        b.dv(dv1)
        dv_m = b.mil("dv")

        def ac2(a):
            a.wait("dv", dv_m)
            a.inc(a.activation(sq[:], veps[:], AF.Sqrt), "ac")
        b.ac(ac2)
        ac_m2 = b.mil("ac")

        def dv2(v):
            v.wait("ac", ac_m2)
            v.inc(v.reciprocal(rs[:], sq[:]), "dv")
        b.dv(dv2)
        return m, rs, b.mil("dv")



def emit_l0ffn(h, sub):
    b, dram = h.b, h.dram
    SA, SB, SC, ps = h.SA, h.SB, h.SC, h.ps
    xres, ident, onesr, halfr = h.xres, h.ident, h.onesr, h.halfr
    SD = h.SD
    rowbuf = SD

    m, rs, ln_m = h.ln_stats()

    def loads(sy):
        sy.wait("dv", b.x_dv)
        for o in range(8):
            sy.inc(sy.dma_start(out=SB[:, o * 1024:(o + 1) * 1024],
                                in_=dram["L0_ffn_W"][o * 128:(o + 1) * 128, :]), "wd", 16)
        sy.inc(sy.dma_start(out=SC[0:4, 0:100], in_=dram["L0_ffn_lora_down"][:, :]), "wd", 16)
        sy.inc(sy.dma_start(out=SC[0:50, 128:132], in_=dram["L0_ffn_lora_up"][:, :]), "wd", 16)
        sy.inc(sy.dma_start(out=SC[0:100, 256:1280], in_=dram["L0_ffn_down_aux"][:, :]), "wd", 16)
        for c in range(8):
            sy.inc(sy.dma_start(out=SC[:, 1280 + c * 50:1280 + (c + 1) * 50],
                                in_=dram["L0_ffn_up_aux"][c * 128:(c + 1) * 128, :]), "wd", 16)
        sy.inc(sy.dma_start(out=SD[0:1, 0:1024], in_=dram["L0_ffn_b"][:, :]), "wd", 16)
    b.sy(loads)
    wd_m = b.mil("wd")
    if globals().get("_STOP_STAGE") == "loads":
        return

    ldT = SC[0:100, 1792:1796]
    luT = SC[0:4, 1900:1950]
    D1 = SC[0:4, 2048:3072]
    D2 = SC[0:50, 3072:4096]
    uaT = SC[0:50, 4096:5120]

    def pe1(t):
        t.wait("wd", wd_m)
        t.wait("dv", b.IDENT)
        t.transpose(ps[0][0:100, 0:4], SC[0:4, 0:100], ident[0:4, 0:4])
        t.inc(t.transpose(ps[1][0:4, 0:50], SC[0:50, 128:132], ident[0:50, 0:50]), "pe")
    b.pe(pe1)
    pe1m = b.mil("pe")

    def dv1(v):
        v.wait("pe", pe1m)
        v.tensor_copy(ldT, ps[0][0:100, 0:4])
        v.inc(v.tensor_copy(luT, ps[1][0:4, 0:50]), "dv")
    b.dv(dv1)
    dv1m = b.mil("dv")
    if globals().get("_STOP_STAGE") == "tp1":
        return

    def pe2(t):
        t.wait("dv", dv1m)
        for c in range(8):
            if c >= 2:
                t.wait("dv", dv1m + c - 1)
            t.inc(t.transpose(ps[2 + c % 2][0:50, 0:128],
                              SC[:, 1280 + c * 50:1280 + (c + 1) * 50], ident[:]), "pe")
        t.matmul(ps[4][0:4, 0:512], ldT, SC[0:100, 256:768], start=True, stop=True)
        t.inc(t.matmul(ps[5][0:4, 0:512], ldT, SC[0:100, 768:1280],
                       start=True, stop=True), "pe")
    b.pe(pe2)
    pe2m = b.mil("pe")

    def dv2(v):
        for c in range(8):
            v.wait("pe", pe2m - 9 + c + 1)
            v.inc(v.tensor_copy(uaT[:, c * 128:(c + 1) * 128], ps[2 + c % 2][0:50, 0:128]), "dv")
        v.wait("pe", pe2m)
        v.tensor_copy(D1[:, 0:512], ps[4][0:4, 0:512])
        v.inc(v.tensor_copy(D1[:, 512:1024], ps[5][0:4, 0:512]), "dv")
    b.dv(dv2)
    dv2m = b.mil("dv")
    if globals().get("_STOP_STAGE") == "tp2":
        return

    def pe3(t):
        t.wait("dv", dv2m)
        t.matmul(ps[4][0:50, 0:512], luT, D1[:, 0:512], start=True, stop=True)
        t.inc(t.matmul(ps[5][0:50, 0:512], luT, D1[:, 512:1024],
                       start=True, stop=True), "pe")
    b.pe(pe3)
    pe3m = b.mil("pe")

    def dv3(v):
        v.wait("pe", pe3m)
        v.tensor_copy(D2[:, 0:512], ps[4][0:50, 0:512])
        v.inc(v.tensor_copy(D2[:, 512:1024], ps[5][0:50, 0:512]), "dv")
    b.dv(dv3)
    dv3m = b.mil("dv")
    if globals().get("_STOP_STAGE") == "lilora":
        return

    def pe4(t):
        t.wait("dv", dv3m)
        for c in range(8):
            if c >= 2:
                t.wait("ac", ac4base + 2 * (c - 1))
            t.matmul(ps[4 + 2 * (c % 2)][:, 0:512], D2[:, c * 128:(c + 1) * 128],
                     uaT[:, 0:512], start=True, stop=True)
            t.inc(t.matmul(ps[5 + 2 * (c % 2)][:, 0:512], D2[:, c * 128:(c + 1) * 128],
                           uaT[:, 512:1024], start=True, stop=True), "pe")
    ac4base = b.mil("ac")
    b.pe(pe4)
    pe4m = b.mil("pe")

    def ac4(a):
        for c in range(8):
            a.wait("pe", pe4m - 8 + c + 1)
            for jo in range(2):
                a.inc(a.activation(SA[:, c * 1024 + jo * 512:c * 1024 + (jo + 1) * 512],
                                   ps[4 + 2 * (c % 2) + jo][:, 0:512],
                                   AF.Copy, scale=0.5), "ac")
    b.ac(ac4)
    ac4m = b.mil("ac")

    dv4base = b.mil("dv")

    def pe5(t):
        t.wait("ac", ac4m)
        k = 0
        for c in range(8):
            for o in range(8):
                if k >= 2:
                    t.wait("dv", dv4base + k - 1)
                t.inc(t.transpose(ps[k % 2][:, 0:128],
                                  SB[:, o * 1024 + c * 128:o * 1024 + (c + 1) * 128],
                                  ident[:]), "pe")
                k += 1
    b.pe(pe5)
    pe5m = b.mil("pe")

    def dv5(v):
        v.wait("ac", ac4m)
        k = 0
        for c in range(8):
            for o in range(8):
                v.wait("pe", pe5m - 64 + k + 1)
                dst = SA[:, c * 1024 + o * 128:c * 1024 + (o + 1) * 128]
                v.inc(v.scalar_tensor_tensor(dst, ps[k % 2][:, 0:128], 0.5, dst,
                                             ALU.mult, ALU.add), "dv")
                k += 1
    b.dv(dv5)
    wfe_m = b.mil("dv")



    if globals().get("_STOP_AFTER_WFE"):
        b.x_dv = b.mil("dv")
        return
    dv0 = b.mil("dv")
    pe0 = b.mil("pe")

    def dvloop(v):
        v.wait("dv", ln_m)
        for i in range(NT):
            xt = SC[:, (i % 2) * 1024:(i % 2 + 1) * 1024]
            xf = SC[:, 2048 + (i % 2) * 1024:2048 + (i % 2 + 1) * 1024]
            if i >= 2:
                v.wait("pe", pe0 + 10 * (i - 2) + 10)
            v.inc(v.tensor_scalar(xt, xres[:, i * D:(i + 1) * D],
                                  m[:, i:i + 1], rs[:, i:i + 1],
                                  ALU.subtract, ALU.mult), "dv")
            for c in range(8):
                v.wait("pe", pe0 + 10 * i + c + 1)
                v.inc(v.tensor_copy(xf[:, c * 128:(c + 1) * 128], ps[c % 2][:, 0:128]), "dv")
            for oh in range(2):
                v.wait("pe", pe0 + 10 * i + 9 + oh)
                dst = xres[:, i * D + oh * 512:i * D + (oh + 1) * 512]
                v.inc(v.tensor_tensor(dst, ps[2 + 2 * (i % 2) + oh][:, 0:512], dst,
                                      ALU.add), "dv")
    peT = None

    def peloop(t):
        t.wait("dv", wfe_m)
        t.wait("wd", wd_m)
        for i in range(NT):
            xt = SC[:, (i % 2) * 1024:(i % 2 + 1) * 1024]
            xf = SC[:, 2048 + (i % 2) * 1024:2048 + (i % 2 + 1) * 1024]
            t.wait("dv", dv0 + 11 * i + 1)
            for c in range(8):
                if i >= 1 or c >= 2:
                    t.wait("dv", dv0 + 11 * i + c)
                t.inc(t.transpose(ps[c % 2][:, 0:128], xt[:, c * 128:(c + 1) * 128],
                                  ident[:]), "pe")
            t.wait("dv", dv0 + 11 * i + 9)
            if i >= 2:
                t.wait("dv", dv0 + 11 * (i - 2) + 11)
            for oh in range(2):
                bank = ps[2 + 2 * (i % 2) + oh]
                for c in range(8):
                    t.matmul(bank[:, 0:512], xf[:, c * 128:(c + 1) * 128],
                             SA[:, c * 1024 + oh * 512:c * 1024 + (oh + 1) * 512],
                             start=(c == 0), stop=False)
                t.inc(t.matmul(bank[:, 0:512], halfr[0:1, 0:128],
                               SD[0:1, oh * 512:(oh + 1) * 512],
                               start=False, stop=True), "pe")
    b.pe(peloop)
    b.dv(dvloop)
    b.expect("pe", pe0 + 10 * NT)
    b.expect("dv", dv0 + 11 * NT)
    b.x_dv = b.mil("dv")


def emit_l0attn(h, sub):
    b, dram = h.b, h.dram
    SA, SB, SC, SD, ps = h.SA, h.SB, h.SC, h.SD, h.ps
    xres, ident, scrA = h.xres, h.ident, h.scrA
    ent = {s: b.mil(s) for s in Bld.SEMS}

    m, rs, ln_m = h.ln_stats()

    def loads(sy):
        for s in ("pe", "dv", "ac"):
            sy.wait(s, ent[s])
        sy.inc(sy.dma_start(out=SC[:, 0:2048], in_=dram["C_T0"][:, :]), "wd", 16)
        sy.inc(sy.dma_start(out=SC[:, 2048:4096], in_=dram["C_T1"][:, :]), "wd", 16)
    b.sy(loads)
    tab_m = b.mil("wd")

    dv0, pe0, ac0 = b.mil("dv"), b.mil("pe"), b.mil("ac")

    def dvA(v):
        for s in ("pe", "ac"):
            v.wait(s, ent[s])
        v.wait("dv", ln_m)
        for i in range(NT):
            xt = SD[:, (i % 2) * 1024:(i % 2 + 1) * 1024]
            if i >= 2:
                v.wait("pe", pe0 + 8 * (i - 1))
            v.inc(v.tensor_scalar(xt, xres[:, i * D:(i + 1) * D],
                                  m[:, i:i + 1], rs[:, i:i + 1],
                                  ALU.subtract, ALU.mult), "dv")

    def peA(t):
        t.wait("ac", ent["ac"])
        for i in range(NT):
            xt = SD[:, (i % 2) * 1024:(i % 2 + 1) * 1024]
            t.wait("dv", dv0 + i + 1)
            for c in range(8):
                k = 8 * i + c
                if k >= 2:
                    t.wait("ac", ac0 + k - 1)
                t.inc(t.transpose(ps[k % 2][:, 0:128], xt[:, c * 128:(c + 1) * 128],
                                  ident[:]), "pe")

    def acA(a):
        a.wait("dv", ent["dv"])
        a.wait("pe", ent["pe"])
        for i in range(NT):
            for c in range(8):
                k = 8 * i + c
                a.wait("pe", pe0 + k + 1)
                a.inc(a.activation(SA[:, c * 2048 + i * 128:c * 2048 + (i + 1) * 128],
                                   ps[k % 2][:, 0:128], AF.Copy), "ac")
    b.dv(dvA)
    b.pe(peA)
    b.ac(acA)
    xlnf_m = b.mil("ac")

    outbank_guard = {0: 0, 1: 0}
    WNM = ["L0_attn_wq", "L0_attn_wk", "L0_attn_wv", "L0_attn_wg", "L0_attn_wo"]
    wo3 = dram["L0_attn_wo"].rearrange("(c p) j -> p c j", p=128)
    dec3 = dram["C_decTx"].rearrange("(g u) q -> u g q", u=128)

    for p in range(8):
        wT_ms = {}
        for wi, nm in enumerate(WNM):
            dv_pre = b.mil("dv")

            def w_sy(sy, nm=nm, wi=wi, p=p):
                sy.wait("pe", b.mil("pe"))
                if wi == 4:
                    sy.inc(sy.dma_start(out=SD[:, 0:1024],
                                        in_=wo3[:, :, p * 128:(p + 1) * 128]), "wd", 16)
                else:
                    sy.inc(sy.dma_start(out=SD[:, 0:1024],
                                        in_=dram[nm][p * 128:(p + 1) * 128, :]), "wd", 16)
            b.sy(w_sy)
            wdm = b.mil("wd")
            pe_tb = b.mil("pe")

            def w_pe(t, wdm=wdm, dv_pre=dv_pre):
                t.wait("wd", wdm)
                t.wait("dv", dv_pre)
                for c in range(8):
                    if c >= 2:
                        t.wait("dv", dv_pre + c - 1)
                    t.inc(t.transpose(ps[6 + c % 2][:, 0:128],
                                      SD[:, c * 128:(c + 1) * 128], ident[:]), "pe")
            b.pe(w_pe)

            def w_dv(v, pe_tb=pe_tb, wi=wi):
                for c in range(8):
                    v.wait("pe", pe_tb + c + 1)
                    dst = (SC[:, 4096 + c * 128:4096 + (c + 1) * 128] if wi == 4
                           else SD[:, 1024 + c * 128:1024 + (c + 1) * 128])
                    v.inc(v.tensor_copy(dst, ps[6 + c % 2][:, 0:128]), "dv")
            b.dv(w_dv)
            wT_m = b.mil("dv")
            wT_ms[wi] = wT_m
            if wi == 4:
                break

            if wi in (0, 1, 3):
                dstb = {0: 0, 1: 2048, 3: 6144}[wi]
                pe_pb = b.mil("pe")
                dv_eb = b.mil("dv")
                ac_eb = b.mil("ac")
                evs_per_tile = 10 if wi in (0, 1) else 0

                def w_proj(t, wT_m=wT_m, pe_pb=pe_pb, dv_eb=dv_eb, ac_eb=ac_eb,
                           wi=wi, evs=evs_per_tile, ac_now=None):
                    t.wait("dv", wT_m)
                    t.wait("ac", ac_now if ac_now else xlnf_m)
                    t.wait("wd", tab_m)
                    for tt in range(4):
                        if tt >= 2:
                            if wi in (0, 1):
                                t.wait("dv", dv_eb + evs * (tt - 1))
                            else:
                                t.wait("ac", ac_eb + (tt - 1))
                        bank = ps[2 + tt % 2]
                        for c in range(7):
                            t.matmul(bank[:, 0:512],
                                     SD[:, 1024 + c * 128:1024 + (c + 1) * 128],
                                     SA[:, c * 2048 + tt * 512:c * 2048 + (tt + 1) * 512],
                                     start=(c == 0), stop=False)
                        t.inc(t.matmul(bank[:, 0:512],
                                       SD[:, 1024 + 7 * 128:1024 + 8 * 128],
                                       SA[:, 7 * 2048 + tt * 512:7 * 2048 + (tt + 1) * 512],
                                       start=False, stop=True), "pe")
                b.pe(lambda t, f=w_proj, a=b.mil("ac"): f(t, ac_now=a))
                pe_pm = b.mil("pe")

                if wi in (0, 1):
                    tb = 0 if wi == 0 else 2048

                    def w_rope(v, pe_pb=pe_pb, dstb=dstb, tb=tb):
                        for tt in range(4):
                            v.wait("pe", pe_pb + tt + 1)
                            bank = ps[2 + tt % 2]
                            cols = slice(tt * 512, (tt + 1) * 512)
                            tcols = slice(tb + tt * 512, tb + (tt + 1) * 512)
                            for hh in range(2):
                                base = 64 * hh
                                qv = SB[base:base + 64, dstb + tt * 512:dstb + (tt + 1) * 512]
                                qlo = SB[base:base + 32, dstb + tt * 512:dstb + (tt + 1) * 512]
                                qhi = SB[base + 32:base + 64, dstb + tt * 512:dstb + (tt + 1) * 512]
                                v.link(v.tensor_tensor(qv, bank[base:base + 64, 0:512],
                                                       SC[0:64, tcols], ALU.mult))
                                v.link(v.tensor_tensor(scrA[base:base + 32, 0:512],
                                                       bank[base + 32:base + 64, 0:512],
                                                       SC[64:96, tcols], ALU.mult))
                                v.link(v.tensor_tensor(qlo, qlo, scrA[base:base + 32, 0:512],
                                                       ALU.subtract))
                                v.link(v.tensor_tensor(scrA[base + 32:base + 64, 0:512],
                                                       bank[base:base + 32, 0:512],
                                                       SC[96:128, tcols], ALU.mult))
                                v.inc(v.tensor_tensor(qhi, qhi, scrA[base + 32:base + 64, 0:512],
                                                      ALU.add), "dv")
                    b.dv(w_rope)
                else:
                    def w_gev(a, pe_pb=pe_pb, dstb=dstb):
                        for tt in range(4):
                            a.wait("pe", pe_pb + tt + 1)
                            a.inc(a.activation(SB[:, dstb + tt * 512:dstb + (tt + 1) * 512],
                                               ps[2 + tt % 2][:, 0:512], AF.Silu), "ac")
                    b.ac(w_gev)
            else:
                pe_pb = b.mil("pe")
                ac_eb = b.mil("ac")

                def v_proj(t, wT_m=wT_m, pe_pb=pe_pb, ac_eb=ac_eb, ac_now=None):
                    t.wait("dv", wT_m)
                    t.wait("ac", ac_now if ac_now else xlnf_m)
                    for tt in range(NT):
                        if tt >= 2:
                            t.wait("ac", ac_eb + tt - 1)
                        bank = ps[2 + tt % 2]
                        for c in range(7):
                            t.matmul(bank[:, 0:128],
                                     SA[:, c * 2048 + tt * 128:c * 2048 + (tt + 1) * 128],
                                     SD[:, 1024 + c * 128:1024 + (c + 1) * 128],
                                     start=(c == 0), stop=False)
                        t.inc(t.matmul(bank[:, 0:128],
                                       SA[:, 7 * 2048 + tt * 128:7 * 2048 + (tt + 1) * 128],
                                       SD[:, 1024 + 7 * 128:1024 + 8 * 128],
                                       start=False, stop=True), "pe")
                b.pe(lambda t, f=v_proj, a=b.mil("ac"): f(t, ac_now=a))
                pe_pm = b.mil("pe")

                def v_ev(a, pe_pb=pe_pb):
                    for tt in range(NT):
                        a.wait("pe", pe_pb + tt + 1)
                        a.inc(a.activation(SB[:, 4096 + tt * 128:4096 + (tt + 1) * 128],
                                           ps[2 + tt % 2][:, 0:128], AF.Copy), "ac")
                b.ac(v_ev)

        qkvg_dv = b.mil("dv")
        qkvg_ac = b.mil("ac")

        for hh in range(2):
            head = 2 * p + hh
            base = 64 * hh

            def d_sy(sy, head=head):
                sy.wait("dv", b.mil("dv"))
                sy.inc(sy.dma_start(out=SD[:, 2048:4480],
                                    in_=dec3[:, head * 19:(head + 1) * 19, :]), "wd", 16)
            b.sy(d_sy)
            decm = b.mil("wd")

            for jq in range(4):
                nu = min(4 * jq + 4, NT)
                pe_sb = b.mil("pe")
                dv_wb = b.mil("dv")

                obg = outbank_guard[jq % 2]

                def s_pe(t, jq=jq, nu=nu, pe_sb=pe_sb, dv_wb=dv_wb, base=base,
                         decm=decm, qkvg_dv=qkvg_dv, qkvg_ac=qkvg_ac, obg=obg):
                    t.wait("dv", max(qkvg_dv, dv_wb, obg))
                    t.wait("ac", qkvg_ac)
                    t.wait("wd", decm)
                    for iu in range(nu):
                        if iu >= 2:
                            t.wait("dv", dv_wb + iu - 1)
                        t.inc(t.matmul(ps[iu % 2][:, 0:512],
                                       SB[base:base + 64, 2048 + iu * 128:2048 + (iu + 1) * 128],
                                       SB[base:base + 64, jq * 512:(jq + 1) * 512],
                                       start=True, stop=True), "pe")
                        t.wait("dv", dv_wb + iu + 1)
                        t.inc(t.matmul(ps[4 + jq % 2][0:64, 0:512],
                                       SB[:, 4096 + iu * 128 + base:4096 + iu * 128 + base + 64],
                                       scrA[:, (iu % 2) * 512:(iu % 2 + 1) * 512],
                                       start=(iu == 0), stop=(iu == nu - 1)), "pe")
                b.pe(s_pe)
                pe_sm = b.mil("pe")

                def s_dv(v, jq=jq, nu=nu, pe_sb=pe_sb, head=head):
                    for iu in range(nu):
                        v.wait("pe", pe_sb + 2 * iu + 1)
                        d0 = 4 * jq - iu + 3
                        v.inc(v.tensor_tensor(scrA[:, (iu % 2) * 512:(iu % 2 + 1) * 512],
                                              ps[iu % 2][:, 0:512],
                                              SD[:, 2048 + d0 * 128:2048 + (d0 + 4) * 128],
                                              ALU.mult), "dv")
                b.dv(s_dv)

                pe_all = b.mil("pe")

                def z_dv(v, jq=jq, base=base, pe_all=pe_all):
                    v.wait("pe", pe_all)
                    zsl = SB[base:base + 64, 6144 + jq * 512:6144 + (jq + 1) * 512]
                    v.inc(v.tensor_tensor(zsl, ps[4 + jq % 2][0:64, 0:512], zsl,
                                          ALU.mult), "dv")
                b.dv(z_dv)
                outbank_guard[jq % 2] = b.mil("dv")

        z_dvm = b.mil("dv")
        wo_m = wT_ms[4]
        pe_wb = b.mil("pe")
        dv_rb = b.mil("dv")

        def wo_pe(t, z_dvm=z_dvm, wo_m=wo_m, pe_wb=pe_wb, dv_rb=dv_rb):
            t.wait("dv", max(z_dvm, wo_m))
            t.wait("ac", b.mil("ac"))
            k = 0
            for tt in range(NT):
                for oh in range(2):
                    if k >= 2:
                        t.wait("dv", dv_rb + k - 1)
                    t.inc(t.matmul(ps[2 + k % 2][:, 0:512],
                                   SB[:, 6144 + tt * 128:6144 + (tt + 1) * 128],
                                   SC[:, 4096 + oh * 512:4096 + (oh + 1) * 512],
                                   start=True, stop=True), "pe")
                    k += 1
        b.pe(wo_pe)
        pe_wm = b.mil("pe")

        def wo_dv(v, pe_wb=pe_wb):
            k = 0
            for tt in range(NT):
                for oh in range(2):
                    v.wait("pe", pe_wb + k + 1)
                    dst = xres[:, tt * D + oh * 512:tt * D + (oh + 1) * 512]
                    v.inc(v.tensor_tensor(dst, ps[2 + k % 2][:, 0:512], dst,
                                          ALU.add), "dv")
                    k += 1
        b.dv(wo_dv)

    b.x_dv = b.mil("dv")


def emit_l1ffn(h, sub):
    if globals().get("_STUB_L1FFN", True):
        return
    raise NotImplementedError("L1 ffn real implementation")


def emit_l1attn(h, sub):
    b, dram = h.b, h.dram
    SA, SB, SC, SD, ps = h.SA, h.SB, h.SC, h.SD, h.ps
    xres, ident, scrA, onesr, onesc = h.xres, h.ident, h.scrA, h.onesr, h.onesc
    ent = {s: b.mil(s) for s in Bld.SEMS}

    m, rs, ln_m = h.ln_stats()

    def loads(sy):
        for s in ("pe", "dv", "ac"):
            sy.wait(s, ent[s])
        sy.inc(sy.dma_start(out=SC[:, 0:2048], in_=dram["C_U0h"][:, :]), "wd", 16)
        sy.inc(sy.dma_start(out=SC[:, 2048:4096], in_=dram["C_U1h"][:, :]), "wd", 16)
        sy.inc(sy.dma_start(out=SB[:, 0:2048], in_=dram["C_U0l"][:, :]), "wd", 16)
        sy.inc(sy.dma_start(out=SB[:, 2048:4096], in_=dram["C_U1l"][:, :]), "wd", 16)
        sy.inc(sy.dma_start(out=SC[0:1, 4096:5120], in_=dram["L1_attn_bout"][:, :]), "wd", 16)
    b.sy(loads)
    tab_m = b.mil("wd")

    def tabs_dv(v):
        v.wait("wd", tab_m)
        for s in ("pe", "ac"):
            v.wait(s, ent[s])
        v.link(v.tensor_tensor(SC[:, 0:2048], SC[:, 0:2048], SB[:, 0:2048], ALU.add))
        v.inc(v.tensor_tensor(SC[:, 2048:4096], SC[:, 2048:4096], SB[:, 2048:4096],
                              ALU.add), "dv")
    b.dv(tabs_dv)
    tabc_m = b.mil("dv")

    dv0, pe0, ac0 = b.mil("dv"), b.mil("pe"), b.mil("ac")

    def dvA(v):
        v.wait("dv", ln_m)
        for i in range(NT):
            xt = SD[:, (i % 2) * 1024:(i % 2 + 1) * 1024]
            if i >= 2:
                v.wait("pe", pe0 + 8 * (i - 1))
            v.inc(v.tensor_scalar(xt, xres[:, i * D:(i + 1) * D],
                                  m[:, i:i + 1], rs[:, i:i + 1],
                                  ALU.subtract, ALU.mult), "dv")

    def peA(t):
        t.wait("ac", ent["ac"])
        for i in range(NT):
            xt = SD[:, (i % 2) * 1024:(i % 2 + 1) * 1024]
            t.wait("dv", dv0 + i + 1)
            for c in range(8):
                k = 8 * i + c
                if k >= 2:
                    t.wait("ac", ac0 + k - 1)
                t.inc(t.transpose(ps[k % 2][:, 0:128], xt[:, c * 128:(c + 1) * 128],
                                  ident[:]), "pe")

    def acA(a):
        a.wait("dv", ent["dv"])
        a.wait("pe", ent["pe"])
        for i in range(NT):
            for c in range(8):
                k = 8 * i + c
                a.wait("pe", pe0 + k + 1)
                a.inc(a.activation(SA[:, c * 2048 + i * 128:c * 2048 + (i + 1) * 128],
                                   ps[k % 2][:, 0:128], AF.Copy), "ac")
    b.dv(dvA)
    b.pe(peA)
    b.ac(acA)
    xlnf_m = b.mil("ac")

    wq3 = dram["L1_attn_Wq"].rearrange("(c p) j -> p c j", p=128)
    wkv3 = dram["L1_attn_Wkv"].rearrange("(c p) j -> p c j", p=128)
    outbank_guard = {0: 0, 1: 0}
    sumbank_guard = {0: 0, 1: 0}

    for p in range(8):
        for wi, w3 in enumerate([wq3, wkv3]):
            sdoff = wi * 1024
            dv_pre = b.mil("dv")

            def w_sy(sy, w3=w3, sdoff=sdoff, p=p):
                sy.wait("pe", b.mil("pe"))
                sy.inc(sy.dma_start(out=SD[:, sdoff:sdoff + 1024],
                                    in_=w3[:, :, p * 128:(p + 1) * 128]), "wd", 16)
            b.sy(w_sy)
            wdm = b.mil("wd")
            pe_pb = b.mil("pe")
            dv_eb = b.mil("dv")

            def w_proj(t, wdm=wdm, sdoff=sdoff, pe_pb=pe_pb, dv_eb=dv_eb,
                       ac_now=b.mil("ac")):
                t.wait("wd", wdm)
                t.wait("dv", max(tabc_m, b.waited.get(("tensor", "dv"), 0), dv_eb))
                t.wait("ac", max(xlnf_m, ac_now))
                for tt in range(4):
                    if tt >= 2:
                        t.wait("dv", dv_eb + 10 * (tt - 1))
                    bank = ps[2 + tt % 2]
                    for c in range(7):
                        t.matmul(bank[:, 0:512], SD[:, sdoff + c * 128:sdoff + (c + 1) * 128],
                                 SA[:, c * 2048 + tt * 512:c * 2048 + (tt + 1) * 512],
                                 start=(c == 0), stop=False)
                    t.inc(t.matmul(bank[:, 0:512], SD[:, sdoff + 7 * 128:sdoff + 8 * 128],
                                   SA[:, 7 * 2048 + tt * 512:7 * 2048 + (tt + 1) * 512],
                                   start=False, stop=True), "pe")
            b.pe(w_proj)

            dstb = wi * 2048
            tb = 0 if wi == 0 else 2048

            def w_rope(v, pe_pb=pe_pb, dstb=dstb, tb=tb):
                for tt in range(4):
                    v.wait("pe", pe_pb + tt + 1)
                    bank = ps[2 + tt % 2]
                    tcols = slice(tb + tt * 512, tb + (tt + 1) * 512)
                    for hh in range(2):
                        base = 64 * hh
                        qv = SB[base:base + 64, dstb + tt * 512:dstb + (tt + 1) * 512]
                        qlo = SB[base:base + 32, dstb + tt * 512:dstb + (tt + 1) * 512]
                        qhi = SB[base + 32:base + 64, dstb + tt * 512:dstb + (tt + 1) * 512]
                        v.link(v.tensor_tensor(qv, bank[base:base + 64, 0:512],
                                               SC[0:64, tcols], ALU.mult))
                        v.link(v.tensor_tensor(scrA[base:base + 32, 0:512],
                                               bank[base + 32:base + 64, 0:512],
                                               SC[64:96, tcols], ALU.mult))
                        v.link(v.tensor_tensor(qlo, qlo, scrA[base:base + 32, 0:512],
                                               ALU.subtract))
                        v.link(v.tensor_tensor(scrA[base + 32:base + 64, 0:512],
                                               bank[base:base + 32, 0:512],
                                               SC[96:128, tcols], ALU.mult))
                        v.inc(v.tensor_tensor(qhi, qhi, scrA[base + 32:base + 64, 0:512],
                                              ALU.add), "dv")
            b.dv(w_rope)

        kvf_dv = b.mil("dv")
        pe_kb = b.mil("pe")
        ac_kb = b.mil("ac")

        def kv_pe(t, kvf_dv=kvf_dv, ac_kb=ac_kb):
            t.wait("dv", kvf_dv)
            for tt in range(NT):
                if tt >= 2:
                    t.wait("ac", ac_kb + tt - 1)
                t.inc(t.transpose(ps[2 + tt % 2][:, 0:128],
                                  SB[:, 2048 + tt * 128:2048 + (tt + 1) * 128],
                                  ident[:]), "pe")
        b.pe(kv_pe)

        def kv_ac(a, pe_kb=pe_kb):
            for tt in range(NT):
                a.wait("pe", pe_kb + tt + 1)
                a.inc(a.activation(SB[:, 4096 + tt * 128:4096 + (tt + 1) * 128],
                                   ps[2 + tt % 2][:, 0:128], AF.Copy), "ac")
        b.ac(kv_ac)

        def wo_sy(sy, p=p):
            sy.wait("pe", b.mil("pe"))
            sy.inc(sy.dma_start(out=SD[:, 0:1024],
                                in_=dram["L1_attn_Wout"][p * 128:(p + 1) * 128, :]), "wd", 16)
        b.sy(wo_sy)
        wout_m = b.mil("wd")
        qkv_dv = b.mil("dv")
        qkv_ac = b.mil("ac")

        for hh in range(2):
            base = 64 * hh
            for jq in range(4):
                pe_sb = b.mil("pe")
                ac_eb = b.mil("ac")
                obg = outbank_guard[jq % 2]
                sbg = sumbank_guard[jq % 2]

                def s_pe(t, jq=jq, base=base, pe_sb=pe_sb, ac_eb=ac_eb,
                         obg=obg, sbg=sbg, qkv_dv=qkv_dv, qkv_ac=qkv_ac):
                    t.wait("dv", max(qkv_dv, obg, sbg))
                    t.wait("ac", qkv_ac)
                    for iu in range(NT):
                        if iu >= 2:
                            t.wait("ac", ac_eb + iu - 1)
                        t.inc(t.matmul(ps[iu % 2][:, 0:512],
                                       SB[base:base + 64, 2048 + iu * 128:2048 + (iu + 1) * 128],
                                       SB[base:base + 64, jq * 512:(jq + 1) * 512],
                                       start=True, stop=True), "pe")
                        t.wait("ac", ac_eb + iu + 1)
                        t.matmul(ps[4 + jq % 2][0:64, 0:512],
                                 SB[:, 4096 + iu * 128 + base:4096 + iu * 128 + base + 64],
                                 scrA[:, (iu % 2) * 512:(iu % 2 + 1) * 512],
                                 start=(iu == 0), stop=(iu == NT - 1))
                        t.inc(t.matmul(ps[6 + jq % 2][0:1, 0:512], onesc[:],
                                       scrA[:, (iu % 2) * 512:(iu % 2 + 1) * 512],
                                       start=(iu == 0), stop=(iu == NT - 1)), "pe")
                b.pe(s_pe)
                pe_sm = b.mil("pe")

                def s_ac(a, pe_sb=pe_sb, jq=jq):
                    for iu in range(NT):
                        a.wait("pe", pe_sb + 2 * iu + 1)
                        a.inc(a.activation(scrA[:, (iu % 2) * 512:(iu % 2 + 1) * 512],
                                           ps[iu % 2][:, 0:512], AF.Exp), "ac")
                b.ac(s_ac)

                dv_rb = b.mil("dv")

                def e_dv1(v, pe_sm=pe_sm, jq=jq):
                    v.wait("pe", pe_sm)
                    v.inc(v.reciprocal(SD[0:1, 2048:2560], ps[6 + jq % 2][0:1, 0:512]), "dv")
                b.dv(e_dv1)
                rec_m = b.mil("dv")
                pe_bb = b.mil("pe")

                def e_pe(t, rec_m=rec_m, base=base):
                    t.wait("dv", rec_m)
                    t.inc(t.matmul(ps[2][base:base + 64, 0:512], onesr[0:1, 0:64],
                                   SD[0:1, 2048:2560], start=True, stop=True), "pe")
                b.pe(e_pe)
                pe_bm = b.mil("pe")

                def e_dv2(v, pe_bm=pe_bm, base=base, jq=jq):
                    v.wait("pe", pe_bm)
                    v.link(v.tensor_copy(SD[base:base + 64, 2560:3072],
                                         ps[2][base:base + 64, 0:512]))
                    zsl = SB[base:base + 64, 6144 + jq * 512:6144 + (jq + 1) * 512]
                    v.inc(v.tensor_tensor(zsl, ps[4 + jq % 2][0:64, 0:512],
                                          SD[base:base + 64, 2560:3072], ALU.mult), "dv")
                b.dv(e_dv2)
                outbank_guard[jq % 2] = b.mil("dv")
                sumbank_guard[jq % 2] = rec_m

        z_dvm = b.mil("dv")
        pe_wb = b.mil("pe")
        dv_rb = b.mil("dv")

        def wo_pe(t, z_dvm=z_dvm, wout_m=wout_m, pe_wb=pe_wb, dv_rb=dv_rb,
                  p=p, ac_now=b.mil("ac")):
            t.wait("dv", z_dvm)
            t.wait("wd", wout_m)
            t.wait("ac", ac_now)
            k = 0
            for tt in range(NT):
                for oh in range(2):
                    if k >= 2:
                        t.wait("dv", dv_rb + k - 1)
                    bank = ps[2 + k % 2]
                    if p == 0:
                        t.matmul(bank[:, 0:512],
                                 SB[:, 6144 + tt * 128:6144 + (tt + 1) * 128],
                                 SD[:, oh * 512:(oh + 1) * 512],
                                 start=True, stop=False)
                        t.inc(t.matmul(bank[:, 0:512], onesr[0:1, 0:128],
                                       SC[0:1, 4096 + oh * 512:4096 + (oh + 1) * 512],
                                       start=False, stop=True), "pe")
                    else:
                        t.inc(t.matmul(bank[:, 0:512],
                                       SB[:, 6144 + tt * 128:6144 + (tt + 1) * 128],
                                       SD[:, oh * 512:(oh + 1) * 512],
                                       start=True, stop=True), "pe")
                    k += 1
        b.pe(wo_pe)
        pe_wm = b.mil("pe")

        def wo_dv(v, pe_wb=pe_wb):
            k = 0
            for tt in range(NT):
                for oh in range(2):
                    v.wait("pe", pe_wb + k + 1)
                    dst = xres[:, tt * D + oh * 512:tt * D + (oh + 1) * 512]
                    v.inc(v.tensor_tensor(dst, ps[2 + k % 2][:, 0:512], dst,
                                          ALU.add), "dv")
                    k += 1
        b.dv(wo_dv)

    b.x_dv = b.mil("dv")


def emit_l2ffn(h, sub):
    b, dram = h.b, h.dram
    SA, SB, SC, SD, ps = h.SA, h.SB, h.SC, h.SD, h.ps
    xres, ident, onesr, smalls = h.xres, h.ident, h.onesr, h.smalls
    ent = {s: b.mil(s) for s in Bld.SEMS}

    m, rs, ln_m = h.ln_stats()

    def loads(sy):
        for s in ("pe", "dv", "ac"):
            sy.wait(s, ent[s])
        for c in range(8):
            sy.inc(sy.dma_start(out=SB[:, c * 256:(c + 1) * 256],
                                in_=dram["L2_ffn_U1"][c * 128:(c + 1) * 128, :]), "wd", 16)
        for rc in range(2):
            sy.inc(sy.dma_start(out=SA[:, rc * 4096:(rc + 1) * 4096],
                                in_=dram["L2_ffn_V1"][rc * 128:(rc + 1) * 128, :]), "wd", 16)
        for hc in range(32):
            sy.inc(sy.dma_start(out=SA[:, 8192 + hc * 256:8192 + (hc + 1) * 256],
                                in_=dram["L2_ffn_U2"][hc * 128:(hc + 1) * 128, :]), "wd", 16)
        for rc in range(2):
            sy.inc(sy.dma_start(out=SB[:, 2048 + rc * 1024:2048 + (rc + 1) * 1024],
                                in_=dram["L2_ffn_V2"][rc * 128:(rc + 1) * 128, :]), "wd", 16)
        sy.inc(sy.dma_start(out=SC[0:1, 0:4096], in_=dram["L2_ffn_b1"][:, :]), "wd", 16)
        sy.inc(sy.dma_start(out=SD[0:1, 0:1024], in_=dram["L2_ffn_b2"][:, :]), "wd", 16)
    b.sy(loads)
    wd_m = b.mil("wd")

    pe_bb = b.mil("pe")
    dv_bb = b.mil("dv")

    def b1_pe(t):
        t.wait("wd", wd_m)
        t.wait("dv", max(ent["dv"], b.x_dv))
        t.wait("ac", ent["ac"])
        for hc in range(32):
            if hc >= 2:
                t.wait("dv", dv_bb + hc - 1)
            t.inc(t.transpose(ps[hc % 2][0:128, 0:1],
                              SC[0:1, hc * 128:(hc + 1) * 128], ident[0:1, 0:1]), "pe")
    b.pe(b1_pe)

    def b1_dv(v):
        for hc in range(32):
            v.wait("pe", pe_bb + hc + 1)
            v.inc(v.tensor_copy(smalls[:, hc:hc + 1], ps[hc % 2][0:128, 0:1]), "dv")
    b.dv(b1_dv)
    b1_m = b.mil("dv")

    dv0, pe0, ac0 = b.mil("dv"), b.mil("pe"), b.mil("ac")
    DVN, PEN, ACN = 7, 46, 40

    def dvloop(v):
        v.wait("dv", max(ln_m, b1_m))
        for i in range(NT):
            xt = SC[:, (i % 2) * 1024:(i % 2 + 1) * 1024]
            pfo = 4096 + (i % 2) * 128
            qfo = 4608 + (i % 2) * 128
            if i >= 2:
                v.wait("pe", pe0 + PEN * (i - 1))
            v.inc(v.tensor_scalar(xt, xres[:, i * D:(i + 1) * D],
                                  m[:, i:i + 1], rs[:, i:i + 1],
                                  ALU.subtract, ALU.mult), "dv")
            for rc in range(2):
                v.wait("pe", pe0 + PEN * i + 8 + rc + 1)
                v.inc(v.tensor_copy(SC[:, 4096 + (i % 2) * 256 + rc * 128:
                                       4096 + (i % 2) * 256 + (rc + 1) * 128],
                                    ps[2 + rc][:, 0:128]), "dv")
            for rc in range(2):
                v.wait("pe", pe0 + PEN * i + 42 + rc + 1)
                v.inc(v.tensor_copy(SC[:, 4608 + (i % 2) * 256 + rc * 128:
                                       4608 + (i % 2) * 256 + (rc + 1) * 128],
                                    ps[6 + rc][:, 0:128]), "dv")
            for oh in range(2):
                v.wait("pe", pe0 + PEN * i + 44 + oh + 1)
                dst = xres[:, i * D + oh * 512:i * D + (oh + 1) * 512]
                v.inc(v.scalar_tensor_tensor(dst, ps[2 + oh][:, 0:512], 0.5, dst,
                                             ALU.mult, ALU.add), "dv")

    def peloop(t):
        t.wait("dv", b1_m)
        t.wait("wd", wd_m)
        for i in range(NT):
            xt = SC[:, (i % 2) * 1024:(i % 2 + 1) * 1024]
            xfb = 2048 + (i % 2) * 1024
            pfb = 4096 + (i % 2) * 256
            qfb = 4608 + (i % 2) * 256
            dvb = dv0 + DVN * i
            acb = ac0 + ACN * i
            t.wait("dv", dvb + 1)
            for c in range(8):
                k = 8 * i + c
                if k >= 2:
                    t.wait("ac", ac0 + ACN * (i if c >= 2 else i - 1) +
                           (c - 2 + 1 if c >= 2 else 7 + c))
                t.inc(t.transpose(ps[k % 2][:, 0:128], xt[:, c * 128:(c + 1) * 128],
                                  ident[:]), "pe")
            t.wait("ac", acb + 8)
            if i >= 1:
                t.wait("dv", dv0 + DVN * (i - 1) + 7)
            for rc in range(2):
                for c in range(7):
                    t.matmul(ps[2 + rc][:, 0:128],
                             SB[:, c * 256 + rc * 128:c * 256 + (rc + 1) * 128],
                             SC[:, xfb + c * 128:xfb + (c + 1) * 128],
                             start=(c == 0), stop=False)
                t.inc(t.matmul(ps[2 + rc][:, 0:128],
                               SB[:, 7 * 256 + rc * 128:7 * 256 + (rc + 1) * 128],
                               SC[:, xfb + 7 * 128:xfb + 8 * 128],
                               start=False, stop=True), "pe")
            t.wait("dv", dvb + 3)
            for hc in range(32):
                kk = 32 * i + hc
                if kk >= 2:
                    t.wait("ac", ac0 + ACN * (i if hc >= 2 else i - 1) +
                           (8 + hc - 1 if hc >= 2 else 39 + hc))
                t.matmul(ps[4 + hc % 2][:, 0:128],
                         SA[:, hc * 128:(hc + 1) * 128],
                         SC[:, pfb:pfb + 128], start=True, stop=False)
                t.inc(t.matmul(ps[4 + hc % 2][:, 0:128],
                               SA[:, 4096 + hc * 128:4096 + (hc + 1) * 128],
                               SC[:, pfb + 128:pfb + 256],
                               start=False, stop=True), "pe")
            t.wait("ac", acb + 40)
            for rc in range(2):
                for hc in range(31):
                    t.matmul(ps[6 + rc][:, 0:128],
                             SA[:, 8192 + hc * 256 + rc * 128:8192 + hc * 256 + (rc + 1) * 128],
                             SB[:, 4096 + hc * 128:4096 + (hc + 1) * 128],
                             start=(hc == 0), stop=False)
                t.inc(t.matmul(ps[6 + rc][:, 0:128],
                               SA[:, 8192 + 31 * 256 + rc * 128:8192 + 31 * 256 + (rc + 1) * 128],
                               SB[:, 4096 + 31 * 128:4096 + 32 * 128],
                               start=False, stop=True), "pe")
            t.wait("dv", dvb + 5)
            for oh in range(2):
                for rc in range(2):
                    t.matmul(ps[2 + oh][:, 0:512],
                             SC[:, qfb + rc * 128:qfb + (rc + 1) * 128],
                             SB[:, 2048 + rc * 1024 + oh * 512:2048 + rc * 1024 + (oh + 1) * 512],
                             start=(rc == 0), stop=False)
                t.inc(t.matmul(ps[2 + oh][:, 0:512], onesr[0:1, 0:128],
                               SD[0:1, oh * 512:(oh + 1) * 512],
                               start=False, stop=True), "pe")

    def acloop(a):
        a.wait("wd", wd_m)
        a.wait("dv", b1_m)
        for i in range(NT):
            xfb = 2048 + (i % 2) * 1024
            acb = ac0 + ACN * i
            for c in range(8):
                a.wait("pe", pe0 + PEN * i + c + 1)
                a.inc(a.activation(SC[:, xfb + c * 128:xfb + (c + 1) * 128],
                                   ps[(8 * i + c) % 2][:, 0:128], AF.Copy), "ac")
            for hc in range(32):
                a.wait("pe", pe0 + PEN * i + 10 + hc + 1)
                a.inc(a.activation(SB[:, 4096 + hc * 128:4096 + (hc + 1) * 128],
                                   ps[4 + hc % 2][:, 0:128], AF.Gelu_apprx_tanh,
                                   bias=smalls[:, hc:hc + 1]), "ac")
    b.pe(peloop)
    b.dv(dvloop)
    b.ac(acloop)
    b.expect("pe", pe0 + PEN * NT)
    b.expect("dv", dv0 + DVN * NT)
    b.expect("ac", ac0 + ACN * NT)
    b.x_dv = b.mil("dv")


def emit_l2attn(h, sub):
    b, dram = h.b, h.dram
    SA, SB, SC, SD, ps = h.SA, h.SB, h.SC, h.SD, h.ps
    xres, ident, scrA, onesr, onesc, smalls, iop = (
        h.xres, h.ident, h.scrA, h.onesr, h.onesc, h.smalls, h.iop)
    ent = {s: b.mil(s) for s in Bld.SEMS}

    m, rs, ln_m = h.ln_stats()

    def loads(sy):
        for s in ("pe", "dv", "ac"):
            sy.wait(s, ent[s])
        sy.inc(sy.dma_start(out=smalls[0:1, 176:192], in_=dram["L2_attn_A_log"][:, :]), "wd", 16)
        sy.inc(sy.dma_start(out=SC[0:1, 2176:3200], in_=dram["L2_attn_dt_bias"][:, :]), "wd", 16)
        sy.inc(sy.dma_start(out=SC[0:1, 3200:3264], in_=dram["L2_attn_norm_w"][:, :]), "wd", 16)
        sy.inc(sy.dma_start(out=SC[0:1, 4096:5120], in_=dram["L2_attn_bg"][:, :]), "wd", 16)
        for c in range(8):
            sy.inc(sy.dma_start(out=SC[:, 2048 + c * 16:2048 + (c + 1) * 16],
                                in_=dram["L2_attn_Wb"][c * 128:(c + 1) * 128, :]), "wd", 16)
    b.sy(loads)
    wd_m = b.mil("wd")

    dv0, pe0, ac0 = b.mil("dv"), b.mil("pe"), b.mil("ac")

    def dvA(v):
        v.wait("dv", ln_m)
        for i in range(NT):
            xt = SD[:, (i % 2) * 1024:(i % 2 + 1) * 1024]
            if i >= 2:
                v.wait("pe", pe0 + 8 * (i - 1))
            v.inc(v.tensor_scalar(xt, xres[:, i * D:(i + 1) * D],
                                  m[:, i:i + 1], rs[:, i:i + 1],
                                  ALU.subtract, ALU.mult), "dv")

    def peA(t):
        t.wait("ac", ent["ac"])
        for i in range(NT):
            xt = SD[:, (i % 2) * 1024:(i % 2 + 1) * 1024]
            t.wait("dv", dv0 + i + 1)
            for c in range(8):
                k = 8 * i + c
                if k >= 2:
                    t.wait("ac", ac0 + k - 1)
                t.inc(t.transpose(ps[k % 2][:, 0:128], xt[:, c * 128:(c + 1) * 128],
                                  ident[:]), "pe")

    def acA(a):
        a.wait("dv", ent["dv"])
        a.wait("pe", ent["pe"])
        for i in range(NT):
            for c in range(8):
                k = 8 * i + c
                a.wait("pe", pe0 + k + 1)
                a.inc(a.activation(SA[:, c * 2048 + i * 128:c * 2048 + (i + 1) * 128],
                                   ps[k % 2][:, 0:128], AF.Copy), "ac")
    b.dv(dvA)
    b.pe(peA)
    b.ac(acA)
    xlnf_m = b.mil("ac")

    def prep_ac(a):
        a.wait("wd", wd_m)
        a.wait("dv", ent["dv"])
        a.wait("pe", ent["pe"])
        a.link(a.activation(smalls[0:1, 192:208], smalls[0:1, 176:192], AF.Exp))
        a.inc(a.mul(smalls[0:1, 208:224], smalls[0:1, 192:208], -1.0), "ac")
    b.ac(prep_ac)
    lnA_m = b.mil("ac")

    def prep_gp(g):
        g.wait("ac", lnA_m)
        g.inc(g.iota(SD[:, 4224:4352], pattern=[[1, 128]], base=-1, channel_multiplier=-1,
                     allow_small_or_imprecise_dtypes=True), "gp")
    b.gp(prep_gp)
    gp_m = b.mil("gp")

    pe_pr = b.mil("pe")

    def prep_pe(t):
        t.wait("ac", max(lnA_m, xlnf_m))
        t.inc(t.matmul(ps[2][:, 0:16], onesr[0:1, 0:128], smalls[0:1, 208:224],
                       start=True, stop=True), "pe")
        t.inc(t.matmul(ps[3][:, 0:64], onesr[0:1, 0:128], SC[0:1, 3200:3264],
                       start=True, stop=True), "pe")
    b.pe(prep_pe)

    def prep_dv(v):
        v.wait("pe", pe_pr + 2)
        v.wait("gp", gp_m)
        v.link(v.tensor_copy(smalls[:, 0:16], ps[2][:, 0:16]))
        v.link(v.tensor_copy(smalls[:, 32:96], ps[3][:, 0:64]))
        v.link(v.tensor_scalar(SD[:, 4352:4480], SD[:, 4224:4352], -1.0, None, ALU.max))
        v.link(v.tensor_scalar(SD[:, 4096:4224], SD[:, 4224:4352], 0.0, None, ALU.is_ge))
        v.inc(v.tensor_scalar(smalls[:, 96:97], iop[:], -1.0, 127.0, ALU.mult, ALU.add), "dv")
    b.dv(prep_dv)
    prep_m = b.mil("dv")

    pe_bb = b.mil("pe")
    ac_bb = b.mil("ac")

    def beta_pe(t):
        t.wait("dv", prep_m)
        t.wait("wd", wd_m)
        for tt in range(NT):
            if tt >= 2:
                t.wait("ac", ac_bb + tt - 1)
            bank = ps[4 + tt % 2]
            for c in range(7):
                t.matmul(bank[:, 0:16], SA[:, c * 2048 + tt * 128:c * 2048 + (tt + 1) * 128],
                         SC[:, 2048 + c * 16:2048 + (c + 1) * 16], start=(c == 0), stop=False)
            t.inc(t.matmul(bank[:, 0:16], SA[:, 7 * 2048 + tt * 128:7 * 2048 + (tt + 1) * 128],
                           SC[:, 2048 + 7 * 16:2048 + 8 * 16], start=False, stop=True), "pe")
    b.pe(beta_pe)

    def beta_ac(a):
        for tt in range(NT):
            a.wait("pe", pe_bb + tt + 1)
            a.inc(a.activation(SD[:, 2112 + tt * 16:2112 + (tt + 1) * 16],
                               ps[4 + tt % 2][:, 0:16], AF.Sigmoid), "ac")
    b.ac(beta_ac)
    beta_m = b.mil("ac")

    wq3 = dram["L2_attn_Wq"].rearrange("(c p) j -> p c j", p=128)
    wk3 = dram["L2_attn_Wk"].rearrange("(c p) j -> p c j", p=128)
    wv3 = dram["L2_attn_Wv"].rearrange("(c p) j -> p c j", p=128)
    wf3 = dram["L2_attn_Wf"].rearrange("(c p) j -> p c j", p=128)
    wg3 = dram["L2_attn_Wg"].rearrange("(c p) j -> p c j", p=128)

    for p in range(8):
        for wi, (w3, dstb, actf) in enumerate([
                (wq3, 0, AF.Silu), (wk3, 2048, AF.Silu), (wv3, 4096, AF.Silu),
                (wf3, 6144, AF.Sigmoid)]):
            sco = (wi % 2) * 1024

            def pw_sy(sy, w3=w3, sco=sco, p=p):
                sy.wait("pe", b.mil("pe"))
                sy.inc(sy.dma_start(out=SC[:, sco:sco + 1024],
                                    in_=w3[:, :, p * 128:(p + 1) * 128]), "wd", 16)
            b.sy(pw_sy)
            wdm = b.mil("wd")

            if wi == 3:
                def ndt_ac(a, p=p, dvn=b.mil("dv")):
                    a.wait("dv", dvn)
                    a.inc(a.mul(scrA[0:1, 512:640],
                                SC[0:1, 2176 + p * 128:2176 + (p + 1) * 128], -1.0), "ac")
                b.ac(ndt_ac)
            ndt_m = b.mil("ac")
            pe_qb = b.mil("pe")
            ac_qb = b.mil("ac")

            def pw_pe(t, wdm=wdm, sco=sco, wi=wi, pe_qb=pe_qb, ac_qb=ac_qb,
                      dvn=b.mil("dv"), acn=b.mil("ac"), ndt_m=ndt_m):
                t.wait("wd", wdm)
                if wi == 3:
                    t.wait("ac", ndt_m)
                t.wait("dv", dvn)
                t.wait("ac", max(xlnf_m, beta_m, acn))
                for tt in range(NT):
                    if tt >= 2:
                        t.wait("ac", ac_qb + tt - 1)
                    bank = ps[4 + tt % 2]
                    for c in range(7):
                        t.matmul(bank[:, 0:128],
                                 SA[:, c * 2048 + tt * 128:c * 2048 + (tt + 1) * 128],
                                 SC[:, sco + c * 128:sco + (c + 1) * 128],
                                 start=(c == 0), stop=False)
                    last = t.matmul(bank[:, 0:128],
                                    SA[:, 7 * 2048 + tt * 128:7 * 2048 + (tt + 1) * 128],
                                    SC[:, sco + 7 * 128:sco + 8 * 128],
                                    start=False, stop=(wi != 3))
                    if wi == 3:
                        last = t.matmul(bank[:, 0:128], onesr[0:1, 0:128],
                                        scrA[0:1, 512:640],
                                        start=False, stop=True)
                    t.inc(last, "pe")
            b.pe(pw_pe)

            def pw_ac(a, pe_qb=pe_qb, dstb=dstb, actf=actf):
                for tt in range(NT):
                    a.wait("pe", pe_qb + tt + 1)
                    a.inc(a.activation(SB[:, dstb + tt * 128:dstb + (tt + 1) * 128],
                                       ps[4 + tt % 2][:, 0:128], actf), "ac")
            b.ac(pw_ac)
        proj_ac = b.mil("ac")

        for hh in range(2):
            base = 64 * hh
            head = 2 * p + hh
            acq = b.mil("ac")

            def nrm_ac(a, base=base, acq=acq, proj_ac=proj_ac, dvz=b.mil("dv")):
                a.wait("ac", proj_ac)
                a.wait("dv", dvz)
                for wi, qb in ((0, 0), (1, 2048)):
                    for tt in range(NT):
                        a.link(a.activation(
                            SD[:, 2752:2816],
                            SB[:, qb + tt * 128 + base:qb + tt * 128 + base + 64],
                            AF.Square, accum_out=smalls[:, 112 + wi * 16 + tt:
                                                        113 + wi * 16 + tt]))
                a.inc(a.activation(smalls[:, 144:176], smalls[:, 112:144], AF.Sqrt), "ac")
            b.ac(nrm_ac)
            nrm_acm = b.mil("ac")

            def nrm_dv(v, base=base, head=head, nrm_acm=nrm_acm):
                v.wait("ac", nrm_acm)
                v.link(v.tensor_scalar(smalls[:, 144:176], smalls[:, 144:176],
                                       float(np.float32(1e-12)), None, ALU.max))
                v.link(v.reciprocal(smalls[:, 176:208], smalls[:, 144:176]))
                for tt in range(NT):
                    qs = SB[:, tt * 128 + base:tt * 128 + base + 64]
                    v.link(v.tensor_scalar(qs, qs, smalls[:, 176 + tt:177 + tt],
                                           None, ALU.mult))
                    ks = SB[:, 2048 + tt * 128 + base:2048 + tt * 128 + base + 64]
                    v.link(v.tensor_scalar(ks, ks, smalls[:, 192 + tt:193 + tt],
                                           SD[:, 2112 + tt * 16 + head:2113 + tt * 16 + head],
                                           ALU.mult, ALU.mult))
                v.link(v.tensor_scalar(smalls[:, 97:98], iop[:], smalls[:, head:head + 1],
                                       None, ALU.mult))
                v.link(v.tensor_scalar(smalls[:, 98:99], smalls[:, 96:97],
                                       smalls[:, head:head + 1], None, ALU.mult))
                v.link(v.tensor_scalar(smalls[:, 99:100], smalls[:, head:head + 1],
                                       128.0, None, ALU.mult))
                v.link(v.tensor_scalar(scrA[:, 0:128], SD[:, 4352:4480],
                                       smalls[:, head:head + 1], None, ALU.mult))
                v.inc(v.memset(SD[0:64, 2048:2112], 0.0), "dv")
            b.dv(nrm_dv)
            nrm_dvm = b.mil("dv")

            def apw_ac(a, nrm_dvm=nrm_dvm):
                a.wait("dv", nrm_dvm)
                a.link(a.activation(smalls[:, 100:103], smalls[:, 97:100], AF.Exp))
                a.inc(a.activation(SD[:, 3264:3392], scrA[:, 0:128], AF.Exp), "ac")
            b.ac(apw_ac)
            apw_m = b.mil("ac")

            def dt_dv(v, apw_m=apw_m):
                v.wait("ac", apw_m)
                v.inc(v.tensor_tensor(SD[:, 3264:3392], SD[:, 3264:3392],
                                      SD[:, 4096:4224], ALU.mult), "dv")
            b.dv(dt_dv)
            dt_m = b.mil("dv")

            for cc in range(NT):
                qsl = SB[:, cc * 128 + base:cc * 128 + base + 64]
                ksl = SB[:, 2048 + cc * 128 + base:2048 + cc * 128 + base + 64]
                vsl = SB[:, 4096 + cc * 128 + base:4096 + cc * 128 + base + 64]
                gsl = SB[:, 6144 + cc * 128 + base:6144 + cc * 128 + base + 64]
                dv_c0 = b.mil("dv")

                def sc_pe1(t, qsl=qsl, ksl=ksl, dt_m=dt_m, dv_c0=dv_c0):
                    t.wait("dv", max(dt_m, dv_c0))
                    t.inc(t.transpose(ps[6][0:64, 0:128], qsl, ident[:]), "pe")
                    t.inc(t.transpose(ps[7][0:64, 0:128], ksl, ident[:]), "pe")
                b.pe(sc_pe1)
                pe_c1 = b.mil("pe")

                def sc_dv1(v, pe_c1=pe_c1):
                    v.wait("pe", pe_c1)
                    v.link(v.tensor_copy(SD[0:64, 2368:2496], ps[6][0:64, 0:128]))
                    v.inc(v.tensor_copy(SD[0:64, 2496:2624], ps[7][0:64, 0:128]), "dv")
                b.dv(sc_dv1)
                dv_c1 = b.mil("dv")

                def sc_pe2(t, cc=cc, dv_c1=dv_c1):
                    t.wait("dv", dv_c1)
                    t.inc(t.matmul(ps[cc % 2][:, 0:128], SD[0:64, 2496:2624],
                                   SD[0:64, 2368:2496], start=True, stop=True), "pe")
                b.pe(sc_pe2)
                pe_c2 = b.mil("pe")

                def sc_dv2(v, cc=cc, pe_c2=pe_c2):
                    v.wait("pe", pe_c2)
                    v.inc(v.tensor_tensor(SD[:, 2624:2752], ps[cc % 2][:, 0:128],
                                          SD[:, 3264:3392], ALU.mult), "dv")
                b.dv(sc_dv2)
                dv_c2 = b.mil("dv")

                def sc_pe3(t, vsl=vsl, dv_c2=dv_c2):
                    t.wait("dv", dv_c2)
                    t.inc(t.matmul(ps[2][:, 0:64], SD[:, 2624:2752], vsl,
                                   start=True, stop=True), "pe")
                    t.inc(t.matmul(ps[3][:, 0:64], SD[0:64, 2368:2496],
                                   SD[0:64, 2048:2112], start=True, stop=True), "pe")
                b.pe(sc_pe3)
                pe_c3 = b.mil("pe")

                def sc_dv3(v, cc=cc, qsl=qsl, ksl=ksl, gsl=gsl, pe_c3=pe_c3):
                    v.wait("pe", pe_c3)
                    v.link(v.tensor_scalar(qsl, ps[3][:, 0:64], smalls[:, 100:101],
                                           None, ALU.mult))
                    v.link(v.tensor_tensor(qsl, qsl, ps[2][:, 0:64], ALU.add))
                    v.link(v.tensor_tensor(qsl, qsl, gsl, ALU.mult))
                    v.inc(v.tensor_scalar(SD[:, 2752:2816], ksl, smalls[:, 101:102],
                                          None, ALU.mult), "dv")
                b.dv(sc_dv3)
                dv_c3 = b.mil("dv")

                def sc_pe4(t, vsl=vsl, dv_c3=dv_c3):
                    t.wait("dv", dv_c3)
                    t.inc(t.matmul(ps[4][0:64, 0:64], SD[:, 2752:2816], vsl,
                                   start=True, stop=True), "pe")
                b.pe(sc_pe4)
                pe_c4 = b.mil("pe")

                def sc_dv4(v, pe_c4=pe_c4):
                    v.wait("pe", pe_c4)
                    v.link(v.tensor_scalar(SD[0:64, 2048:2112], SD[0:64, 2048:2112],
                                           smalls[0:64, 102:103], None, ALU.mult))
                    v.inc(v.tensor_tensor(SD[0:64, 2048:2112], SD[0:64, 2048:2112],
                                          ps[4][0:64, 0:64], ALU.add), "dv")
                b.dv(sc_dv4)

            rms_acb = b.mil("ac")

            def rms_ac(a, base=base, dvz=b.mil("dv")):
                a.wait("dv", dvz)
                for tt in range(NT):
                    ins = a.activation(SD[:, 2752:2816],
                                       SB[:, tt * 128 + base:tt * 128 + base + 64],
                                       AF.Square, accum_out=smalls[:, 112 + tt:113 + tt])
                    if tt < NT - 1:
                        a.link(ins)
                    else:
                        a.inc(ins, "ac")
                        a.wait("ac", b.cnt["ac"])
            b.ac(rms_ac)
            rms_acm = b.mil("ac")

            def rms_dv(v, rms_acm=rms_acm):
                v.wait("ac", rms_acm)
                v.inc(v.tensor_scalar(smalls[:, 128:144], smalls[:, 112:128],
                                      1.0 / 64.0, 1e-5, ALU.mult, ALU.add), "dv")
            b.dv(rms_dv)
            rms_dvm = b.mil("dv")

            def rms_ac2(a, rms_dvm=rms_dvm):
                a.wait("dv", rms_dvm)
                a.inc(a.activation(smalls[:, 144:160], smalls[:, 128:144], AF.Sqrt), "ac")
            b.ac(rms_ac2)
            rms_ac2m = b.mil("ac")

            def rms_dv2(v, rms_ac2m=rms_ac2m, base=base):
                v.wait("ac", rms_ac2m)
                v.link(v.reciprocal(smalls[:, 160:176], smalls[:, 144:160]))
                for tt in range(NT):
                    osl = SB[:, tt * 128 + base:tt * 128 + base + 64]
                    v.link(v.tensor_scalar(osl, osl, smalls[:, 160 + tt:161 + tt],
                                           None, ALU.mult))
                    v.inc(v.tensor_tensor(osl, osl, smalls[:, 32:96], ALU.mult), "dv")
            b.dv(rms_dv2)

        onorm_dv = b.mil("dv")

        def gw_sy(sy, p=p):
            sy.wait("pe", b.mil("pe"))
            sy.inc(sy.dma_start(out=SC[:, 0:1024],
                                in_=wg3[:, :, p * 128:(p + 1) * 128]), "wd", 16)
        b.sy(gw_sy)
        gw_m = b.mil("wd")
        pe_gb = b.mil("pe")
        ac_gb = b.mil("ac")

        def g_pe(t, gw_m=gw_m, p=p, onorm_dv=onorm_dv, pe_gb=pe_gb, ac_gb=ac_gb,
                 acn=b.mil("ac")):
            t.wait("wd", gw_m)
            t.wait("dv", onorm_dv)
            t.wait("ac", acn)
            for tt in range(NT):
                if tt >= 2:
                    t.wait("ac", ac_gb + tt - 1)
                bank = ps[4 + tt % 2]
                for c in range(8):
                    t.matmul(bank[:, 0:128],
                             SA[:, c * 2048 + tt * 128:c * 2048 + (tt + 1) * 128],
                             SC[:, c * 128:(c + 1) * 128],
                             start=(c == 0), stop=False)
                t.inc(t.matmul(bank[:, 0:128], onesr[0:1, 0:128],
                               SC[0:1, 4096 + p * 128:4096 + (p + 1) * 128],
                               start=False, stop=True), "pe")
        b.pe(g_pe)

        def g_ac(a, pe_gb=pe_gb):
            for tt in range(NT):
                a.wait("pe", pe_gb + tt + 1)
                a.inc(a.activation(SB[:, 2048 + tt * 128:2048 + (tt + 1) * 128],
                                   ps[4 + tt % 2][:, 0:128], AF.Sigmoid), "ac")
        b.ac(g_ac)
        gate_ac = b.mil("ac")

        def z_dv(v, gate_ac=gate_ac):
            v.wait("ac", gate_ac)
            for tt in range(NT):
                zsl = SB[:, tt * 128:(tt + 1) * 128]
                v.inc(v.tensor_tensor(zsl, zsl, SB[:, 2048 + tt * 128:2048 + (tt + 1) * 128],
                                      ALU.mult), "dv")
        b.dv(z_dv)
        z_m = b.mil("dv")

        def zw_sy(sy, p=p):
            sy.wait("pe", b.mil("pe"))
            sy.inc(sy.dma_start(out=SC[:, 0:1024],
                                in_=dram["L2_attn_Wo"][p * 128:(p + 1) * 128, :]), "wd", 16)
        b.sy(zw_sy)
        zw_m = b.mil("wd")
        pe_zb = b.mil("pe")
        ac_zb = b.mil("ac")

        def zt_pe(t, z_m=z_m, pe_zb=pe_zb, ac_zb=ac_zb):
            t.wait("dv", z_m)
            for tt in range(NT):
                if tt >= 2:
                    t.wait("ac", ac_zb + tt - 1)
                t.inc(t.transpose(ps[4 + tt % 2][:, 0:128],
                                  SB[:, tt * 128:(tt + 1) * 128], ident[:]), "pe")
        b.pe(zt_pe)

        def zt_ac(a, pe_zb=pe_zb):
            for tt in range(NT):
                a.wait("pe", pe_zb + tt + 1)
                a.inc(a.activation(SB[:, 4096 + tt * 128:4096 + (tt + 1) * 128],
                                   ps[4 + tt % 2][:, 0:128], AF.Copy), "ac")
        b.ac(zt_ac)
        zf_ac = b.mil("ac")

        pe_wb = b.mil("pe")
        dv_rb = b.mil("dv")

        def wo_pe(t, zf_ac=zf_ac, zw_m=zw_m, pe_wb=pe_wb, dv_rb=dv_rb):
            t.wait("ac", zf_ac)
            t.wait("wd", zw_m)
            t.wait("dv", b.mil("dv"))
            k = 0
            for tt in range(NT):
                for oh in range(2):
                    if k >= 2:
                        t.wait("dv", dv_rb + k - 1)
                    t.inc(t.matmul(ps[k % 2][:, 0:512],
                                   SB[:, 4096 + tt * 128:4096 + (tt + 1) * 128],
                                   SC[:, oh * 512:(oh + 1) * 512],
                                   start=True, stop=True), "pe")
                    k += 1
        b.pe(wo_pe)
        pe_wm = b.mil("pe")

        def wo_dv(v, pe_wb=pe_wb):
            k = 0
            for tt in range(NT):
                for oh in range(2):
                    v.wait("pe", pe_wb + k + 1)
                    dst = xres[:, tt * D + oh * 512:tt * D + (oh + 1) * 512]
                    v.inc(v.tensor_tensor(dst, ps[k % 2][:, 0:512], dst, ALU.add), "dv")
                    k += 1
        b.dv(wo_dv)

    b.x_dv = b.mil("dv")


def _emit_qdq(h, src, dst, W, scol):
    b = h.b
    SD, smalls = h.SD, h.smalls
    nblk = W // 16
    sc = smalls[:, scol:scol + nblk]
    rsc = smalls[:, scol + 64:scol + 64 + nblk]
    xn = SD[:, 0:W]
    vab = SD[:, 1024:1024 + W]
    B = SD[:, 2048:2048 + W]
    A = xn

    def q_dv1(v, dvz=b.mil("dv"), acz=b.mil("ac"), pez=b.mil("pe"),
              stz=b.mil("st")):
        v.wait("dv", dvz)
        v.wait("ac", acz)
        v.wait("pe", pez)
        v.wait("st", stz)
        src3 = AP(src.tensor, src.offset, [src.ap[0], [16, nblk], [1, 16]])
        v.link(v.tensor_reduce(sc, src3, axis=AX.X, op=ALU.max,
                               apply_absolute_value=True))
        v.link(v.tensor_scalar(sc, sc, h.c6[:], float(np.float32(1e-12)),
                               ALU.mult, ALU.max))
        v.link(v.reciprocal(rsc, sc))
        rscb = AP(rsc.tensor, rsc.offset, [rsc.ap[0], [1, nblk], [0, 16]])
        xn3 = AP(xn.tensor, xn.offset, [xn.ap[0], [16, nblk], [1, 16]])
        v.inc(v.tensor_tensor(xn3, src3, rscb, ALU.mult), "dv")
    b.dv(q_dv1)
    m1 = b.mil("dv")

    def q_ac1(a, m1=m1):
        a.wait("dv", m1)
        a.link(a.activation(vab, xn, AF.Abs))
        a.inc(a.activation(dst, xn, AF.Sign), "ac")
    b.ac(q_ac1)
    m2 = b.mil("ac")

    def q_dv2(v, m2=m2):
        v.wait("ac", m2)
        v.link(v.tensor_scalar(A, vab, 5.0, None, ALU.is_gt))
        v.link(v.tensor_scalar(A, A, 2.0, None, ALU.mult))
        v.link(v.scalar_tensor_tensor(A, vab, 3.5, A, ALU.is_gt, ALU.add))
        v.link(v.scalar_tensor_tensor(A, vab, 2.5, A, ALU.is_gt, ALU.add))
        v.link(v.tensor_scalar(B, vab, 1.75, None, ALU.is_gt))
        v.link(v.scalar_tensor_tensor(B, vab, 1.25, B, ALU.is_gt, ALU.add))
        v.link(v.scalar_tensor_tensor(B, vab, 0.75, B, ALU.is_gt, ALU.add))
        v.link(v.scalar_tensor_tensor(B, vab, 0.25, B, ALU.is_gt, ALU.add))
        v.link(v.scalar_tensor_tensor(B, B, 0.5, A, ALU.mult, ALU.add))
        v.link(v.tensor_tensor(B, B, dst, ALU.mult))
        scb = AP(sc.tensor, sc.offset, [sc.ap[0], [1, nblk], [0, 16]])
        B3 = AP(B.tensor, B.offset, [B.ap[0], [16, nblk], [1, 16]])
        dst3 = AP(dst.tensor, dst.offset, [dst.ap[0], [16, nblk], [1, 16]])
        v.inc(v.tensor_tensor(dst3, B3, scb, ALU.mult), "dv")
    b.dv(q_dv2)
    return b.mil("dv")


def emit_l3ffn(h, sub):
    b, dram, nc = h.b, h.dram, h.nc
    SA, SB, SC, SD, ps = h.SA, h.SB, h.SC, h.SD, h.ps
    xres, ident, onesr, smalls, scrA = h.xres, h.ident, h.onesr, h.smalls, h.scrA
    ent = {s: b.mil(s) for s in Bld.SEMS}

    m, rs, ln_m = h.ln_stats()

    if not hasattr(h, "l3_scr"):
        h.l3_scr = {
            "wgqT": nc.dram_tensor("wgqT_scr", [1024, 2816], F32).ap(),
            "wuqT": nc.dram_tensor("wuqT_scr", [1024, 2816], F32).ap(),
            "wdqT": nc.dram_tensor("wdqT_scr", [2816, 1024], F32).ap(),
        }
        for wname, scrname, ntile, wid in (
                ("L3_ffn_w_gate", "wgqT", 22, 1024),
                ("L3_ffn_w_up", "wuqT", 22, 1024),
                ("L3_ffn_w_down", "wdqT", 8, 2816)):
            scr = h.l3_scr[scrname]
            for it in range(ntile):
                def wq_sy(sy, wname=wname, it=it, wid=wid, dvz=b.mil("dv"),
                          pez=b.mil("pe"), stz=b.mil("st")):
                    sy.wait("dv", dvz)
                    sy.wait("pe", pez)
                    sy.wait("st", stz)
                    for s in ("ac",):
                        sy.wait(s, ent[s])
                    sy.inc(sy.dma_start(out=SB[:, 0:wid],
                                        in_=dram[wname][it * 128:(it + 1) * 128, :]),
                           "wd", 16)
                b.sy(wq_sy)
                wdm = b.mil("wd")

                def wq_gate(v, wdm=wdm):
                    v.wait("wd", wdm)
                    v.inc(v.tensor_copy(smalls[0:1, 255:256], smalls[0:1, 255:256])
                          if False else
                          v.tensor_scalar(scrA[0:1, 0:1], SB[0:1, 0:1], 1.0, None,
                                          ALU.mult), "dv")
                b.dv(wq_gate)
                off = 0
                while off < wid:
                    wseg = min(1024, wid - off)
                    _emit_qdq(h, SB[:, off:off + wseg],
                              SB[:, 4096 + off:4096 + off + wseg], wseg, 112)
                    off += wseg
                nch = wid // 128
                pe_tb = b.mil("pe")
                dv_eb = b.mil("dv")

                def wq_pe(t, qm=b.mil("dv"), pe_tb=pe_tb, dv_eb=dv_eb, nch=nch):
                    t.wait("dv", qm)
                    for c in range(nch):
                        if c >= 2:
                            t.wait("dv", dv_eb + c - 1)
                        t.inc(t.transpose(ps[6 + c % 2][:, 0:128],
                                          SB[:, 4096 + c * 128:4096 + (c + 1) * 128],
                                          ident[:]), "pe")
                b.pe(wq_pe)

                def wq_dv(v, pe_tb=pe_tb, nch=nch):
                    for c in range(nch):
                        v.wait("pe", pe_tb + c + 1)
                        v.inc(v.tensor_copy(SB[:, c * 128:(c + 1) * 128],
                                            ps[6 + c % 2][:, 0:128]), "dv")
                b.dv(wq_dv)
                tq_m = b.mil("dv")

                def wq_out(sy, scr=scr, it=it, nch=nch, tq_m=tq_m):
                    sy.wait("dv", tq_m)
                    for c in range(nch):
                        sy.inc(sy.dma_start(
                            out=scr[c * 128:(c + 1) * 128, it * 128:(it + 1) * 128],
                            in_=SB[:, c * 128:(c + 1) * 128]), "st", 16)
                b.sy(wq_out)
        wq_done_st = b.mil("st")

        def wq_wait(sy, wq_done_st=wq_done_st):
            sy.wait("st", wq_done_st)
        b.sy(wq_wait)
    scr = h.l3_scr

    OBW = [512, 512, 512, 512, 512, 256]
    OBO = [0, 512, 1024, 1536, 2048, 2560]
    st_guard = b.mil("st")

    for sti in range(8):
        t0, t1 = 2 * sti, 2 * sti + 1
        dvz = b.mil("dv")

        def ln_dv(v, t0=t0, t1=t1, dvz=dvz, ln_m=ln_m, pez=b.mil("pe"),
                  acz=b.mil("ac")):
            v.wait("dv", max(ln_m, dvz))
            v.wait("pe", pez)
            v.wait("ac", acz)
            for j, tt in enumerate((t0, t1)):
                v.inc(v.tensor_scalar(SC[:, j * 1024:(j + 1) * 1024],
                                      xres[:, tt * D:(tt + 1) * D],
                                      m[:, tt:tt + 1], rs[:, tt:tt + 1],
                                      ALU.subtract, ALU.mult), "dv")
        b.dv(ln_dv)
        for j in range(2):
            _emit_qdq(h, SC[:, j * 1024:(j + 1) * 1024], SB[:, j * 1024:(j + 1) * 1024],
                      1024, 112)
        xq_m = b.mil("dv")

        pe_xb = b.mil("pe")
        dv_xb = b.mil("dv")

        def xq_pe(t, xq_m=xq_m, pe_xb=pe_xb, dv_xb=dv_xb):
            t.wait("dv", xq_m)
            for k in range(16):
                if k >= 2:
                    t.wait("dv", dv_xb + k - 1)
                t.inc(t.transpose(ps[6 + k % 2][:, 0:128],
                                  SB[:, k * 128:(k + 1) * 128], ident[:]), "pe")
        b.pe(xq_pe)

        def xq_dv(v, pe_xb=pe_xb):
            for k in range(16):
                v.wait("pe", pe_xb + k + 1)
                v.inc(v.tensor_copy(SB[:, 2048 + k * 128:2048 + (k + 1) * 128],
                                    ps[6 + k % 2][:, 0:128]), "dv")
        b.dv(xq_dv)
        xqf_m = b.mil("dv")

        down_open = [False]

        for obi in range(6):
            obw, obo = OBW[obi], OBO[obi]
            def wb_sy(sy, obo=obo, obw=obw, dvz=b.mil("dv"), pez=b.mil("pe")):
                sy.wait("dv", dvz)
                sy.wait("pe", pez)
                for c in range(8):
                    sy.inc(sy.dma_start(out=SA[:, c * 512:c * 512 + obw],
                                        in_=scr["wgqT"][c * 128:(c + 1) * 128,
                                                        obo:obo + obw]), "wd", 16)
                    sy.inc(sy.dma_start(out=SA[:, 4096 + c * 512:4096 + c * 512 + obw],
                                        in_=scr["wuqT"][c * 128:(c + 1) * 128,
                                                        obo:obo + obw]), "wd", 16)
                for c in range(obw // 128):
                    sy.inc(sy.dma_start(out=SA[:, 8192 + c * 1024:8192 + (c + 1) * 1024],
                                        in_=scr["wdqT"][obo + c * 128:obo + (c + 1) * 128, :]),
                           "wd", 16)
            b.sy(wb_sy)
            wbm = b.mil("wd")

            for j in range(2):
                pe_gb = b.mil("pe")
                ac_gb = b.mil("ac")

                def gu_pe(t, j=j, obw=obw, wbm=wbm, xqf_m=xqf_m, pe_gb=pe_gb,
                          acz=b.mil("ac"), dvz=b.mil("dv")):
                    t.wait("wd", wbm)
                    t.wait("dv", max(xqf_m, dvz))
                    t.wait("ac", acz)
                    for gi, sab in ((0, 0), (1, 4096)):
                        bank = ps[gi]
                        for c in range(7):
                            t.matmul(bank[:, 0:obw],
                                     SB[:, 2048 + (j * 8 + c) * 128:
                                        2048 + (j * 8 + c + 1) * 128],
                                     SA[:, sab + c * 512:sab + c * 512 + obw],
                                     start=(c == 0), stop=False)
                        t.inc(t.matmul(bank[:, 0:obw],
                                       SB[:, 2048 + (j * 8 + 7) * 128:
                                          2048 + (j * 8 + 8) * 128],
                                       SA[:, sab + 7 * 512:sab + 7 * 512 + obw],
                                       start=False, stop=True), "pe")
                b.pe(gu_pe)
                pe_gm = b.mil("pe")

                def gu_ac(a, pe_gm=pe_gm, obw=obw):
                    a.wait("pe", pe_gm - 1)
                    a.link(a.activation(SB[:, 4096:4096 + obw], ps[0][:, 0:obw], AF.Silu))
                    a.wait("pe", pe_gm)
                    a.inc(a.activation(SB[:, 5120:5120 + obw], ps[1][:, 0:obw], AF.Copy),
                          "ac")
                b.ac(gu_ac)
                gu_acm = b.mil("ac")

                def h_dv(v, gu_acm=gu_acm, obw=obw):
                    v.wait("ac", gu_acm)
                    v.inc(v.tensor_tensor(SB[:, 4096:4096 + obw], SB[:, 4096:4096 + obw],
                                          SB[:, 5120:5120 + obw], ALU.mult), "dv")
                b.dv(h_dv)
                hq_m = _emit_qdq(h, SB[:, 4096:4096 + obw], SB[:, 6144:6144 + obw],
                                 obw, 112)

                nch = obw // 128
                pe_hb = b.mil("pe")
                dv_hb = b.mil("dv")

                def hq_pe(t, hq_m=hq_m, pe_hb=pe_hb, dv_hb=dv_hb, nch=nch):
                    t.wait("dv", hq_m)
                    for c in range(nch):
                        if c >= 2:
                            t.wait("dv", dv_hb + c - 1)
                        t.inc(t.transpose(ps[6 + c % 2][:, 0:128],
                                          SB[:, 6144 + c * 128:6144 + (c + 1) * 128],
                                          ident[:]), "pe")
                b.pe(hq_pe)

                def hq_dv(v, pe_hb=pe_hb, nch=nch):
                    for c in range(nch):
                        v.wait("pe", pe_hb + c + 1)
                        v.inc(v.tensor_copy(SB[:, 7168 + c * 128:7168 + (c + 1) * 128],
                                            ps[6 + c % 2][:, 0:128]), "dv")
                b.dv(hq_dv)
                hqf_m = b.mil("dv")

                def dn_pe(t, j=j, obi=obi, nch=nch, hqf_m=hqf_m):
                    t.wait("dv", hqf_m)
                    for oh in range(2):
                        bank = ps[2 + 2 * j + oh]
                        for c in range(nch):
                            ins = t.matmul(bank[:, 0:512],
                                           SB[:, 7168 + c * 128:7168 + (c + 1) * 128],
                                           SA[:, 8192 + c * 1024 + oh * 512:
                                              8192 + c * 1024 + (oh + 1) * 512],
                                           start=(obi == 0 and c == 0),
                                           stop=(obi == 5 and c == nch - 1))
                            if oh == 1 and c == nch - 1:
                                t.inc(ins, "pe")

                b.pe(dn_pe)
        pe_dn = b.mil("pe")

        def res_dv(v, t0=t0, t1=t1, pe_dn=pe_dn):
            v.wait("pe", pe_dn)
            for j, tt in enumerate((t0, t1)):
                for oh in range(2):
                    dst = xres[:, tt * D + oh * 512:tt * D + (oh + 1) * 512]
                    v.inc(v.scalar_tensor_tensor(dst, ps[2 + 2 * j + oh][:, 0:512],
                                                 0.5, dst, ALU.mult, ALU.add), "dv")
        b.dv(res_dv)

    b.x_dv = b.mil("dv")


def emit_l3attn(h, sub):
    b, dram = h.b, h.dram
    SA, SB, SC, SD, ps = h.SA, h.SB, h.SC, h.SD, h.ps
    xres, ident, onesr, onesc, smalls, scrA, stats = (
        h.xres, h.ident, h.onesr, h.onesc, h.smalls, h.scrA, h.stats)
    ent = {s: b.mil(s) for s in Bld.SEMS}

    m, rs, ln_m = h.ln_stats()

    def loads(sy):
        for s in ("pe", "dv", "ac"):
            sy.wait(s, ent[s])
        sy.wait("st", ent["st"])
        sy.inc(sy.dma_start(out=SD[0:1, 2144:2160],
                            in_=dram["L3_attn_temp"].rearrange("h o -> o h")), "wd", 16)
        sy.inc(sy.dma_start(out=SC[0:1, 3072:4096], in_=dram["L3_attn_bout"][:, :]), "wd", 16)
    b.sy(loads)
    t_m = b.mil("wd")

    dv0, pe0, ac0 = b.mil("dv"), b.mil("pe"), b.mil("ac")

    def dvA(v):
        v.wait("dv", ln_m)
        for i in range(NT):
            xt = SD[:, (i % 2) * 1024:(i % 2 + 1) * 1024]
            if i >= 2:
                v.wait("pe", pe0 + 8 * (i - 1))
            v.inc(v.tensor_scalar(xt, xres[:, i * D:(i + 1) * D],
                                  m[:, i:i + 1], rs[:, i:i + 1],
                                  ALU.subtract, ALU.mult), "dv")

    def peA(t):
        t.wait("ac", ent["ac"])
        for i in range(NT):
            xt = SD[:, (i % 2) * 1024:(i % 2 + 1) * 1024]
            t.wait("dv", dv0 + i + 1)
            for c in range(8):
                k = 8 * i + c
                if k >= 2:
                    t.wait("ac", ac0 + k - 1)
                t.inc(t.transpose(ps[k % 2][:, 0:128], xt[:, c * 128:(c + 1) * 128],
                                  ident[:]), "pe")

    def acA(a):
        a.wait("dv", ent["dv"])
        a.wait("pe", ent["pe"])
        for i in range(NT):
            for c in range(8):
                k = 8 * i + c
                a.wait("pe", pe0 + k + 1)
                a.inc(a.activation(SA[:, c * 2048 + i * 128:c * 2048 + (i + 1) * 128],
                                   ps[k % 2][:, 0:128], AF.Copy), "ac")
    b.dv(dvA)
    b.pe(peA)
    b.ac(acA)
    xlnf_m = b.mil("ac")

    pe_tb = b.mil("pe")

    def tb_pe(t, t_m=t_m, xlnf_m=xlnf_m):
        t.wait("wd", t_m)
        t.wait("ac", xlnf_m)
        t.inc(t.matmul(ps[2][:, 0:16], onesr[0:1, 0:128], SD[0:1, 2144:2160],
                       start=True, stop=True), "pe")
    b.pe(tb_pe)

    def tb_dv(v, pe_tb=pe_tb):
        v.wait("pe", pe_tb + 1)
        v.inc(v.tensor_copy(SD[:, 2128:2144], ps[2][:, 0:16]), "dv")
    b.dv(tb_dv)
    tb_m = b.mil("dv")

    def _stage_w(nm, p, col_slice, dst):
        def s_sy(sy, nm=nm, p=p, col_slice=col_slice, dvz=b.mil("dv"),
                 pez=b.mil("pe")):
            sy.wait("dv", dvz)
            sy.wait("pe", pez)
            if col_slice:
                w3 = dram[nm].rearrange("(c p) j -> p c j", p=128)
                sy.inc(sy.dma_start(out=SC[:, 0:1024],
                                    in_=w3[:, :, p * 128:(p + 1) * 128]), "wd", 16)
            else:
                sy.inc(sy.dma_start(out=SC[:, 0:1024],
                                    in_=dram[nm][p * 128:(p + 1) * 128, :]), "wd", 16)
        b.sy(s_sy)
        wdm = b.mil("wd")
        pe_b = b.mil("pe")
        dv_b = b.mil("dv")

        def s_pe(t, wdm=wdm, dv_b=dv_b):
            t.wait("wd", wdm)
            t.wait("dv", dv_b)
            for c in range(8):
                if c >= 2:
                    t.wait("dv", dv_b + c - 1)
                t.inc(t.transpose(ps[6 + c % 2][:, 0:128],
                                  SC[:, c * 128:(c + 1) * 128], ident[:]), "pe")
        b.pe(s_pe)

        def s_dv(v, pe_b=pe_b, dst=dst):
            for c in range(8):
                v.wait("pe", pe_b + c + 1)
                v.inc(v.tensor_copy(dst[:, c * 128:(c + 1) * 128],
                                    ps[6 + c % 2][:, 0:128]), "dv")
        b.dv(s_dv)
        return b.mil("dv")

    def _w_pair(p, wT_m):
        pe_b = b.mil("pe")
        ac_b = b.mil("ac")

        def w_pe(t, wT_m=wT_m, pe_b=pe_b, ac_b=ac_b, acz=b.mil("ac"),
                 xlnf_m=xlnf_m):
            t.wait("dv", wT_m)
            t.wait("ac", max(xlnf_m, acz))
            for tt in range(NT):
                if tt >= 2:
                    t.wait("ac", ac_b + 2 * (tt - 1))
                bank = ps[2 + tt % 2]
                for c in range(7):
                    t.matmul(bank[:, 0:128],
                             SA[:, c * 2048 + tt * 128:c * 2048 + (tt + 1) * 128],
                             SC[:, 1024 + c * 128:1024 + (c + 1) * 128],
                             start=(c == 0), stop=False)
                t.inc(t.matmul(bank[:, 0:128],
                               SA[:, 7 * 2048 + tt * 128:7 * 2048 + (tt + 1) * 128],
                               SC[:, 1024 + 7 * 128:1024 + 8 * 128],
                               start=False, stop=True), "pe")
        b.pe(w_pe)

        def w_ac(a, pe_b=pe_b):
            for tt in range(NT):
                a.wait("pe", pe_b + tt + 1)
                a.link(a.activation(SB[:, tt * 128:(tt + 1) * 128],
                                    ps[2 + tt % 2][:, 0:128], AF.Copy))
                a.inc(a.activation(SB[:, 2048 + tt * 128:2048 + (tt + 1) * 128],
                                   ps[2 + tt % 2][:, 0:128], AF.Square), "ac")
        b.ac(w_ac)
        return b.mil("ac")

    for p in range(8):
        wT_m = _stage_w("L3_attn_Wqkv", p, False, SC[:, 1024:2048])
        w_m = _w_pair(p, wT_m)

        pe_s = b.mil("pe")

        def n_pe(t, w_m=w_m, pe_s=pe_s, dvz=b.mil("dv")):
            t.wait("ac", w_m)
            t.wait("dv", dvz)
            ins = None
            for tt in range(NT):
                ins = t.matmul(ps[4][0:1, 0:128], onesc[:],
                               SB[:, 2048 + tt * 128:2048 + (tt + 1) * 128],
                               start=(tt == 0), stop=(tt == NT - 1))
            t.inc(ins, "pe")
        b.pe(n_pe)
        pe_sm = b.mil("pe")

        def n_ac(a, pe_sm=pe_sm, p=p):
            a.wait("pe", pe_sm)
            a.inc(a.activation(SD[0:1, 2304:2432], ps[4][0:1, 0:128], AF.Sqrt), "ac")
        b.ac(n_ac)
        n_acm = b.mil("ac")

        def n_dv(v, n_acm=n_acm, p=p):
            v.wait("ac", n_acm)
            v.link(v.tensor_scalar(SD[0:1, 2304:2432], SD[0:1, 2304:2432],
                                   float(np.float32(1e-12)), None, ALU.max))
            v.link(v.reciprocal(SD[0:1, 1024 + p * 128:1024 + (p + 1) * 128],
                                SD[0:1, 2304:2432]))
            v.inc(v.tensor_tensor(SD[0:1, 2304:2432],
                                  SD[0:1, 1024 + p * 128:1024 + (p + 1) * 128],
                                  SD[0:1, 1024 + p * 128:1024 + (p + 1) * 128],
                                  ALU.mult), "dv")
        b.dv(n_dv)
        n_dvm = b.mil("dv")

        pe_b2 = b.mil("pe")

        def i_pe(t, n_dvm=n_dvm):
            t.wait("dv", n_dvm)
            t.inc(t.matmul(ps[4][:, 0:128], onesr[0:1, 0:128], SD[0:1, 2304:2432],
                           start=True, stop=True), "pe")
        b.pe(i_pe)
        pe_b2m = b.mil("pe")

        def lg_dv(v, pe_b2m=pe_b2m, p=p, tb_m=tb_m):
            v.wait("pe", pe_b2m)
            v.wait("dv", tb_m)
            for tt in range(NT):
                v.link(v.tensor_tensor(scrA[:, 256:384], SB[:, 2048 + tt * 128:
                                                            2048 + (tt + 1) * 128],
                                       ps[4][:, 0:128], ALU.mult))
                src3 = AP(scrA[:, 256:384].tensor, scrA[:, 256:384].offset,
                          [[1024, 128], [64, 2], [1, 64]])
                ins = v.tensor_reduce(smalls[:, tt * 16 + 2 * p:tt * 16 + 2 * p + 2],
                                      src3, axis=AX.X, op=ALU.add)
                if tt == NT - 1:
                    v.inc(ins, "dv")
                else:
                    v.link(ins)
        b.dv(lg_dv)

    sm_dvb = b.mil("dv")

    def sm_dv(v, sm_dvb=sm_dvb):
        for tt in range(NT):
            sl = smalls[:, tt * 16:(tt + 1) * 16]
            v.link(v.tensor_tensor(sl, sl, SD[:, 2128:2144], ALU.mult))
            v.link(v.reduce_max(stats[:, tt:tt + 1], sl, axis=AX.X))
        v.inc(v.tensor_scalar(stats[:, 16:32], stats[:, 0:16], -1.0, None, ALU.mult),
              "dv")
    b.dv(sm_dv)
    sm_m1 = b.mil("dv")

    def sm_ac(a, sm_m1=sm_m1):
        a.wait("dv", sm_m1)
        ins = None
        for tt in range(NT):
            ins = a.activation(scrA[:, tt * 16:(tt + 1) * 16],
                               smalls[:, tt * 16:(tt + 1) * 16], AF.Exp,
                               bias=stats[:, 16 + tt:17 + tt])
        a.inc(ins, "ac")
    b.ac(sm_ac)
    sm_m2 = b.mil("ac")

    def sm_dv2(v, sm_m2=sm_m2):
        v.wait("ac", sm_m2)
        for tt in range(NT):
            v.link(v.reduce_sum(stats[:, 32 + tt:33 + tt],
                                scrA[:, tt * 16:(tt + 1) * 16], axis=AX.X))
        v.link(v.reciprocal(stats[:, 48:64], stats[:, 32:48]))
        ins = None
        for tt in range(NT):
            ins = v.tensor_scalar(scrA[:, tt * 16:(tt + 1) * 16],
                                  scrA[:, tt * 16:(tt + 1) * 16],
                                  stats[:, 48 + tt:49 + tt], None, ALU.mult)
            if tt == NT - 1:
                v.inc(ins, "dv")
            else:
                v.link(ins)
    b.dv(sm_dv2)
    pi_m = b.mil("dv")

    pe_pc = b.mil("pe")

    def pc_pe(t, pi_m=pi_m):
        t.wait("dv", pi_m)
        ins = None
        for tt in range(NT):
            ins = t.matmul(ps[5][0:1, 0:16], onesc[:], scrA[:, tt * 16:(tt + 1) * 16],
                           start=(tt == 0), stop=(tt == NT - 1))
        t.inc(ins, "pe")
    b.pe(pc_pe)
    pc_m = b.mil("pe")

    def pc_ac(a, pc_m=pc_m):
        a.wait("pe", pc_m)
        a.inc(a.activation(SD[0:1, 2048:2064], ps[5][0:1, 0:16], AF.Copy,
                           bias=1e-8), "ac")
    b.ac(pc_ac)
    pcs_m0 = b.mil("ac")

    def pc_dv(v, pcs_m0=pcs_m0):
        v.wait("ac", pcs_m0)
        v.inc(v.reciprocal(SD[0:1, 2160:2176], SD[0:1, 2048:2064]), "dv")
    b.dv(pc_dv)
    pcs_m = b.mil("dv")

    for p in range(8):
        wT_m = _stage_w("L3_attn_Wqkv", p, False, SC[:, 1024:2048])
        w_m = _w_pair(p, wT_m)

        for hh in range(2):
            head = 2 * p + hh
            base = 64 * hh

            def d_pe(t, w_m=w_m, base=base, head=head, dvz=b.mil("dv"),
                     acz=b.mil("ac")):
                t.wait("ac", max(w_m, acz))
                t.wait("dv", max(dvz, pcs_m))
                ins = None
                for tt in range(NT):
                    ins = t.matmul(ps[4][0:1, 0:64],
                                   scrA[:, tt * 16 + head:tt * 16 + head + 1],
                                   SB[:, 2048 + tt * 128 + base:
                                      2048 + tt * 128 + base + 64],
                                   start=(tt == 0), stop=(tt == NT - 1))
                t.inc(ins, "pe")
            b.pe(d_pe)
            d_m = b.mil("pe")

            def at_dv(v, d_m=d_m, head=head):
                v.wait("pe", d_m)
                v.link(v.tensor_scalar(SD[0:1, 2064:2128], ps[4][0:1, 0:64],
                                       SD[0:1, 2160 + head:2161 + head], None,
                                       ALU.mult))
                v.link(v.tensor_scalar(SD[0:1, 2064:2128], SD[0:1, 2064:2128],
                                       1.0, None, ALU.add))
                v.link(v.reciprocal(SD[0:1, 2240:2304], SD[0:1, 2064:2128]))
                v.inc(v.tensor_scalar(SD[0:1, 2240:2304], SD[0:1, 2240:2304],
                                      -1.0, None, ALU.mult), "dv")
            b.dv(at_dv)
            nat_m = b.mil("dv")

            def nb_pe(t, nat_m=nat_m):
                t.wait("dv", nat_m)
                t.inc(t.matmul(ps[5][:, 0:64], onesr[0:1, 0:128],
                               SD[0:1, 2240:2304], start=True, stop=True), "pe")
            b.pe(nb_pe)
            nb_m = b.mil("pe")

            def z_dv(v, nb_m=nb_m, base=base, head=head):
                v.wait("pe", nb_m)
                v.link(v.tensor_copy(SD[:, 2176:2240], ps[5][:, 0:64]))
                for tt in range(NT):
                    zsl = SB[:, 4096 + tt * 128 + base:4096 + tt * 128 + base + 64]
                    v.link(v.tensor_scalar(zsl,
                                           SB[:, tt * 128 + base:tt * 128 + base + 64],
                                           scrA[:, tt * 16 + head:tt * 16 + head + 1],
                                           None, ALU.mult))
                    ins = v.tensor_tensor(zsl, zsl, SD[:, 2176:2240], ALU.mult)
                    if tt == NT - 1:
                        v.inc(ins, "dv")
                    else:
                        v.link(ins)
            b.dv(z_dv)
        z_m = b.mil("dv")

        pe_zb = b.mil("pe")
        ac_zb = b.mil("ac")

        def zt_pe(t, z_m=z_m, pe_zb=pe_zb, ac_zb=ac_zb):
            t.wait("dv", z_m)
            for tt in range(NT):
                if tt >= 2:
                    t.wait("ac", ac_zb + tt - 1)
                t.inc(t.transpose(ps[2 + tt % 2][:, 0:128],
                                  SB[:, 4096 + tt * 128:4096 + (tt + 1) * 128],
                                  ident[:]), "pe")
        b.pe(zt_pe)

        def zt_ac(a, pe_zb=pe_zb):
            for tt in range(NT):
                a.wait("pe", pe_zb + tt + 1)
                a.inc(a.activation(SB[:, 6144 + tt * 128:6144 + (tt + 1) * 128],
                                   ps[2 + tt % 2][:, 0:128], AF.Copy), "ac")
        b.ac(zt_ac)
        zf_m = b.mil("ac")

        woT_m = _stage_w("L3_attn_Wout", p, True, SC[:, 4096:5120])
        pe_wb = b.mil("pe")
        dv_rb = b.mil("dv")

        def wo_pe(t, zf_m=zf_m, woT_m=woT_m, pe_wb=pe_wb, dv_rb=dv_rb, p=p,
                  t_m=t_m):
            t.wait("ac", zf_m)
            t.wait("dv", woT_m)
            t.wait("wd", t_m)
            k = 0
            for tt in range(NT):
                for oh in range(2):
                    if k >= 2:
                        t.wait("dv", dv_rb + k - 1)
                    bank = ps[0 + k % 2]
                    if p == 0:
                        t.matmul(bank[:, 0:512],
                                 SB[:, 6144 + tt * 128:6144 + (tt + 1) * 128],
                                 SC[:, 4096 + oh * 512:4096 + (oh + 1) * 512],
                                 start=True, stop=False)
                        t.inc(t.matmul(bank[:, 0:512], onesr[0:1, 0:128],
                                       SC[0:1, 3072 + oh * 512:3072 + (oh + 1) * 512],
                                       start=False, stop=True), "pe")
                    else:
                        t.inc(t.matmul(bank[:, 0:512],
                                       SB[:, 6144 + tt * 128:6144 + (tt + 1) * 128],
                                       SC[:, 4096 + oh * 512:4096 + (oh + 1) * 512],
                                       start=True, stop=True), "pe")
                    k += 1
        b.pe(wo_pe)
        pe_wm = b.mil("pe")

        def wo_dv(v, pe_wb=pe_wb):
            k = 0
            for tt in range(NT):
                for oh in range(2):
                    v.wait("pe", pe_wb + k + 1)
                    dst = xres[:, tt * D + oh * 512:tt * D + (oh + 1) * 512]
                    v.inc(v.tensor_tensor(dst, ps[0 + k % 2][:, 0:512], dst,
                                          ALU.add), "dv")
                    k += 1
        b.dv(wo_dv)

    b.x_dv = b.mil("dv")


def _bf_diag_vectors(up, dn):
    out = {}
    i = np.arange(4096)
    for tag, w in (("up", up), ("dn", dn)):
        dself = np.zeros((12, 4096), np.float32)
        dcross = np.zeros((12, 4096), np.float32)
        for s in range(12):
            s2 = 1 << s
            blk = i // (2 * s2)
            j = i % s2
            side = (i // s2) % 2
            wr = np.asarray(w[s], np.float32).reshape(4096 // (2 * s2), s2, 2, 2)
            dself[s] = wr[blk, j, side, side]
            dcross[s] = wr[blk, j, side, 1 - side]
        out[f"BF_{tag}_self"] = dself
        out[f"BF_{tag}_cross"] = dcross
    return out


def kernel(**inputs):
    cst = host_constants()
    nc = build_program(N_SUB)
    base = {k: np.ascontiguousarray(v) for k, v in cst.items()}
    base.update(_bf_diag_vectors(np.asarray(inputs["L1_ffn_up_weights"]),
                                 np.asarray(inputs["L1_ffn_down_weights"])))
    skip = {"L0_attn_x", "L1_ffn_up_weights", "L1_ffn_down_weights"}
    for nm, v in inputs.items():
        if nm in skip:
            continue
        a = np.ascontiguousarray(np.asarray(v, np.float32))
        if a.ndim == 1:
            a = a.reshape(1, -1)
        base[nm] = a
    x = np.ascontiguousarray(np.asarray(inputs["L0_attn_x"], np.float32))
    in_maps = [dict(base, x_in=np.ascontiguousarray(x[c])) for c in range(8)]
    res = run_bass_kernel_spmd(nc, in_maps, core_ids=list(range(8)))
    return np.stack([res.results[i]["y"] for i in range(8)], axis=0)


# version 04
# speedup vs baseline: 1.0019x; 1.0019x over previous
import contextlib
import os
import subprocess
import sys
import tempfile

import numpy as np

import concourse.bass as bass
import concourse.mybir as mybir
from concourse.bass import AP
from concourse.bass_utils import run_bass_kernel_spmd

F32 = mybir.dt.float32
AF = mybir.ActivationFunctionType
ALU = mybir.AluOpType
AX = mybir.AxisListType

S = 2048
D = 1024
NT = S // 128
NCH = D // 128
N_SUB = 12

_JAX_TABLE_SCRIPT = r"""
import sys, numpy as np
import jax
jax.config.update('jax_platforms', 'cpu')
jax.config.update('jax_enable_x64', True)
import jax.numpy as jnp
S, d, H = 2048, 64, 16
inv0 = 1.0 / 10000.0 ** (jnp.arange(0, d, 2, dtype=jnp.float32) / d)
t32 = jnp.arange(S, dtype=jnp.float32)
fr0 = jnp.einsum('i,j->ij', t32, inv0)
emb0 = jnp.concatenate([fr0, fr0], axis=-1)
cos0 = np.asarray(jnp.cos(emb0))
sin0 = np.asarray(jnp.sin(emb0))
gam = 1.0 - jnp.exp2(-5.0 - jnp.arange(H, dtype=jnp.float32))
pow0 = np.asarray(jnp.power(np.asarray(gam)[:, None], jnp.arange(S, dtype=jnp.float32)[None, :]))
np.save(sys.argv[1] + "c.npy", cos0)
np.save(sys.argv[1] + "s.npy", sin0)
np.save(sys.argv[1] + "p.npy", pow0)
np.save(sys.argv[1] + "e.npy", np.asarray(emb0))
"""

_CONST_CACHE = {}


def _xla_tables():
    if "t" in _CONST_CACHE:
        return _CONST_CACHE["t"]
    out = None
    try:
        with tempfile.TemporaryDirectory() as td:
            p = os.path.join(td, "t")
            env = {**os.environ, "JAX_PLATFORMS": "cpu"}
            r = subprocess.run([sys.executable, "-c", _JAX_TABLE_SCRIPT, p],
                               env=env, capture_output=True, timeout=900)
            if r.returncode == 0 and os.path.exists(p + "c.npy"):
                out = {"cos0": np.load(p + "c.npy"), "sin0": np.load(p + "s.npy"),
                       "pow0": np.load(p + "p.npy"), "emb0": np.load(p + "e.npy")}
    except Exception:
        out = None
    if out is None:
        d = 64
        inv0 = (1.0 / 10000.0 ** (np.arange(0, d, 2, dtype=np.float64) / d)).astype(np.float32)
        fr0 = (np.arange(S, dtype=np.float32)[:, None] * inv0[None, :])
        emb0 = np.concatenate([fr0, fr0], -1)
        cos0 = np.cos(emb0.astype(np.float64)).astype(np.float32)
        sin0 = np.sin(emb0.astype(np.float64)).astype(np.float32)
        gam = (1.0 - np.exp2(-5.0 - np.arange(16, dtype=np.float64)))
        pow0 = np.power(gam[:, None], np.arange(S, dtype=np.float64)[None, :]).astype(np.float32)
        out = dict(cos0=cos0, sin0=sin0, pow0=pow0, emb0=emb0.astype(np.float32))
    _CONST_CACHE["t"] = out
    return out


def host_constants():
    if "c" in _CONST_CACHE:
        return _CONST_CACHE["c"]
    t = _xla_tables()
    cos0, sin0, pow0 = t["cos0"], t["sin0"], t["pow0"]
    c = {}
    cos0q = (cos0.astype(np.float64) * 0.125).astype(np.float32)
    sin0q = (sin0.astype(np.float64) * 0.125).astype(np.float32)
    c["C_T0"] = np.concatenate([cos0q.T, sin0q.T], axis=0).astype(np.float32)
    c["C_T1"] = np.concatenate([cos0.T, sin0.T], axis=0).astype(np.float32)
    u = np.arange(128)[:, None]
    q = np.arange(128)[None, :]
    decT = np.zeros((16, 19, 128, 128), np.float32)
    for dd in range(19):
        idx = (dd - 3) * 128 + q - u
        valid = (idx >= 0) & (idx < S)
        ii = np.clip(idx, 0, S - 1)
        for hh in range(16):
            decT[hh, dd] = np.where(valid, pow0[hh][ii], np.float32(0.0))
    c["C_decTx"] = decT.reshape(16 * 19 * 128, 128)
    emb1 = t["emb0"].astype(np.float32)
    c64 = np.cos(emb1.astype(np.float64))
    s64 = np.sin(emb1.astype(np.float64))
    ch = c64.astype(np.float32)
    cl = (c64 - ch.astype(np.float64)).astype(np.float32)
    sh = s64.astype(np.float32)
    sl = (s64 - sh.astype(np.float64)).astype(np.float32)
    cqh = (c64 * 0.125).astype(np.float32)
    cql = (c64 * 0.125 - cqh.astype(np.float64)).astype(np.float32)
    sqh = (s64 * 0.125).astype(np.float32)
    sql = (s64 * 0.125 - sqh.astype(np.float64)).astype(np.float32)
    c["C_U0h"] = np.concatenate([cqh.T, sqh.T], axis=0).astype(np.float32)
    c["C_U0l"] = np.concatenate([cql.T, sql.T], axis=0).astype(np.float32)
    c["C_U1h"] = np.concatenate([ch.T, sh.T], axis=0).astype(np.float32)
    c["C_U1l"] = np.concatenate([cl.T, sl.T], axis=0).astype(np.float32)
    _CONST_CACHE["c"] = c
    return c



class Bld:
    SEMS = ["pe", "dv", "ac", "gp", "wd", "xd", "st", "pd"]

    def __init__(self, nc, ctx):
        self.nc = nc
        self.ctx = ctx
        self.blk = ctx.enter_context(nc.Block())
        self.sem = {s: ctx.enter_context(nc.semaphore(s)) for s in self.SEMS}
        self.cnt = {s: 0 for s in self.SEMS}
        self.waited = {}
        self.x_dv = 0
        self.x_xd = 0

    def sec(self, eng, fn):
        getattr(self.blk, eng)(lambda e, fn=fn, eng=eng: fn(_E(self, eng, e)))

    def mil(self, s):
        return self.cnt[s]

    def expect(self, s, v):
        assert self.cnt[s] == v, f"sem {s}: ledger {self.cnt[s]} != expected {v}"

    def pe(self, fn):
        self.sec("tensor", fn)

    def dv(self, fn):
        self.sec("vector", fn)

    def ac(self, fn):
        self.sec("scalar", fn)

    def gp(self, fn):
        self.sec("gpsimd", fn)

    def sy(self, fn):
        self.sec("sync", fn)


class _E:
    def __init__(self, b, name, e):
        self.b = b
        self.name = name
        self.e = e

    def wait(self, s, v):
        if v is None or v <= 0:
            return
        k = (self.name, s)
        if v <= self.b.waited.get(k, 0):
            return
        self.e.wait_ge(self.b.sem[s], v)
        self.b.waited[k] = v

    def inc(self, ins, s, k=1):
        ins.then_inc(self.b.sem[s], k)
        self.b.cnt[s] += k
        return self.b.cnt[s]

    OWN = {"tensor": "pe", "vector": "dv", "scalar": "ac", "gpsimd": "gp"}

    def link(self, ins):
        own = self.OWN[self.name]
        v = self.inc(ins, own)
        self.wait(own, v)
        return v

    def __getattr__(self, a):
        return getattr(self.e, a)



def build_program(n_sub=12):
    nc = bass.Bass("TRN2", target_bir_lowering=False)
    cst = host_constants()
    dram = {}

    def din(name, shape):
        dram[name] = nc.dram_tensor(name, list(shape), F32, kind="ExternalInput").ap()

    din("x_in", (S, D))
    for k, v in cst.items():
        din(k, v.shape)
    for nm, shp in [
        ("L0_attn_wq", (D, D)), ("L0_attn_wk", (D, D)), ("L0_attn_wv", (D, D)),
        ("L0_attn_wg", (D, D)), ("L0_attn_wo", (D, D)),
        ("L0_ffn_W", (D, D)), ("L0_ffn_b", (1, D)),
        ("L0_ffn_lora_down", (4, 100)), ("L0_ffn_lora_up", (50, 4)),
        ("L0_ffn_down_aux", (100, D)), ("L0_ffn_up_aux", (D, 50)),
        ("L1_attn_Wq", (D, D)), ("L1_attn_Wkv", (D, D)), ("L1_attn_Wout", (D, D)),
        ("L1_attn_bout", (1, D)),
        ("L1_ffn_W_exp", (D, 4096)), ("L1_ffn_b_exp", (1, 4096)),
        ("L1_ffn_up_bias", (1, 4096)),
        ("L1_ffn_W_con", (4096, D)), ("L1_ffn_b_con", (1, D)), ("L1_ffn_down_bias", (1, D)),
        ("L2_attn_Wq", (D, D)), ("L2_attn_Wk", (D, D)), ("L2_attn_Wv", (D, D)),
        ("L2_attn_Wf", (D, D)), ("L2_attn_Wb", (D, 16)), ("L2_attn_A_log", (1, 16)),
        ("L2_attn_dt_bias", (1, D)), ("L2_attn_Wg", (D, D)), ("L2_attn_bg", (1, D)),
        ("L2_attn_Wo", (D, D)), ("L2_attn_norm_w", (1, 64)),
        ("L2_ffn_U1", (D, 256)), ("L2_ffn_V1", (256, 4096)), ("L2_ffn_b1", (1, 4096)),
        ("L2_ffn_U2", (4096, 256)), ("L2_ffn_V2", (256, D)), ("L2_ffn_b2", (1, D)),
        ("L3_attn_Wqkv", (D, D)), ("L3_attn_temp", (16, 1)),
        ("L3_attn_Wout", (D, D)), ("L3_attn_bout", (1, D)),
        ("L3_ffn_w_gate", (2816, D)), ("L3_ffn_w_up", (2816, D)), ("L3_ffn_w_down", (D, 2816)),
        ("BF_up_self", (12, 4096)), ("BF_up_cross", (12, 4096)),
        ("BF_dn_self", (12, 4096)), ("BF_dn_cross", (12, 4096)),
    ]:
        din(nm, shp)
    y_out = nc.dram_tensor("y", [S, D], F32, kind="ExternalOutput").ap()

    with contextlib.ExitStack() as ctx:
        def sb(name, shape):
            return ctx.enter_context(nc.sbuf_tensor(name, list(shape), F32))

        def psb(name, shape):
            return ctx.enter_context(nc.psum_tensor(name, list(shape), F32))

        xres = sb("xres", (128, NT * D))
        SA = sb("SA", (128, 16 * 1024))
        SB = sb("SB", (128, 8 * 1024))
        SC = sb("SC", (128, 5 * 1024))
        ident = sb("ident", (128, 128))
        iof = sb("iof", (128, 128))
        iop = sb("iop", (128, 1))
        onesr = sb("onesr", (1, 512))
        onesc = sb("onesc", (128, 1))
        stats = sb("stats", (128, 10 * 16))
        scrA = sb("scrA", (128, 1024))
        smalls = sb("smalls", (128, 256))
        SD = sb("SD", (128, 4480))
        halfr = sb("halfr", (1, 128))
        c6 = sb("c6", (128, 1))
        ps = [psb(f"ps{i}", (128, 512)) for i in range(8)]

        b = Bld(nc, ctx)

        def pre_gp(g):
            g.iota(iof[:], pattern=[[1, 128]], base=0, channel_multiplier=0,
                   allow_small_or_imprecise_dtypes=True)
            g.inc(g.iota(iop[:], pattern=[[0, 1]], base=0, channel_multiplier=1,
                         allow_small_or_imprecise_dtypes=True), "gp")
        b.gp(pre_gp)

        def pre_dv(v):
            v.wait("gp", 1)
            v.memset(onesr[0:1, :], 1.0)
            v.memset(halfr[0:1, :], 0.5)
            v.memset(onesc[:], 1.0)
            v.link(v.memset(scrA[:, 0:1], 6.0))
            v.link(v.reciprocal(c6[:], scrA[:, 0:1]))
            v.inc(v.tensor_scalar(ident[:], iof[:], iop[:], None, ALU.is_equal), "dv")
        b.dv(pre_dv)
        b.IDENT = b.mil("dv")
        b.x_dv = b.mil("dv")

        def pre_sy(sy):
            for i in range(NT):
                sy.inc(sy.dma_start(out=xres[:, i * D:(i + 1) * D],
                                    in_=dram["x_in"][i * 128:(i + 1) * 128, :]), "xd", 16)
        b.sy(pre_sy)
        b.x_xd = b.mil("xd")

        h = Helpers(b, nc, dram, dict(
            xres=xres, SA=SA, SB=SB, SC=SC, SD=SD, ident=ident, iof=iof, iop=iop,
            onesr=onesr, onesc=onesc, halfr=halfr, c6=c6, stats=stats, scrA=scrA,
            smalls=smalls, ps=ps))

        if globals().get("_STOP_STAGE") == "pre":
            n_sub = 0
        subs = [emit_l0ffn, emit_l0attn, emit_l0ffn,
                emit_l1ffn, emit_l1attn, emit_l1ffn,
                emit_l2ffn, emit_l2attn, emit_l2ffn,
                emit_l3ffn, emit_l3attn, emit_l3ffn]
        for i in range(min(n_sub, 12)):
            subs[i](h, i)

        if globals().get("_DUMP_STATS"):
            def dump2_sy(sy):
                sy.wait("dv", b.mil("dv"))
                sy.wait("ac", b.mil("ac"))
                sy.inc(sy.dma_start(out=y_out[0:128, 0:160], in_=stats[:, :]), "st", 16)
                sy.wait("st", b.cnt["st"])
            b.sy(dump2_sy)
            return nc
        if globals().get("_DUMP_SA"):
            def dump_sy(sy):
                sy.wait("dv", b.mil("dv"))
                sy.wait("ac", b.mil("ac"))
                sy.wait("pe", b.mil("pe"))
                for i in range(NT):
                    sy.inc(sy.dma_start(out=y_out[i * 128:(i + 1) * 128, :],
                                        in_=SA[:, i * 1024:(i + 1) * 1024]), "st", 16)
                sy.wait("st", b.cnt["st"])
            b.sy(dump_sy)
            return nc
        dv_dep, xd_dep = b.x_dv, b.x_xd

        def out_sy(sy):
            sy.wait("dv", dv_dep)
            sy.wait("xd", xd_dep)
            for i in range(NT):
                sy.inc(sy.dma_start(out=y_out[i * 128:(i + 1) * 128, :],
                                    in_=xres[:, i * D:(i + 1) * D]), "st", 16)
            sy.wait("st", b.cnt["st"])
        b.sy(out_sy)
    return nc


class Helpers:
    def __init__(self, b, nc, dram, env):
        self.b = b
        self.nc = nc
        self.dram = dram
        for k, v in env.items():
            setattr(self, k, v)

    def ln_stats(self):
        b, st, xres, scrA = self.b, self.stats, self.xres, self.scrA
        xdep, dvdep = b.x_xd, b.x_dv
        s1, s2 = st[:, 0:16], st[:, 16:32]
        m, rs = st[:, 32:48], st[:, 48:64]
        ex2, msq = st[:, 64:80], st[:, 80:96]
        var, veps, sq = st[:, 96:112], st[:, 112:128], st[:, 128:144]
        ac0 = b.mil("ac")

        def ac1(a):
            a.wait("xd", xdep)
            a.wait("dv", dvdep)
            for i in range(NT):
                if i >= 1:
                    a.wait("ac", ac0 + i)
                a.inc(a.activation(scrA[:, 0:1024], xres[:, i * D:(i + 1) * D],
                                   AF.Square, accum_out=s2[:, i:i + 1]), "ac")
        b.ac(ac1)
        ac_m = b.mil("ac")

        def dv1(v):
            v.wait("xd", xdep)
            v.wait("dv", dvdep)
            ins = None
            for i in range(NT):
                ins = v.reduce_sum(s1[:, i:i + 1], xres[:, i * D:(i + 1) * D], axis=AX.X)
            v.link(ins)
            v.wait("ac", ac_m)
            v.link(v.tensor_scalar(m[:], s1[:], 1.0 / 1024.0, None, ALU.mult))
            v.link(v.tensor_scalar(ex2[:], s2[:], 1.0 / 1024.0, None, ALU.mult))
            v.link(v.tensor_tensor(msq[:], m[:], m[:], ALU.mult))
            v.link(v.tensor_tensor(var[:], ex2[:], msq[:], ALU.subtract))
            v.inc(v.tensor_scalar(veps[:], var[:], 1e-5, None, ALU.add), "dv")
        b.dv(dv1)
        dv_m = b.mil("dv")

        def ac2(a):
            a.wait("dv", dv_m)
            a.inc(a.activation(sq[:], veps[:], AF.Sqrt), "ac")
        b.ac(ac2)
        ac_m2 = b.mil("ac")

        def dv2(v):
            v.wait("ac", ac_m2)
            v.inc(v.reciprocal(rs[:], sq[:]), "dv")
        b.dv(dv2)
        return m, rs, b.mil("dv")



def emit_l0ffn(h, sub):
    b, dram = h.b, h.dram
    SA, SB, SC, ps = h.SA, h.SB, h.SC, h.ps
    xres, ident, onesr, halfr = h.xres, h.ident, h.onesr, h.halfr
    SD = h.SD
    rowbuf = SD

    m, rs, ln_m = h.ln_stats()

    def loads(sy):
        sy.wait("dv", b.x_dv)
        for o in range(8):
            sy.inc(sy.dma_start(out=SB[:, o * 1024:(o + 1) * 1024],
                                in_=dram["L0_ffn_W"][o * 128:(o + 1) * 128, :]), "wd", 16)
        sy.inc(sy.dma_start(out=SC[0:4, 0:100], in_=dram["L0_ffn_lora_down"][:, :]), "wd", 16)
        sy.inc(sy.dma_start(out=SC[0:50, 128:132], in_=dram["L0_ffn_lora_up"][:, :]), "wd", 16)
        sy.inc(sy.dma_start(out=SC[0:100, 256:1280], in_=dram["L0_ffn_down_aux"][:, :]), "wd", 16)
        for c in range(8):
            sy.inc(sy.dma_start(out=SC[:, 1280 + c * 50:1280 + (c + 1) * 50],
                                in_=dram["L0_ffn_up_aux"][c * 128:(c + 1) * 128, :]), "wd", 16)
        sy.inc(sy.dma_start(out=SD[0:1, 0:1024], in_=dram["L0_ffn_b"][:, :]), "wd", 16)
    b.sy(loads)
    wd_m = b.mil("wd")
    if globals().get("_STOP_STAGE") == "loads":
        return

    ldT = SC[0:100, 1792:1796]
    luT = SC[0:4, 1900:1950]
    D1 = SC[0:4, 2048:3072]
    D2 = SC[0:50, 3072:4096]
    uaT = SC[0:50, 4096:5120]

    def pe1(t):
        t.wait("wd", wd_m)
        t.wait("dv", b.IDENT)
        t.transpose(ps[0][0:100, 0:4], SC[0:4, 0:100], ident[0:4, 0:4])
        t.inc(t.transpose(ps[1][0:4, 0:50], SC[0:50, 128:132], ident[0:50, 0:50]), "pe")
    b.pe(pe1)
    pe1m = b.mil("pe")

    def dv1(v):
        v.wait("pe", pe1m)
        v.tensor_copy(ldT, ps[0][0:100, 0:4])
        v.inc(v.tensor_copy(luT, ps[1][0:4, 0:50]), "dv")
    b.dv(dv1)
    dv1m = b.mil("dv")
    if globals().get("_STOP_STAGE") == "tp1":
        return

    def pe2(t):
        t.wait("dv", dv1m)
        for c in range(8):
            if c >= 2:
                t.wait("dv", dv1m + c - 1)
            t.inc(t.transpose(ps[2 + c % 2][0:50, 0:128],
                              SC[:, 1280 + c * 50:1280 + (c + 1) * 50], ident[:]), "pe")
        t.matmul(ps[4][0:4, 0:512], ldT, SC[0:100, 256:768], start=True, stop=True)
        t.inc(t.matmul(ps[5][0:4, 0:512], ldT, SC[0:100, 768:1280],
                       start=True, stop=True), "pe")
    b.pe(pe2)
    pe2m = b.mil("pe")

    def dv2(v):
        for c in range(8):
            v.wait("pe", pe2m - 9 + c + 1)
            v.inc(v.tensor_copy(uaT[:, c * 128:(c + 1) * 128], ps[2 + c % 2][0:50, 0:128]), "dv")
        v.wait("pe", pe2m)
        v.tensor_copy(D1[:, 0:512], ps[4][0:4, 0:512])
        v.inc(v.tensor_copy(D1[:, 512:1024], ps[5][0:4, 0:512]), "dv")
    b.dv(dv2)
    dv2m = b.mil("dv")
    if globals().get("_STOP_STAGE") == "tp2":
        return

    def pe3(t):
        t.wait("dv", dv2m)
        t.matmul(ps[4][0:50, 0:512], luT, D1[:, 0:512], start=True, stop=True)
        t.inc(t.matmul(ps[5][0:50, 0:512], luT, D1[:, 512:1024],
                       start=True, stop=True), "pe")
    b.pe(pe3)
    pe3m = b.mil("pe")

    def dv3(v):
        v.wait("pe", pe3m)
        v.tensor_copy(D2[:, 0:512], ps[4][0:50, 0:512])
        v.inc(v.tensor_copy(D2[:, 512:1024], ps[5][0:50, 0:512]), "dv")
    b.dv(dv3)
    dv3m = b.mil("dv")
    if globals().get("_STOP_STAGE") == "lilora":
        return

    def pe4(t):
        t.wait("dv", dv3m)
        for c in range(8):
            if c >= 2:
                t.wait("ac", ac4base + 2 * (c - 1))
            t.matmul(ps[4 + 2 * (c % 2)][:, 0:512], D2[:, c * 128:(c + 1) * 128],
                     uaT[:, 0:512], start=True, stop=True)
            t.inc(t.matmul(ps[5 + 2 * (c % 2)][:, 0:512], D2[:, c * 128:(c + 1) * 128],
                           uaT[:, 512:1024], start=True, stop=True), "pe")
    ac4base = b.mil("ac")
    b.pe(pe4)
    pe4m = b.mil("pe")

    def ac4(a):
        for c in range(8):
            a.wait("pe", pe4m - 8 + c + 1)
            for jo in range(2):
                a.inc(a.activation(SA[:, c * 1024 + jo * 512:c * 1024 + (jo + 1) * 512],
                                   ps[4 + 2 * (c % 2) + jo][:, 0:512],
                                   AF.Copy, scale=0.5), "ac")
    b.ac(ac4)
    ac4m = b.mil("ac")

    dv4base = b.mil("dv")

    def pe5(t):
        t.wait("ac", ac4m)
        k = 0
        for c in range(8):
            for o in range(8):
                if k >= 2:
                    t.wait("dv", dv4base + k - 1)
                t.inc(t.transpose(ps[k % 2][:, 0:128],
                                  SB[:, o * 1024 + c * 128:o * 1024 + (c + 1) * 128],
                                  ident[:]), "pe")
                k += 1
    b.pe(pe5)
    pe5m = b.mil("pe")

    def dv5(v):
        v.wait("ac", ac4m)
        k = 0
        for c in range(8):
            for o in range(8):
                v.wait("pe", pe5m - 64 + k + 1)
                dst = SA[:, c * 1024 + o * 128:c * 1024 + (o + 1) * 128]
                v.inc(v.scalar_tensor_tensor(dst, ps[k % 2][:, 0:128], 0.5, dst,
                                             ALU.mult, ALU.add), "dv")
                k += 1
    b.dv(dv5)
    wfe_m = b.mil("dv")



    if globals().get("_STOP_AFTER_WFE"):
        b.x_dv = b.mil("dv")
        return
    dv0 = b.mil("dv")
    pe0 = b.mil("pe")

    def dvloop(v):
        v.wait("dv", ln_m)
        for i in range(NT):
            xt = SC[:, (i % 2) * 1024:(i % 2 + 1) * 1024]
            xf = SC[:, 2048 + (i % 2) * 1024:2048 + (i % 2 + 1) * 1024]
            if i >= 2:
                v.wait("pe", pe0 + 10 * (i - 2) + 10)
            v.inc(v.tensor_scalar(xt, xres[:, i * D:(i + 1) * D],
                                  m[:, i:i + 1], rs[:, i:i + 1],
                                  ALU.subtract, ALU.mult), "dv")
            for c in range(8):
                v.wait("pe", pe0 + 10 * i + c + 1)
                v.inc(v.tensor_copy(xf[:, c * 128:(c + 1) * 128], ps[c % 2][:, 0:128]), "dv")
            for oh in range(2):
                v.wait("pe", pe0 + 10 * i + 9 + oh)
                dst = xres[:, i * D + oh * 512:i * D + (oh + 1) * 512]
                v.inc(v.tensor_tensor(dst, ps[2 + 2 * (i % 2) + oh][:, 0:512], dst,
                                      ALU.add), "dv")
    peT = None

    def peloop(t):
        t.wait("dv", wfe_m)
        t.wait("wd", wd_m)
        for i in range(NT):
            xt = SC[:, (i % 2) * 1024:(i % 2 + 1) * 1024]
            xf = SC[:, 2048 + (i % 2) * 1024:2048 + (i % 2 + 1) * 1024]
            t.wait("dv", dv0 + 11 * i + 1)
            for c in range(8):
                if i >= 1 or c >= 2:
                    t.wait("dv", dv0 + 11 * i + c)
                t.inc(t.transpose(ps[c % 2][:, 0:128], xt[:, c * 128:(c + 1) * 128],
                                  ident[:]), "pe")
            t.wait("dv", dv0 + 11 * i + 9)
            if i >= 2:
                t.wait("dv", dv0 + 11 * (i - 2) + 11)
            for oh in range(2):
                bank = ps[2 + 2 * (i % 2) + oh]
                for c in range(8):
                    t.matmul(bank[:, 0:512], xf[:, c * 128:(c + 1) * 128],
                             SA[:, c * 1024 + oh * 512:c * 1024 + (oh + 1) * 512],
                             start=(c == 0), stop=False)
                t.inc(t.matmul(bank[:, 0:512], halfr[0:1, 0:128],
                               SD[0:1, oh * 512:(oh + 1) * 512],
                               start=False, stop=True), "pe")
    b.pe(peloop)
    b.dv(dvloop)
    b.expect("pe", pe0 + 10 * NT)
    b.expect("dv", dv0 + 11 * NT)
    b.x_dv = b.mil("dv")


def emit_l0attn(h, sub):
    b, dram = h.b, h.dram
    SA, SB, SC, SD, ps = h.SA, h.SB, h.SC, h.SD, h.ps
    xres, ident, scrA = h.xres, h.ident, h.scrA
    ent = {s: b.mil(s) for s in Bld.SEMS}

    m, rs, ln_m = h.ln_stats()

    def loads(sy):
        for s in ("pe", "dv", "ac"):
            sy.wait(s, ent[s])
        sy.inc(sy.dma_start(out=SC[:, 0:2048], in_=dram["C_T0"][:, :]), "wd", 16)
        sy.inc(sy.dma_start(out=SC[:, 2048:4096], in_=dram["C_T1"][:, :]), "wd", 16)
    b.sy(loads)
    tab_m = b.mil("wd")

    dv0, pe0, ac0 = b.mil("dv"), b.mil("pe"), b.mil("ac")

    def dvA(v):
        for s in ("pe", "ac"):
            v.wait(s, ent[s])
        v.wait("dv", ln_m)
        for i in range(NT):
            xt = SD[:, (i % 2) * 1024:(i % 2 + 1) * 1024]
            if i >= 2:
                v.wait("pe", pe0 + 8 * (i - 1))
            v.inc(v.tensor_scalar(xt, xres[:, i * D:(i + 1) * D],
                                  m[:, i:i + 1], rs[:, i:i + 1],
                                  ALU.subtract, ALU.mult), "dv")

    def peA(t):
        t.wait("ac", ent["ac"])
        for i in range(NT):
            xt = SD[:, (i % 2) * 1024:(i % 2 + 1) * 1024]
            t.wait("dv", dv0 + i + 1)
            for c in range(8):
                k = 8 * i + c
                if k >= 2:
                    t.wait("ac", ac0 + k - 1)
                t.inc(t.transpose(ps[k % 2][:, 0:128], xt[:, c * 128:(c + 1) * 128],
                                  ident[:]), "pe")

    def acA(a):
        a.wait("dv", ent["dv"])
        a.wait("pe", ent["pe"])
        for i in range(NT):
            for c in range(8):
                k = 8 * i + c
                a.wait("pe", pe0 + k + 1)
                a.inc(a.activation(SA[:, c * 2048 + i * 128:c * 2048 + (i + 1) * 128],
                                   ps[k % 2][:, 0:128], AF.Copy), "ac")
    b.dv(dvA)
    b.pe(peA)
    b.ac(acA)
    xlnf_m = b.mil("ac")

    outbank_guard = {0: 0, 1: 0}
    WNM = ["L0_attn_wq", "L0_attn_wk", "L0_attn_wv", "L0_attn_wg", "L0_attn_wo"]
    wo3 = dram["L0_attn_wo"].rearrange("(c p) j -> p c j", p=128)
    dec3 = dram["C_decTx"].rearrange("(g u) q -> u g q", u=128)

    for p in range(8):
        wT_ms = {}
        for wi, nm in enumerate(WNM):
            dv_pre = b.mil("dv")

            def w_sy(sy, nm=nm, wi=wi, p=p):
                sy.wait("pe", b.mil("pe"))
                if wi == 4:
                    sy.inc(sy.dma_start(out=SD[:, 0:1024],
                                        in_=wo3[:, :, p * 128:(p + 1) * 128]), "wd", 16)
                else:
                    sy.inc(sy.dma_start(out=SD[:, 0:1024],
                                        in_=dram[nm][p * 128:(p + 1) * 128, :]), "wd", 16)
            b.sy(w_sy)
            wdm = b.mil("wd")
            pe_tb = b.mil("pe")

            def w_pe(t, wdm=wdm, dv_pre=dv_pre):
                t.wait("wd", wdm)
                t.wait("dv", dv_pre)
                for c in range(8):
                    if c >= 2:
                        t.wait("dv", dv_pre + c - 1)
                    t.inc(t.transpose(ps[6 + c % 2][:, 0:128],
                                      SD[:, c * 128:(c + 1) * 128], ident[:]), "pe")
            b.pe(w_pe)

            def w_dv(v, pe_tb=pe_tb, wi=wi):
                for c in range(8):
                    v.wait("pe", pe_tb + c + 1)
                    dst = (SC[:, 4096 + c * 128:4096 + (c + 1) * 128] if wi == 4
                           else SD[:, 1024 + c * 128:1024 + (c + 1) * 128])
                    v.inc(v.tensor_copy(dst, ps[6 + c % 2][:, 0:128]), "dv")
            b.dv(w_dv)
            wT_m = b.mil("dv")
            wT_ms[wi] = wT_m
            if wi == 4:
                break

            if wi in (0, 1, 3):
                dstb = {0: 0, 1: 2048, 3: 6144}[wi]
                pe_pb = b.mil("pe")
                dv_eb = b.mil("dv")
                ac_eb = b.mil("ac")
                evs_per_tile = 10 if wi in (0, 1) else 0

                def w_proj(t, wT_m=wT_m, pe_pb=pe_pb, dv_eb=dv_eb, ac_eb=ac_eb,
                           wi=wi, evs=evs_per_tile, ac_now=None):
                    t.wait("dv", wT_m)
                    t.wait("ac", ac_now if ac_now else xlnf_m)
                    t.wait("wd", tab_m)
                    for tt in range(4):
                        if tt >= 2:
                            if wi in (0, 1):
                                t.wait("dv", dv_eb + evs * (tt - 1))
                            else:
                                t.wait("ac", ac_eb + (tt - 1))
                        bank = ps[2 + tt % 2]
                        for c in range(7):
                            t.matmul(bank[:, 0:512],
                                     SD[:, 1024 + c * 128:1024 + (c + 1) * 128],
                                     SA[:, c * 2048 + tt * 512:c * 2048 + (tt + 1) * 512],
                                     start=(c == 0), stop=False)
                        t.inc(t.matmul(bank[:, 0:512],
                                       SD[:, 1024 + 7 * 128:1024 + 8 * 128],
                                       SA[:, 7 * 2048 + tt * 512:7 * 2048 + (tt + 1) * 512],
                                       start=False, stop=True), "pe")
                b.pe(lambda t, f=w_proj, a=b.mil("ac"): f(t, ac_now=a))
                pe_pm = b.mil("pe")

                if wi in (0, 1):
                    tb = 0 if wi == 0 else 2048

                    def w_rope(v, pe_pb=pe_pb, dstb=dstb, tb=tb):
                        for tt in range(4):
                            v.wait("pe", pe_pb + tt + 1)
                            bank = ps[2 + tt % 2]
                            cols = slice(tt * 512, (tt + 1) * 512)
                            tcols = slice(tb + tt * 512, tb + (tt + 1) * 512)
                            for hh in range(2):
                                base = 64 * hh
                                qv = SB[base:base + 64, dstb + tt * 512:dstb + (tt + 1) * 512]
                                qlo = SB[base:base + 32, dstb + tt * 512:dstb + (tt + 1) * 512]
                                qhi = SB[base + 32:base + 64, dstb + tt * 512:dstb + (tt + 1) * 512]
                                v.link(v.tensor_tensor(qv, bank[base:base + 64, 0:512],
                                                       SC[0:64, tcols], ALU.mult))
                                v.link(v.tensor_tensor(scrA[base:base + 32, 0:512],
                                                       bank[base + 32:base + 64, 0:512],
                                                       SC[64:96, tcols], ALU.mult))
                                v.link(v.tensor_tensor(qlo, qlo, scrA[base:base + 32, 0:512],
                                                       ALU.subtract))
                                v.link(v.tensor_tensor(scrA[base + 32:base + 64, 0:512],
                                                       bank[base:base + 32, 0:512],
                                                       SC[96:128, tcols], ALU.mult))
                                v.inc(v.tensor_tensor(qhi, qhi, scrA[base + 32:base + 64, 0:512],
                                                      ALU.add), "dv")
                    b.dv(w_rope)
                else:
                    def w_gev(a, pe_pb=pe_pb, dstb=dstb):
                        for tt in range(4):
                            a.wait("pe", pe_pb + tt + 1)
                            a.inc(a.activation(SB[:, dstb + tt * 512:dstb + (tt + 1) * 512],
                                               ps[2 + tt % 2][:, 0:512], AF.Silu), "ac")
                    b.ac(w_gev)
            else:
                pe_pb = b.mil("pe")
                ac_eb = b.mil("ac")

                def v_proj(t, wT_m=wT_m, pe_pb=pe_pb, ac_eb=ac_eb, ac_now=None):
                    t.wait("dv", wT_m)
                    t.wait("ac", ac_now if ac_now else xlnf_m)
                    for tt in range(NT):
                        if tt >= 2:
                            t.wait("ac", ac_eb + tt - 1)
                        bank = ps[2 + tt % 2]
                        for c in range(7):
                            t.matmul(bank[:, 0:128],
                                     SA[:, c * 2048 + tt * 128:c * 2048 + (tt + 1) * 128],
                                     SD[:, 1024 + c * 128:1024 + (c + 1) * 128],
                                     start=(c == 0), stop=False)
                        t.inc(t.matmul(bank[:, 0:128],
                                       SA[:, 7 * 2048 + tt * 128:7 * 2048 + (tt + 1) * 128],
                                       SD[:, 1024 + 7 * 128:1024 + 8 * 128],
                                       start=False, stop=True), "pe")
                b.pe(lambda t, f=v_proj, a=b.mil("ac"): f(t, ac_now=a))
                pe_pm = b.mil("pe")

                def v_ev(a, pe_pb=pe_pb):
                    for tt in range(NT):
                        a.wait("pe", pe_pb + tt + 1)
                        a.inc(a.activation(SB[:, 4096 + tt * 128:4096 + (tt + 1) * 128],
                                           ps[2 + tt % 2][:, 0:128], AF.Copy), "ac")
                b.ac(v_ev)

        qkvg_dv = b.mil("dv")
        qkvg_ac = b.mil("ac")

        for hh in range(2):
            head = 2 * p + hh
            base = 64 * hh

            def d_sy(sy, head=head):
                sy.wait("dv", b.mil("dv"))
                sy.inc(sy.dma_start(out=SD[:, 2048:4480],
                                    in_=dec3[:, head * 19:(head + 1) * 19, :]), "wd", 16)
            b.sy(d_sy)
            decm = b.mil("wd")

            for jq in range(4):
                nu = min(4 * jq + 4, NT)
                pe_sb = b.mil("pe")
                dv_wb = b.mil("dv")

                obg = outbank_guard[jq % 2]

                def s_pe(t, jq=jq, nu=nu, pe_sb=pe_sb, dv_wb=dv_wb, base=base,
                         decm=decm, qkvg_dv=qkvg_dv, qkvg_ac=qkvg_ac, obg=obg):
                    t.wait("dv", max(qkvg_dv, dv_wb, obg))
                    t.wait("ac", qkvg_ac)
                    t.wait("wd", decm)
                    for iu in range(nu):
                        if iu >= 2:
                            t.wait("dv", dv_wb + iu - 1)
                        t.inc(t.matmul(ps[iu % 2][:, 0:512],
                                       SB[base:base + 64, 2048 + iu * 128:2048 + (iu + 1) * 128],
                                       SB[base:base + 64, jq * 512:(jq + 1) * 512],
                                       start=True, stop=True), "pe")
                        t.wait("dv", dv_wb + iu + 1)
                        t.inc(t.matmul(ps[4 + jq % 2][0:64, 0:512],
                                       SB[:, 4096 + iu * 128 + base:4096 + iu * 128 + base + 64],
                                       scrA[:, (iu % 2) * 512:(iu % 2 + 1) * 512],
                                       start=(iu == 0), stop=(iu == nu - 1)), "pe")
                b.pe(s_pe)
                pe_sm = b.mil("pe")

                def s_dv(v, jq=jq, nu=nu, pe_sb=pe_sb, head=head):
                    for iu in range(nu):
                        v.wait("pe", pe_sb + 2 * iu + 1)
                        d0 = 4 * jq - iu + 3
                        v.inc(v.tensor_tensor(scrA[:, (iu % 2) * 512:(iu % 2 + 1) * 512],
                                              ps[iu % 2][:, 0:512],
                                              SD[:, 2048 + d0 * 128:2048 + (d0 + 4) * 128],
                                              ALU.mult), "dv")
                b.dv(s_dv)

                pe_all = b.mil("pe")

                def z_dv(v, jq=jq, base=base, pe_all=pe_all):
                    v.wait("pe", pe_all)
                    zsl = SB[base:base + 64, 6144 + jq * 512:6144 + (jq + 1) * 512]
                    v.inc(v.tensor_tensor(zsl, ps[4 + jq % 2][0:64, 0:512], zsl,
                                          ALU.mult), "dv")
                b.dv(z_dv)
                outbank_guard[jq % 2] = b.mil("dv")

        z_dvm = b.mil("dv")
        wo_m = wT_ms[4]
        pe_wb = b.mil("pe")
        dv_rb = b.mil("dv")

        def wo_pe(t, z_dvm=z_dvm, wo_m=wo_m, pe_wb=pe_wb, dv_rb=dv_rb):
            t.wait("dv", max(z_dvm, wo_m))
            t.wait("ac", b.mil("ac"))
            k = 0
            for tt in range(NT):
                for oh in range(2):
                    if k >= 2:
                        t.wait("dv", dv_rb + k - 1)
                    t.inc(t.matmul(ps[2 + k % 2][:, 0:512],
                                   SB[:, 6144 + tt * 128:6144 + (tt + 1) * 128],
                                   SC[:, 4096 + oh * 512:4096 + (oh + 1) * 512],
                                   start=True, stop=True), "pe")
                    k += 1
        b.pe(wo_pe)
        pe_wm = b.mil("pe")

        def wo_dv(v, pe_wb=pe_wb):
            k = 0
            for tt in range(NT):
                for oh in range(2):
                    v.wait("pe", pe_wb + k + 1)
                    dst = xres[:, tt * D + oh * 512:tt * D + (oh + 1) * 512]
                    v.inc(v.tensor_tensor(dst, ps[2 + k % 2][:, 0:512], dst,
                                          ALU.add), "dv")
                    k += 1
        b.dv(wo_dv)

    b.x_dv = b.mil("dv")


def emit_l1ffn(h, sub):
    if globals().get("_STUB_L1FFN", True):
        return
    raise NotImplementedError("L1 ffn real implementation")


def emit_l1attn(h, sub):
    b, dram = h.b, h.dram
    SA, SB, SC, SD, ps = h.SA, h.SB, h.SC, h.SD, h.ps
    xres, ident, scrA, onesr, onesc = h.xres, h.ident, h.scrA, h.onesr, h.onesc
    ent = {s: b.mil(s) for s in Bld.SEMS}

    m, rs, ln_m = h.ln_stats()

    def loads(sy):
        for s in ("pe", "dv", "ac"):
            sy.wait(s, ent[s])
        sy.inc(sy.dma_start(out=SC[:, 0:2048], in_=dram["C_U0h"][:, :]), "wd", 16)
        sy.inc(sy.dma_start(out=SC[:, 2048:4096], in_=dram["C_U1h"][:, :]), "wd", 16)
        sy.inc(sy.dma_start(out=SB[:, 0:2048], in_=dram["C_U0l"][:, :]), "wd", 16)
        sy.inc(sy.dma_start(out=SB[:, 2048:4096], in_=dram["C_U1l"][:, :]), "wd", 16)
        sy.inc(sy.dma_start(out=SC[0:1, 4096:5120], in_=dram["L1_attn_bout"][:, :]), "wd", 16)
    b.sy(loads)
    tab_m = b.mil("wd")

    def tabs_dv(v):
        v.wait("wd", tab_m)
        for s in ("pe", "ac"):
            v.wait(s, ent[s])
        v.link(v.tensor_tensor(SC[:, 0:2048], SC[:, 0:2048], SB[:, 0:2048], ALU.add))
        v.inc(v.tensor_tensor(SC[:, 2048:4096], SC[:, 2048:4096], SB[:, 2048:4096],
                              ALU.add), "dv")
    b.dv(tabs_dv)
    tabc_m = b.mil("dv")

    dv0, pe0, ac0 = b.mil("dv"), b.mil("pe"), b.mil("ac")

    def dvA(v):
        v.wait("dv", ln_m)
        for i in range(NT):
            xt = SD[:, (i % 2) * 1024:(i % 2 + 1) * 1024]
            if i >= 2:
                v.wait("pe", pe0 + 8 * (i - 1))
            v.inc(v.tensor_scalar(xt, xres[:, i * D:(i + 1) * D],
                                  m[:, i:i + 1], rs[:, i:i + 1],
                                  ALU.subtract, ALU.mult), "dv")

    def peA(t):
        t.wait("ac", ent["ac"])
        for i in range(NT):
            xt = SD[:, (i % 2) * 1024:(i % 2 + 1) * 1024]
            t.wait("dv", dv0 + i + 1)
            for c in range(8):
                k = 8 * i + c
                if k >= 2:
                    t.wait("ac", ac0 + k - 1)
                t.inc(t.transpose(ps[k % 2][:, 0:128], xt[:, c * 128:(c + 1) * 128],
                                  ident[:]), "pe")

    def acA(a):
        a.wait("dv", ent["dv"])
        a.wait("pe", ent["pe"])
        for i in range(NT):
            for c in range(8):
                k = 8 * i + c
                a.wait("pe", pe0 + k + 1)
                a.inc(a.activation(SA[:, c * 2048 + i * 128:c * 2048 + (i + 1) * 128],
                                   ps[k % 2][:, 0:128], AF.Copy), "ac")
    b.dv(dvA)
    b.pe(peA)
    b.ac(acA)
    xlnf_m = b.mil("ac")

    wq3 = dram["L1_attn_Wq"].rearrange("(c p) j -> p c j", p=128)
    wkv3 = dram["L1_attn_Wkv"].rearrange("(c p) j -> p c j", p=128)
    outbank_guard = {0: 0, 1: 0}
    sumbank_guard = {0: 0, 1: 0}

    for p in range(8):
        for wi, w3 in enumerate([wq3, wkv3]):
            sdoff = wi * 1024
            dv_pre = b.mil("dv")

            def w_sy(sy, w3=w3, sdoff=sdoff, p=p):
                sy.wait("pe", b.mil("pe"))
                sy.inc(sy.dma_start(out=SD[:, sdoff:sdoff + 1024],
                                    in_=w3[:, :, p * 128:(p + 1) * 128]), "wd", 16)
            b.sy(w_sy)
            wdm = b.mil("wd")
            pe_pb = b.mil("pe")
            dv_eb = b.mil("dv")

            def w_proj(t, wdm=wdm, sdoff=sdoff, pe_pb=pe_pb, dv_eb=dv_eb,
                       ac_now=b.mil("ac")):
                t.wait("wd", wdm)
                t.wait("dv", max(tabc_m, b.waited.get(("tensor", "dv"), 0), dv_eb))
                t.wait("ac", max(xlnf_m, ac_now))
                for tt in range(4):
                    if tt >= 2:
                        t.wait("dv", dv_eb + 10 * (tt - 1))
                    bank = ps[2 + tt % 2]
                    for c in range(7):
                        t.matmul(bank[:, 0:512], SD[:, sdoff + c * 128:sdoff + (c + 1) * 128],
                                 SA[:, c * 2048 + tt * 512:c * 2048 + (tt + 1) * 512],
                                 start=(c == 0), stop=False)
                    t.inc(t.matmul(bank[:, 0:512], SD[:, sdoff + 7 * 128:sdoff + 8 * 128],
                                   SA[:, 7 * 2048 + tt * 512:7 * 2048 + (tt + 1) * 512],
                                   start=False, stop=True), "pe")
            b.pe(w_proj)

            dstb = wi * 2048
            tb = 0 if wi == 0 else 2048

            def w_rope(v, pe_pb=pe_pb, dstb=dstb, tb=tb):
                for tt in range(4):
                    v.wait("pe", pe_pb + tt + 1)
                    bank = ps[2 + tt % 2]
                    tcols = slice(tb + tt * 512, tb + (tt + 1) * 512)
                    for hh in range(2):
                        base = 64 * hh
                        qv = SB[base:base + 64, dstb + tt * 512:dstb + (tt + 1) * 512]
                        qlo = SB[base:base + 32, dstb + tt * 512:dstb + (tt + 1) * 512]
                        qhi = SB[base + 32:base + 64, dstb + tt * 512:dstb + (tt + 1) * 512]
                        v.link(v.tensor_tensor(qv, bank[base:base + 64, 0:512],
                                               SC[0:64, tcols], ALU.mult))
                        v.link(v.tensor_tensor(scrA[base:base + 32, 0:512],
                                               bank[base + 32:base + 64, 0:512],
                                               SC[64:96, tcols], ALU.mult))
                        v.link(v.tensor_tensor(qlo, qlo, scrA[base:base + 32, 0:512],
                                               ALU.subtract))
                        v.link(v.tensor_tensor(scrA[base + 32:base + 64, 0:512],
                                               bank[base:base + 32, 0:512],
                                               SC[96:128, tcols], ALU.mult))
                        v.inc(v.tensor_tensor(qhi, qhi, scrA[base + 32:base + 64, 0:512],
                                              ALU.add), "dv")
            b.dv(w_rope)

        kvf_dv = b.mil("dv")
        pe_kb = b.mil("pe")
        ac_kb = b.mil("ac")

        def kv_pe(t, kvf_dv=kvf_dv, ac_kb=ac_kb):
            t.wait("dv", kvf_dv)
            for tt in range(NT):
                if tt >= 2:
                    t.wait("ac", ac_kb + tt - 1)
                t.inc(t.transpose(ps[2 + tt % 2][:, 0:128],
                                  SB[:, 2048 + tt * 128:2048 + (tt + 1) * 128],
                                  ident[:]), "pe")
        b.pe(kv_pe)

        def kv_ac(a, pe_kb=pe_kb):
            for tt in range(NT):
                a.wait("pe", pe_kb + tt + 1)
                a.inc(a.activation(SB[:, 4096 + tt * 128:4096 + (tt + 1) * 128],
                                   ps[2 + tt % 2][:, 0:128], AF.Copy), "ac")
        b.ac(kv_ac)

        def wo_sy(sy, p=p):
            sy.wait("pe", b.mil("pe"))
            sy.inc(sy.dma_start(out=SD[:, 0:1024],
                                in_=dram["L1_attn_Wout"][p * 128:(p + 1) * 128, :]), "wd", 16)
        b.sy(wo_sy)
        wout_m = b.mil("wd")
        qkv_dv = b.mil("dv")
        qkv_ac = b.mil("ac")

        for hh in range(2):
            base = 64 * hh
            for jq in range(4):
                pe_sb = b.mil("pe")
                ac_eb = b.mil("ac")
                obg = outbank_guard[jq % 2]
                sbg = sumbank_guard[jq % 2]

                def s_pe(t, jq=jq, base=base, pe_sb=pe_sb, ac_eb=ac_eb,
                         obg=obg, sbg=sbg, qkv_dv=qkv_dv, qkv_ac=qkv_ac):
                    t.wait("dv", max(qkv_dv, obg, sbg))
                    t.wait("ac", qkv_ac)
                    for iu in range(NT):
                        if iu >= 2:
                            t.wait("ac", ac_eb + iu - 1)
                        t.inc(t.matmul(ps[iu % 2][:, 0:512],
                                       SB[base:base + 64, 2048 + iu * 128:2048 + (iu + 1) * 128],
                                       SB[base:base + 64, jq * 512:(jq + 1) * 512],
                                       start=True, stop=True), "pe")
                        t.wait("ac", ac_eb + iu + 1)
                        t.matmul(ps[4 + jq % 2][0:64, 0:512],
                                 SB[:, 4096 + iu * 128 + base:4096 + iu * 128 + base + 64],
                                 scrA[:, (iu % 2) * 512:(iu % 2 + 1) * 512],
                                 start=(iu == 0), stop=(iu == NT - 1))
                        t.inc(t.matmul(ps[6 + jq % 2][0:1, 0:512], onesc[:],
                                       scrA[:, (iu % 2) * 512:(iu % 2 + 1) * 512],
                                       start=(iu == 0), stop=(iu == NT - 1)), "pe")
                b.pe(s_pe)
                pe_sm = b.mil("pe")

                def s_ac(a, pe_sb=pe_sb, jq=jq):
                    for iu in range(NT):
                        a.wait("pe", pe_sb + 2 * iu + 1)
                        a.inc(a.activation(scrA[:, (iu % 2) * 512:(iu % 2 + 1) * 512],
                                           ps[iu % 2][:, 0:512], AF.Exp), "ac")
                b.ac(s_ac)

                dv_rb = b.mil("dv")

                def e_dv1(v, pe_sm=pe_sm, jq=jq):
                    v.wait("pe", pe_sm)
                    v.inc(v.reciprocal(SD[0:1, 2048:2560], ps[6 + jq % 2][0:1, 0:512]), "dv")
                b.dv(e_dv1)
                rec_m = b.mil("dv")
                pe_bb = b.mil("pe")

                def e_pe(t, rec_m=rec_m, base=base):
                    t.wait("dv", rec_m)
                    t.inc(t.matmul(ps[2][base:base + 64, 0:512], onesr[0:1, 0:64],
                                   SD[0:1, 2048:2560], start=True, stop=True), "pe")
                b.pe(e_pe)
                pe_bm = b.mil("pe")

                def e_dv2(v, pe_bm=pe_bm, base=base, jq=jq):
                    v.wait("pe", pe_bm)
                    v.link(v.tensor_copy(SD[base:base + 64, 2560:3072],
                                         ps[2][base:base + 64, 0:512]))
                    zsl = SB[base:base + 64, 6144 + jq * 512:6144 + (jq + 1) * 512]
                    v.inc(v.tensor_tensor(zsl, ps[4 + jq % 2][0:64, 0:512],
                                          SD[base:base + 64, 2560:3072], ALU.mult), "dv")
                b.dv(e_dv2)
                outbank_guard[jq % 2] = b.mil("dv")
                sumbank_guard[jq % 2] = rec_m

        z_dvm = b.mil("dv")
        pe_wb = b.mil("pe")
        dv_rb = b.mil("dv")

        def wo_pe(t, z_dvm=z_dvm, wout_m=wout_m, pe_wb=pe_wb, dv_rb=dv_rb,
                  p=p, ac_now=b.mil("ac")):
            t.wait("dv", z_dvm)
            t.wait("wd", wout_m)
            t.wait("ac", ac_now)
            k = 0
            for tt in range(NT):
                for oh in range(2):
                    if k >= 2:
                        t.wait("dv", dv_rb + k - 1)
                    bank = ps[2 + k % 2]
                    if p == 0:
                        t.matmul(bank[:, 0:512],
                                 SB[:, 6144 + tt * 128:6144 + (tt + 1) * 128],
                                 SD[:, oh * 512:(oh + 1) * 512],
                                 start=True, stop=False)
                        t.inc(t.matmul(bank[:, 0:512], onesr[0:1, 0:128],
                                       SC[0:1, 4096 + oh * 512:4096 + (oh + 1) * 512],
                                       start=False, stop=True), "pe")
                    else:
                        t.inc(t.matmul(bank[:, 0:512],
                                       SB[:, 6144 + tt * 128:6144 + (tt + 1) * 128],
                                       SD[:, oh * 512:(oh + 1) * 512],
                                       start=True, stop=True), "pe")
                    k += 1
        b.pe(wo_pe)
        pe_wm = b.mil("pe")

        def wo_dv(v, pe_wb=pe_wb):
            k = 0
            for tt in range(NT):
                for oh in range(2):
                    v.wait("pe", pe_wb + k + 1)
                    dst = xres[:, tt * D + oh * 512:tt * D + (oh + 1) * 512]
                    v.inc(v.tensor_tensor(dst, ps[2 + k % 2][:, 0:512], dst,
                                          ALU.add), "dv")
                    k += 1
        b.dv(wo_dv)

    b.x_dv = b.mil("dv")


def emit_l2ffn(h, sub):
    b, dram = h.b, h.dram
    SA, SB, SC, SD, ps = h.SA, h.SB, h.SC, h.SD, h.ps
    xres, ident, onesr, smalls = h.xres, h.ident, h.onesr, h.smalls
    ent = {s: b.mil(s) for s in Bld.SEMS}

    m, rs, ln_m = h.ln_stats()

    def loads(sy):
        for s in ("pe", "dv", "ac"):
            sy.wait(s, ent[s])
        for c in range(8):
            sy.inc(sy.dma_start(out=SB[:, c * 256:(c + 1) * 256],
                                in_=dram["L2_ffn_U1"][c * 128:(c + 1) * 128, :]), "wd", 16)
        for rc in range(2):
            sy.inc(sy.dma_start(out=SA[:, rc * 4096:(rc + 1) * 4096],
                                in_=dram["L2_ffn_V1"][rc * 128:(rc + 1) * 128, :]), "wd", 16)
        for hc in range(32):
            sy.inc(sy.dma_start(out=SA[:, 8192 + hc * 256:8192 + (hc + 1) * 256],
                                in_=dram["L2_ffn_U2"][hc * 128:(hc + 1) * 128, :]), "wd", 16)
        for rc in range(2):
            sy.inc(sy.dma_start(out=SB[:, 2048 + rc * 1024:2048 + (rc + 1) * 1024],
                                in_=dram["L2_ffn_V2"][rc * 128:(rc + 1) * 128, :]), "wd", 16)
        sy.inc(sy.dma_start(out=SC[0:1, 0:4096], in_=dram["L2_ffn_b1"][:, :]), "wd", 16)
        sy.inc(sy.dma_start(out=SD[0:1, 0:1024], in_=dram["L2_ffn_b2"][:, :]), "wd", 16)
    b.sy(loads)
    wd_m = b.mil("wd")

    pe_bb = b.mil("pe")
    dv_bb = b.mil("dv")

    def b1_pe(t):
        t.wait("wd", wd_m)
        t.wait("dv", max(ent["dv"], b.x_dv))
        t.wait("ac", ent["ac"])
        for hc in range(32):
            if hc >= 2:
                t.wait("dv", dv_bb + hc - 1)
            t.inc(t.transpose(ps[hc % 2][0:128, 0:1],
                              SC[0:1, hc * 128:(hc + 1) * 128], ident[0:1, 0:1]), "pe")
    b.pe(b1_pe)

    def b1_dv(v):
        for hc in range(32):
            v.wait("pe", pe_bb + hc + 1)
            v.inc(v.tensor_copy(smalls[:, hc:hc + 1], ps[hc % 2][0:128, 0:1]), "dv")
    b.dv(b1_dv)
    b1_m = b.mil("dv")

    dv0, pe0, ac0 = b.mil("dv"), b.mil("pe"), b.mil("ac")
    DVN, PEN, ACN = 7, 46, 40

    def dvloop(v):
        v.wait("dv", max(ln_m, b1_m))
        for i in range(NT):
            xt = SC[:, (i % 2) * 1024:(i % 2 + 1) * 1024]
            pfo = 4096 + (i % 2) * 128
            qfo = 4608 + (i % 2) * 128
            if i >= 2:
                v.wait("pe", pe0 + PEN * (i - 1))
            v.inc(v.tensor_scalar(xt, xres[:, i * D:(i + 1) * D],
                                  m[:, i:i + 1], rs[:, i:i + 1],
                                  ALU.subtract, ALU.mult), "dv")
            for rc in range(2):
                v.wait("pe", pe0 + PEN * i + 8 + rc + 1)
                v.inc(v.tensor_copy(SC[:, 4096 + (i % 2) * 256 + rc * 128:
                                       4096 + (i % 2) * 256 + (rc + 1) * 128],
                                    ps[2 + rc][:, 0:128]), "dv")
            for rc in range(2):
                v.wait("pe", pe0 + PEN * i + 42 + rc + 1)
                v.inc(v.tensor_copy(SC[:, 4608 + (i % 2) * 256 + rc * 128:
                                       4608 + (i % 2) * 256 + (rc + 1) * 128],
                                    ps[6 + rc][:, 0:128]), "dv")
            for oh in range(2):
                v.wait("pe", pe0 + PEN * i + 44 + oh + 1)
                dst = xres[:, i * D + oh * 512:i * D + (oh + 1) * 512]
                v.inc(v.scalar_tensor_tensor(dst, ps[2 + oh][:, 0:512], 0.5, dst,
                                             ALU.mult, ALU.add), "dv")

    def peloop(t):
        t.wait("dv", b1_m)
        t.wait("wd", wd_m)
        for i in range(NT):
            xt = SC[:, (i % 2) * 1024:(i % 2 + 1) * 1024]
            xfb = 2048 + (i % 2) * 1024
            pfb = 4096 + (i % 2) * 256
            qfb = 4608 + (i % 2) * 256
            dvb = dv0 + DVN * i
            acb = ac0 + ACN * i
            t.wait("dv", dvb + 1)
            for c in range(8):
                k = 8 * i + c
                if k >= 2:
                    t.wait("ac", ac0 + ACN * (i if c >= 2 else i - 1) +
                           (c - 2 + 1 if c >= 2 else 7 + c))
                t.inc(t.transpose(ps[k % 2][:, 0:128], xt[:, c * 128:(c + 1) * 128],
                                  ident[:]), "pe")
            t.wait("ac", acb + 8)
            if i >= 1:
                t.wait("dv", dv0 + DVN * (i - 1) + 7)
            for rc in range(2):
                for c in range(7):
                    t.matmul(ps[2 + rc][:, 0:128],
                             SB[:, c * 256 + rc * 128:c * 256 + (rc + 1) * 128],
                             SC[:, xfb + c * 128:xfb + (c + 1) * 128],
                             start=(c == 0), stop=False)
                t.inc(t.matmul(ps[2 + rc][:, 0:128],
                               SB[:, 7 * 256 + rc * 128:7 * 256 + (rc + 1) * 128],
                               SC[:, xfb + 7 * 128:xfb + 8 * 128],
                               start=False, stop=True), "pe")
            t.wait("dv", dvb + 3)
            for hc in range(32):
                kk = 32 * i + hc
                if kk >= 2:
                    t.wait("ac", ac0 + ACN * (i if hc >= 2 else i - 1) +
                           (8 + hc - 1 if hc >= 2 else 39 + hc))
                t.matmul(ps[4 + hc % 2][:, 0:128],
                         SA[:, hc * 128:(hc + 1) * 128],
                         SC[:, pfb:pfb + 128], start=True, stop=False)
                t.inc(t.matmul(ps[4 + hc % 2][:, 0:128],
                               SA[:, 4096 + hc * 128:4096 + (hc + 1) * 128],
                               SC[:, pfb + 128:pfb + 256],
                               start=False, stop=True), "pe")
            t.wait("ac", acb + 40)
            for rc in range(2):
                for hc in range(31):
                    t.matmul(ps[6 + rc][:, 0:128],
                             SA[:, 8192 + hc * 256 + rc * 128:8192 + hc * 256 + (rc + 1) * 128],
                             SB[:, 4096 + hc * 128:4096 + (hc + 1) * 128],
                             start=(hc == 0), stop=False)
                t.inc(t.matmul(ps[6 + rc][:, 0:128],
                               SA[:, 8192 + 31 * 256 + rc * 128:8192 + 31 * 256 + (rc + 1) * 128],
                               SB[:, 4096 + 31 * 128:4096 + 32 * 128],
                               start=False, stop=True), "pe")
            t.wait("dv", dvb + 5)
            for oh in range(2):
                for rc in range(2):
                    t.matmul(ps[2 + oh][:, 0:512],
                             SC[:, qfb + rc * 128:qfb + (rc + 1) * 128],
                             SB[:, 2048 + rc * 1024 + oh * 512:2048 + rc * 1024 + (oh + 1) * 512],
                             start=(rc == 0), stop=False)
                t.inc(t.matmul(ps[2 + oh][:, 0:512], onesr[0:1, 0:128],
                               SD[0:1, oh * 512:(oh + 1) * 512],
                               start=False, stop=True), "pe")

    def acloop(a):
        a.wait("wd", wd_m)
        a.wait("dv", b1_m)
        for i in range(NT):
            xfb = 2048 + (i % 2) * 1024
            acb = ac0 + ACN * i
            for c in range(8):
                a.wait("pe", pe0 + PEN * i + c + 1)
                a.inc(a.activation(SC[:, xfb + c * 128:xfb + (c + 1) * 128],
                                   ps[(8 * i + c) % 2][:, 0:128], AF.Copy), "ac")
            for hc in range(32):
                a.wait("pe", pe0 + PEN * i + 10 + hc + 1)
                a.inc(a.activation(SB[:, 4096 + hc * 128:4096 + (hc + 1) * 128],
                                   ps[4 + hc % 2][:, 0:128], AF.Gelu_apprx_tanh,
                                   bias=smalls[:, hc:hc + 1]), "ac")
    b.pe(peloop)
    b.dv(dvloop)
    b.ac(acloop)
    b.expect("pe", pe0 + PEN * NT)
    b.expect("dv", dv0 + DVN * NT)
    b.expect("ac", ac0 + ACN * NT)
    b.x_dv = b.mil("dv")


def emit_l2attn(h, sub):
    b, dram = h.b, h.dram
    SA, SB, SC, SD, ps = h.SA, h.SB, h.SC, h.SD, h.ps
    xres, ident, scrA, onesr, onesc, smalls, iop = (
        h.xres, h.ident, h.scrA, h.onesr, h.onesc, h.smalls, h.iop)
    ent = {s: b.mil(s) for s in Bld.SEMS}

    m, rs, ln_m = h.ln_stats()

    def loads(sy):
        for s in ("pe", "dv", "ac"):
            sy.wait(s, ent[s])
        sy.inc(sy.dma_start(out=smalls[0:1, 176:192], in_=dram["L2_attn_A_log"][:, :]), "wd", 16)
        sy.inc(sy.dma_start(out=SC[0:1, 2176:3200], in_=dram["L2_attn_dt_bias"][:, :]), "wd", 16)
        sy.inc(sy.dma_start(out=SC[0:1, 3200:3264], in_=dram["L2_attn_norm_w"][:, :]), "wd", 16)
        sy.inc(sy.dma_start(out=SC[0:1, 4096:5120], in_=dram["L2_attn_bg"][:, :]), "wd", 16)
        for c in range(8):
            sy.inc(sy.dma_start(out=SC[:, 2048 + c * 16:2048 + (c + 1) * 16],
                                in_=dram["L2_attn_Wb"][c * 128:(c + 1) * 128, :]), "wd", 16)
    b.sy(loads)
    wd_m = b.mil("wd")

    dv0, pe0, ac0 = b.mil("dv"), b.mil("pe"), b.mil("ac")

    def dvA(v):
        v.wait("dv", ln_m)
        for i in range(NT):
            xt = SD[:, (i % 2) * 1024:(i % 2 + 1) * 1024]
            if i >= 2:
                v.wait("pe", pe0 + 8 * (i - 1))
            v.inc(v.tensor_scalar(xt, xres[:, i * D:(i + 1) * D],
                                  m[:, i:i + 1], rs[:, i:i + 1],
                                  ALU.subtract, ALU.mult), "dv")

    def peA(t):
        t.wait("ac", ent["ac"])
        for i in range(NT):
            xt = SD[:, (i % 2) * 1024:(i % 2 + 1) * 1024]
            t.wait("dv", dv0 + i + 1)
            for c in range(8):
                k = 8 * i + c
                if k >= 2:
                    t.wait("ac", ac0 + k - 1)
                t.inc(t.transpose(ps[k % 2][:, 0:128], xt[:, c * 128:(c + 1) * 128],
                                  ident[:]), "pe")

    def acA(a):
        a.wait("dv", ent["dv"])
        a.wait("pe", ent["pe"])
        for i in range(NT):
            for c in range(8):
                k = 8 * i + c
                a.wait("pe", pe0 + k + 1)
                a.inc(a.activation(SA[:, c * 2048 + i * 128:c * 2048 + (i + 1) * 128],
                                   ps[k % 2][:, 0:128], AF.Copy), "ac")
    b.dv(dvA)
    b.pe(peA)
    b.ac(acA)
    xlnf_m = b.mil("ac")

    def prep_ac(a):
        a.wait("wd", wd_m)
        a.wait("dv", ent["dv"])
        a.wait("pe", ent["pe"])
        a.link(a.activation(smalls[0:1, 192:208], smalls[0:1, 176:192], AF.Exp))
        a.inc(a.mul(smalls[0:1, 208:224], smalls[0:1, 192:208], -1.0), "ac")
    b.ac(prep_ac)
    lnA_m = b.mil("ac")

    def prep_gp(g):
        g.wait("ac", lnA_m)
        g.inc(g.iota(SD[:, 4224:4352], pattern=[[1, 128]], base=-1, channel_multiplier=-1,
                     allow_small_or_imprecise_dtypes=True), "gp")
    b.gp(prep_gp)
    gp_m = b.mil("gp")

    pe_pr = b.mil("pe")

    def prep_pe(t):
        t.wait("ac", max(lnA_m, xlnf_m))
        t.inc(t.matmul(ps[2][:, 0:16], onesr[0:1, 0:128], smalls[0:1, 208:224],
                       start=True, stop=True), "pe")
        t.inc(t.matmul(ps[3][:, 0:64], onesr[0:1, 0:128], SC[0:1, 3200:3264],
                       start=True, stop=True), "pe")
    b.pe(prep_pe)

    def prep_dv(v):
        v.wait("pe", pe_pr + 2)
        v.wait("gp", gp_m)
        v.link(v.tensor_copy(smalls[:, 0:16], ps[2][:, 0:16]))
        v.link(v.tensor_copy(smalls[:, 32:96], ps[3][:, 0:64]))
        v.link(v.tensor_scalar(SD[:, 4352:4480], SD[:, 4224:4352], -1.0, None, ALU.max))
        v.link(v.tensor_scalar(SD[:, 4096:4224], SD[:, 4224:4352], 0.0, None, ALU.is_ge))
        v.inc(v.tensor_scalar(smalls[:, 96:97], iop[:], -1.0, 127.0, ALU.mult, ALU.add), "dv")
    b.dv(prep_dv)
    prep_m = b.mil("dv")

    pe_bb = b.mil("pe")
    ac_bb = b.mil("ac")

    def beta_pe(t):
        t.wait("dv", prep_m)
        t.wait("wd", wd_m)
        for tt in range(NT):
            if tt >= 2:
                t.wait("ac", ac_bb + tt - 1)
            bank = ps[4 + tt % 2]
            for c in range(7):
                t.matmul(bank[:, 0:16], SA[:, c * 2048 + tt * 128:c * 2048 + (tt + 1) * 128],
                         SC[:, 2048 + c * 16:2048 + (c + 1) * 16], start=(c == 0), stop=False)
            t.inc(t.matmul(bank[:, 0:16], SA[:, 7 * 2048 + tt * 128:7 * 2048 + (tt + 1) * 128],
                           SC[:, 2048 + 7 * 16:2048 + 8 * 16], start=False, stop=True), "pe")
    b.pe(beta_pe)

    def beta_ac(a):
        for tt in range(NT):
            a.wait("pe", pe_bb + tt + 1)
            a.inc(a.activation(SD[:, 2112 + tt * 16:2112 + (tt + 1) * 16],
                               ps[4 + tt % 2][:, 0:16], AF.Sigmoid), "ac")
    b.ac(beta_ac)
    beta_m = b.mil("ac")

    wq3 = dram["L2_attn_Wq"].rearrange("(c p) j -> p c j", p=128)
    wk3 = dram["L2_attn_Wk"].rearrange("(c p) j -> p c j", p=128)
    wv3 = dram["L2_attn_Wv"].rearrange("(c p) j -> p c j", p=128)
    wf3 = dram["L2_attn_Wf"].rearrange("(c p) j -> p c j", p=128)
    wg3 = dram["L2_attn_Wg"].rearrange("(c p) j -> p c j", p=128)

    for p in range(8):
        for wi, (w3, dstb, actf) in enumerate([
                (wq3, 0, AF.Silu), (wk3, 2048, AF.Silu), (wv3, 4096, AF.Silu),
                (wf3, 6144, AF.Sigmoid)]):
            sco = (wi % 2) * 1024

            def pw_sy(sy, w3=w3, sco=sco, p=p):
                sy.wait("pe", b.mil("pe"))
                sy.inc(sy.dma_start(out=SC[:, sco:sco + 1024],
                                    in_=w3[:, :, p * 128:(p + 1) * 128]), "wd", 16)
            b.sy(pw_sy)
            wdm = b.mil("wd")

            if wi == 3:
                def ndt_ac(a, p=p, dvn=b.mil("dv")):
                    a.wait("dv", dvn)
                    a.inc(a.mul(scrA[0:1, 512:640],
                                SC[0:1, 2176 + p * 128:2176 + (p + 1) * 128], -1.0), "ac")
                b.ac(ndt_ac)
            ndt_m = b.mil("ac")
            pe_qb = b.mil("pe")
            ac_qb = b.mil("ac")

            def pw_pe(t, wdm=wdm, sco=sco, wi=wi, pe_qb=pe_qb, ac_qb=ac_qb,
                      dvn=b.mil("dv"), acn=b.mil("ac"), ndt_m=ndt_m):
                t.wait("wd", wdm)
                if wi == 3:
                    t.wait("ac", ndt_m)
                t.wait("dv", dvn)
                t.wait("ac", max(xlnf_m, beta_m, acn))
                for tt in range(NT):
                    if tt >= 2:
                        t.wait("ac", ac_qb + tt - 1)
                    bank = ps[4 + tt % 2]
                    for c in range(7):
                        t.matmul(bank[:, 0:128],
                                 SA[:, c * 2048 + tt * 128:c * 2048 + (tt + 1) * 128],
                                 SC[:, sco + c * 128:sco + (c + 1) * 128],
                                 start=(c == 0), stop=False)
                    last = t.matmul(bank[:, 0:128],
                                    SA[:, 7 * 2048 + tt * 128:7 * 2048 + (tt + 1) * 128],
                                    SC[:, sco + 7 * 128:sco + 8 * 128],
                                    start=False, stop=(wi != 3))
                    if wi == 3:
                        last = t.matmul(bank[:, 0:128], onesr[0:1, 0:128],
                                        scrA[0:1, 512:640],
                                        start=False, stop=True)
                    t.inc(last, "pe")
            b.pe(pw_pe)

            def pw_ac(a, pe_qb=pe_qb, dstb=dstb, actf=actf):
                for tt in range(NT):
                    a.wait("pe", pe_qb + tt + 1)
                    a.inc(a.activation(SB[:, dstb + tt * 128:dstb + (tt + 1) * 128],
                                       ps[4 + tt % 2][:, 0:128], actf), "ac")
            b.ac(pw_ac)
        proj_ac = b.mil("ac")

        for hh in range(2):
            base = 64 * hh
            head = 2 * p + hh
            acq = b.mil("ac")

            def nrm_ac(a, base=base, acq=acq, proj_ac=proj_ac, dvz=b.mil("dv")):
                a.wait("ac", proj_ac)
                a.wait("dv", dvz)
                for wi, qb in ((0, 0), (1, 2048)):
                    for tt in range(NT):
                        a.link(a.activation(
                            SD[:, 2752:2816],
                            SB[:, qb + tt * 128 + base:qb + tt * 128 + base + 64],
                            AF.Square, accum_out=smalls[:, 112 + wi * 16 + tt:
                                                        113 + wi * 16 + tt]))
                a.inc(a.activation(smalls[:, 144:176], smalls[:, 112:144], AF.Sqrt), "ac")
            b.ac(nrm_ac)
            nrm_acm = b.mil("ac")

            def nrm_dv(v, base=base, head=head, nrm_acm=nrm_acm):
                v.wait("ac", nrm_acm)
                v.link(v.tensor_scalar(smalls[:, 144:176], smalls[:, 144:176],
                                       float(np.float32(1e-12)), None, ALU.max))
                v.link(v.reciprocal(smalls[:, 176:208], smalls[:, 144:176]))
                for tt in range(NT):
                    qs = SB[:, tt * 128 + base:tt * 128 + base + 64]
                    v.link(v.tensor_scalar(qs, qs, smalls[:, 176 + tt:177 + tt],
                                           None, ALU.mult))
                    ks = SB[:, 2048 + tt * 128 + base:2048 + tt * 128 + base + 64]
                    v.link(v.tensor_scalar(ks, ks, smalls[:, 192 + tt:193 + tt],
                                           SD[:, 2112 + tt * 16 + head:2113 + tt * 16 + head],
                                           ALU.mult, ALU.mult))
                v.link(v.tensor_scalar(smalls[:, 97:98], iop[:], smalls[:, head:head + 1],
                                       None, ALU.mult))
                v.link(v.tensor_scalar(smalls[:, 98:99], smalls[:, 96:97],
                                       smalls[:, head:head + 1], None, ALU.mult))
                v.link(v.tensor_scalar(smalls[:, 99:100], smalls[:, head:head + 1],
                                       128.0, None, ALU.mult))
                v.link(v.tensor_scalar(scrA[:, 0:128], SD[:, 4352:4480],
                                       smalls[:, head:head + 1], None, ALU.mult))
                v.inc(v.memset(SD[0:64, 2048:2112], 0.0), "dv")
            b.dv(nrm_dv)
            nrm_dvm = b.mil("dv")

            def apw_ac(a, nrm_dvm=nrm_dvm):
                a.wait("dv", nrm_dvm)
                a.link(a.activation(smalls[:, 100:103], smalls[:, 97:100], AF.Exp))
                a.inc(a.activation(SD[:, 3264:3392], scrA[:, 0:128], AF.Exp), "ac")
            b.ac(apw_ac)
            apw_m = b.mil("ac")

            def dt_dv(v, apw_m=apw_m):
                v.wait("ac", apw_m)
                v.inc(v.tensor_tensor(SD[:, 3264:3392], SD[:, 3264:3392],
                                      SD[:, 4096:4224], ALU.mult), "dv")
            b.dv(dt_dv)
            dt_m = b.mil("dv")

            for cc in range(NT):
                qsl = SB[:, cc * 128 + base:cc * 128 + base + 64]
                ksl = SB[:, 2048 + cc * 128 + base:2048 + cc * 128 + base + 64]
                vsl = SB[:, 4096 + cc * 128 + base:4096 + cc * 128 + base + 64]
                gsl = SB[:, 6144 + cc * 128 + base:6144 + cc * 128 + base + 64]
                dv_c0 = b.mil("dv")

                def sc_pe1(t, qsl=qsl, ksl=ksl, dt_m=dt_m, dv_c0=dv_c0):
                    t.wait("dv", max(dt_m, dv_c0))
                    t.inc(t.transpose(ps[6][0:64, 0:128], qsl, ident[:]), "pe")
                    t.inc(t.transpose(ps[7][0:64, 0:128], ksl, ident[:]), "pe")
                b.pe(sc_pe1)
                pe_c1 = b.mil("pe")

                def sc_dv1(v, pe_c1=pe_c1):
                    v.wait("pe", pe_c1)
                    v.link(v.tensor_copy(SD[0:64, 2368:2496], ps[6][0:64, 0:128]))
                    v.inc(v.tensor_copy(SD[0:64, 2496:2624], ps[7][0:64, 0:128]), "dv")
                b.dv(sc_dv1)
                dv_c1 = b.mil("dv")

                def sc_pe2(t, cc=cc, dv_c1=dv_c1):
                    t.wait("dv", dv_c1)
                    t.inc(t.matmul(ps[cc % 2][:, 0:128], SD[0:64, 2496:2624],
                                   SD[0:64, 2368:2496], start=True, stop=True), "pe")
                b.pe(sc_pe2)
                pe_c2 = b.mil("pe")

                def sc_dv2(v, cc=cc, pe_c2=pe_c2):
                    v.wait("pe", pe_c2)
                    v.inc(v.tensor_tensor(SD[:, 2624:2752], ps[cc % 2][:, 0:128],
                                          SD[:, 3264:3392], ALU.mult), "dv")
                b.dv(sc_dv2)
                dv_c2 = b.mil("dv")

                def sc_pe3(t, vsl=vsl, dv_c2=dv_c2):
                    t.wait("dv", dv_c2)
                    t.inc(t.matmul(ps[2][:, 0:64], SD[:, 2624:2752], vsl,
                                   start=True, stop=True), "pe")
                    t.inc(t.matmul(ps[3][:, 0:64], SD[0:64, 2368:2496],
                                   SD[0:64, 2048:2112], start=True, stop=True), "pe")
                b.pe(sc_pe3)
                pe_c3 = b.mil("pe")

                def sc_dv3(v, cc=cc, qsl=qsl, ksl=ksl, gsl=gsl, pe_c3=pe_c3):
                    v.wait("pe", pe_c3)
                    v.link(v.tensor_scalar(qsl, ps[3][:, 0:64], smalls[:, 100:101],
                                           None, ALU.mult))
                    v.link(v.tensor_tensor(qsl, qsl, ps[2][:, 0:64], ALU.add))
                    v.link(v.tensor_tensor(qsl, qsl, gsl, ALU.mult))
                    v.inc(v.tensor_scalar(SD[:, 2752:2816], ksl, smalls[:, 101:102],
                                          None, ALU.mult), "dv")
                b.dv(sc_dv3)
                dv_c3 = b.mil("dv")

                def sc_pe4(t, vsl=vsl, dv_c3=dv_c3):
                    t.wait("dv", dv_c3)
                    t.inc(t.matmul(ps[4][0:64, 0:64], SD[:, 2752:2816], vsl,
                                   start=True, stop=True), "pe")
                b.pe(sc_pe4)
                pe_c4 = b.mil("pe")

                def sc_dv4(v, pe_c4=pe_c4):
                    v.wait("pe", pe_c4)
                    v.link(v.tensor_scalar(SD[0:64, 2048:2112], SD[0:64, 2048:2112],
                                           smalls[0:64, 102:103], None, ALU.mult))
                    v.inc(v.tensor_tensor(SD[0:64, 2048:2112], SD[0:64, 2048:2112],
                                          ps[4][0:64, 0:64], ALU.add), "dv")
                b.dv(sc_dv4)

            rms_acb = b.mil("ac")

            def rms_ac(a, base=base, dvz=b.mil("dv")):
                a.wait("dv", dvz)
                for tt in range(NT):
                    ins = a.activation(SD[:, 2752:2816],
                                       SB[:, tt * 128 + base:tt * 128 + base + 64],
                                       AF.Square, accum_out=smalls[:, 112 + tt:113 + tt])
                    if tt < NT - 1:
                        a.link(ins)
                    else:
                        a.inc(ins, "ac")
                        a.wait("ac", b.cnt["ac"])
            b.ac(rms_ac)
            rms_acm = b.mil("ac")

            def rms_dv(v, rms_acm=rms_acm):
                v.wait("ac", rms_acm)
                v.inc(v.tensor_scalar(smalls[:, 128:144], smalls[:, 112:128],
                                      1.0 / 64.0, 1e-5, ALU.mult, ALU.add), "dv")
            b.dv(rms_dv)
            rms_dvm = b.mil("dv")

            def rms_ac2(a, rms_dvm=rms_dvm):
                a.wait("dv", rms_dvm)
                a.inc(a.activation(smalls[:, 144:160], smalls[:, 128:144], AF.Sqrt), "ac")
            b.ac(rms_ac2)
            rms_ac2m = b.mil("ac")

            def rms_dv2(v, rms_ac2m=rms_ac2m, base=base):
                v.wait("ac", rms_ac2m)
                v.link(v.reciprocal(smalls[:, 160:176], smalls[:, 144:160]))
                for tt in range(NT):
                    osl = SB[:, tt * 128 + base:tt * 128 + base + 64]
                    v.link(v.tensor_scalar(osl, osl, smalls[:, 160 + tt:161 + tt],
                                           None, ALU.mult))
                    v.inc(v.tensor_tensor(osl, osl, smalls[:, 32:96], ALU.mult), "dv")
            b.dv(rms_dv2)

        onorm_dv = b.mil("dv")

        def gw_sy(sy, p=p):
            sy.wait("pe", b.mil("pe"))
            sy.inc(sy.dma_start(out=SC[:, 0:1024],
                                in_=wg3[:, :, p * 128:(p + 1) * 128]), "wd", 16)
        b.sy(gw_sy)
        gw_m = b.mil("wd")
        pe_gb = b.mil("pe")
        ac_gb = b.mil("ac")

        def g_pe(t, gw_m=gw_m, p=p, onorm_dv=onorm_dv, pe_gb=pe_gb, ac_gb=ac_gb,
                 acn=b.mil("ac")):
            t.wait("wd", gw_m)
            t.wait("dv", onorm_dv)
            t.wait("ac", acn)
            for tt in range(NT):
                if tt >= 2:
                    t.wait("ac", ac_gb + tt - 1)
                bank = ps[4 + tt % 2]
                for c in range(8):
                    t.matmul(bank[:, 0:128],
                             SA[:, c * 2048 + tt * 128:c * 2048 + (tt + 1) * 128],
                             SC[:, c * 128:(c + 1) * 128],
                             start=(c == 0), stop=False)
                t.inc(t.matmul(bank[:, 0:128], onesr[0:1, 0:128],
                               SC[0:1, 4096 + p * 128:4096 + (p + 1) * 128],
                               start=False, stop=True), "pe")
        b.pe(g_pe)

        def g_ac(a, pe_gb=pe_gb):
            for tt in range(NT):
                a.wait("pe", pe_gb + tt + 1)
                a.inc(a.activation(SB[:, 2048 + tt * 128:2048 + (tt + 1) * 128],
                                   ps[4 + tt % 2][:, 0:128], AF.Sigmoid), "ac")
        b.ac(g_ac)
        gate_ac = b.mil("ac")

        def z_dv(v, gate_ac=gate_ac):
            v.wait("ac", gate_ac)
            for tt in range(NT):
                zsl = SB[:, tt * 128:(tt + 1) * 128]
                v.inc(v.tensor_tensor(zsl, zsl, SB[:, 2048 + tt * 128:2048 + (tt + 1) * 128],
                                      ALU.mult), "dv")
        b.dv(z_dv)
        z_m = b.mil("dv")

        def zw_sy(sy, p=p):
            sy.wait("pe", b.mil("pe"))
            sy.inc(sy.dma_start(out=SC[:, 0:1024],
                                in_=dram["L2_attn_Wo"][p * 128:(p + 1) * 128, :]), "wd", 16)
        b.sy(zw_sy)
        zw_m = b.mil("wd")
        pe_zb = b.mil("pe")
        ac_zb = b.mil("ac")

        def zt_pe(t, z_m=z_m, pe_zb=pe_zb, ac_zb=ac_zb):
            t.wait("dv", z_m)
            for tt in range(NT):
                if tt >= 2:
                    t.wait("ac", ac_zb + tt - 1)
                t.inc(t.transpose(ps[4 + tt % 2][:, 0:128],
                                  SB[:, tt * 128:(tt + 1) * 128], ident[:]), "pe")
        b.pe(zt_pe)

        def zt_ac(a, pe_zb=pe_zb):
            for tt in range(NT):
                a.wait("pe", pe_zb + tt + 1)
                a.inc(a.activation(SB[:, 4096 + tt * 128:4096 + (tt + 1) * 128],
                                   ps[4 + tt % 2][:, 0:128], AF.Copy), "ac")
        b.ac(zt_ac)
        zf_ac = b.mil("ac")

        pe_wb = b.mil("pe")
        dv_rb = b.mil("dv")

        def wo_pe(t, zf_ac=zf_ac, zw_m=zw_m, pe_wb=pe_wb, dv_rb=dv_rb):
            t.wait("ac", zf_ac)
            t.wait("wd", zw_m)
            t.wait("dv", b.mil("dv"))
            k = 0
            for tt in range(NT):
                for oh in range(2):
                    if k >= 2:
                        t.wait("dv", dv_rb + k - 1)
                    t.inc(t.matmul(ps[k % 2][:, 0:512],
                                   SB[:, 4096 + tt * 128:4096 + (tt + 1) * 128],
                                   SC[:, oh * 512:(oh + 1) * 512],
                                   start=True, stop=True), "pe")
                    k += 1
        b.pe(wo_pe)
        pe_wm = b.mil("pe")

        def wo_dv(v, pe_wb=pe_wb):
            k = 0
            for tt in range(NT):
                for oh in range(2):
                    v.wait("pe", pe_wb + k + 1)
                    dst = xres[:, tt * D + oh * 512:tt * D + (oh + 1) * 512]
                    v.inc(v.tensor_tensor(dst, ps[k % 2][:, 0:512], dst, ALU.add), "dv")
                    k += 1
        b.dv(wo_dv)

    b.x_dv = b.mil("dv")


def _emit_qdq(h, src, dst, W, scol):
    b = h.b
    SD, smalls = h.SD, h.smalls
    nblk = W // 16
    sc = smalls[:, scol:scol + nblk]
    rsc = smalls[:, scol + 64:scol + 64 + nblk]
    xn = SD[:, 0:W]
    vab = SD[:, 1024:1024 + W]
    B = SD[:, 2048:2048 + W]
    A = xn

    def q_dv1(v, dvz=b.mil("dv"), acz=b.mil("ac"), pez=b.mil("pe"),
              stz=b.mil("st")):
        v.wait("dv", dvz)
        v.wait("ac", acz)
        v.wait("pe", pez)
        v.wait("st", stz)
        src3 = AP(src.tensor, src.offset, [src.ap[0], [16, nblk], [1, 16]])
        v.link(v.tensor_reduce(sc, src3, axis=AX.X, op=ALU.max,
                               apply_absolute_value=True))
        v.link(v.tensor_scalar(sc, sc, h.c6[:], float(np.float32(1e-12)),
                               ALU.mult, ALU.max))
        v.link(v.reciprocal(rsc, sc))
        rscb = AP(rsc.tensor, rsc.offset, [rsc.ap[0], [1, nblk], [0, 16]])
        xn3 = AP(xn.tensor, xn.offset, [xn.ap[0], [16, nblk], [1, 16]])
        v.inc(v.tensor_tensor(xn3, src3, rscb, ALU.mult), "dv")
    b.dv(q_dv1)
    m1 = b.mil("dv")

    def q_ac1(a, m1=m1):
        a.wait("dv", m1)
        a.link(a.activation(vab, xn, AF.Abs))
        a.inc(a.activation(dst, xn, AF.Sign), "ac")
    b.ac(q_ac1)
    m2 = b.mil("ac")

    def q_dv2(v, m2=m2):
        v.wait("ac", m2)
        v.link(v.tensor_scalar(A, vab, 5.0, None, ALU.is_gt))
        v.link(v.tensor_scalar(A, A, 2.0, None, ALU.mult))
        v.link(v.scalar_tensor_tensor(A, vab, 3.5, A, ALU.is_gt, ALU.add))
        v.link(v.scalar_tensor_tensor(A, vab, 2.5, A, ALU.is_gt, ALU.add))
        v.link(v.tensor_scalar(B, vab, 1.75, None, ALU.is_gt))
        v.link(v.scalar_tensor_tensor(B, vab, 1.25, B, ALU.is_gt, ALU.add))
        v.link(v.scalar_tensor_tensor(B, vab, 0.75, B, ALU.is_gt, ALU.add))
        v.link(v.scalar_tensor_tensor(B, vab, 0.25, B, ALU.is_gt, ALU.add))
        v.link(v.scalar_tensor_tensor(B, B, 0.5, A, ALU.mult, ALU.add))
        v.link(v.tensor_tensor(B, B, dst, ALU.mult))
        scb = AP(sc.tensor, sc.offset, [sc.ap[0], [1, nblk], [0, 16]])
        B3 = AP(B.tensor, B.offset, [B.ap[0], [16, nblk], [1, 16]])
        dst3 = AP(dst.tensor, dst.offset, [dst.ap[0], [16, nblk], [1, 16]])
        v.inc(v.tensor_tensor(dst3, B3, scb, ALU.mult), "dv")
    b.dv(q_dv2)
    return b.mil("dv")


def emit_l3ffn(h, sub):
    b, dram, nc = h.b, h.dram, h.nc
    SA, SB, SC, SD, ps = h.SA, h.SB, h.SC, h.SD, h.ps
    xres, ident, onesr, smalls, scrA = h.xres, h.ident, h.onesr, h.smalls, h.scrA
    ent = {s: b.mil(s) for s in Bld.SEMS}

    m, rs, ln_m = h.ln_stats()

    if not hasattr(h, "l3_scr"):
        h.l3_scr = {
            "wgqT": nc.dram_tensor("wgqT_scr", [1024, 2816], F32).ap(),
            "wuqT": nc.dram_tensor("wuqT_scr", [1024, 2816], F32).ap(),
            "wdqT": nc.dram_tensor("wdqT_scr", [2816, 1024], F32).ap(),
        }
        for wname, scrname, ntile, wid in (
                ("L3_ffn_w_gate", "wgqT", 22, 1024),
                ("L3_ffn_w_up", "wuqT", 22, 1024),
                ("L3_ffn_w_down", "wdqT", 8, 2816)):
            scr = h.l3_scr[scrname]
            for it in range(ntile):
                def wq_sy(sy, wname=wname, it=it, wid=wid, dvz=b.mil("dv"),
                          pez=b.mil("pe"), stz=b.mil("st")):
                    sy.wait("dv", dvz)
                    sy.wait("pe", pez)
                    sy.wait("st", stz)
                    for s in ("ac",):
                        sy.wait(s, ent[s])
                    sy.inc(sy.dma_start(out=SB[:, 0:wid],
                                        in_=dram[wname][it * 128:(it + 1) * 128, :]),
                           "wd", 16)
                b.sy(wq_sy)
                wdm = b.mil("wd")

                def wq_gate(v, wdm=wdm):
                    v.wait("wd", wdm)
                    v.inc(v.tensor_copy(smalls[0:1, 255:256], smalls[0:1, 255:256])
                          if False else
                          v.tensor_scalar(scrA[0:1, 0:1], SB[0:1, 0:1], 1.0, None,
                                          ALU.mult), "dv")
                b.dv(wq_gate)
                off = 0
                while off < wid:
                    wseg = min(1024, wid - off)
                    _emit_qdq(h, SB[:, off:off + wseg],
                              SB[:, 4096 + off:4096 + off + wseg], wseg, 112)
                    off += wseg
                nch = wid // 128
                pe_tb = b.mil("pe")
                dv_eb = b.mil("dv")

                def wq_pe(t, qm=b.mil("dv"), pe_tb=pe_tb, dv_eb=dv_eb, nch=nch):
                    t.wait("dv", qm)
                    for c in range(nch):
                        if c >= 2:
                            t.wait("dv", dv_eb + c - 1)
                        t.inc(t.transpose(ps[6 + c % 2][:, 0:128],
                                          SB[:, 4096 + c * 128:4096 + (c + 1) * 128],
                                          ident[:]), "pe")
                b.pe(wq_pe)

                def wq_dv(v, pe_tb=pe_tb, nch=nch):
                    for c in range(nch):
                        v.wait("pe", pe_tb + c + 1)
                        v.inc(v.tensor_copy(SB[:, c * 128:(c + 1) * 128],
                                            ps[6 + c % 2][:, 0:128]), "dv")
                b.dv(wq_dv)
                tq_m = b.mil("dv")

                def wq_out(sy, scr=scr, it=it, nch=nch, tq_m=tq_m):
                    sy.wait("dv", tq_m)
                    for c in range(nch):
                        sy.inc(sy.dma_start(
                            out=scr[c * 128:(c + 1) * 128, it * 128:(it + 1) * 128],
                            in_=SB[:, c * 128:(c + 1) * 128]), "st", 16)
                b.sy(wq_out)
        wq_done_st = b.mil("st")

        def wq_wait(sy, wq_done_st=wq_done_st):
            sy.wait("st", wq_done_st)
        b.sy(wq_wait)
    scr = h.l3_scr

    OBW = [512, 512, 512, 512, 512, 256]
    OBO = [0, 512, 1024, 1536, 2048, 2560]
    st_guard = b.mil("st")

    for sti in range(8):
        t0, t1 = 2 * sti, 2 * sti + 1
        dvz = b.mil("dv")

        def ln_dv(v, t0=t0, t1=t1, dvz=dvz, ln_m=ln_m, pez=b.mil("pe"),
                  acz=b.mil("ac")):
            v.wait("dv", max(ln_m, dvz))
            v.wait("pe", pez)
            v.wait("ac", acz)
            for j, tt in enumerate((t0, t1)):
                v.inc(v.tensor_scalar(SC[:, j * 1024:(j + 1) * 1024],
                                      xres[:, tt * D:(tt + 1) * D],
                                      m[:, tt:tt + 1], rs[:, tt:tt + 1],
                                      ALU.subtract, ALU.mult), "dv")
        b.dv(ln_dv)
        for j in range(2):
            _emit_qdq(h, SC[:, j * 1024:(j + 1) * 1024], SB[:, j * 1024:(j + 1) * 1024],
                      1024, 112)
        xq_m = b.mil("dv")

        pe_xb = b.mil("pe")
        dv_xb = b.mil("dv")

        def xq_pe(t, xq_m=xq_m, pe_xb=pe_xb, dv_xb=dv_xb):
            t.wait("dv", xq_m)
            for k in range(16):
                if k >= 2:
                    t.wait("dv", dv_xb + k - 1)
                t.inc(t.transpose(ps[6 + k % 2][:, 0:128],
                                  SB[:, k * 128:(k + 1) * 128], ident[:]), "pe")
        b.pe(xq_pe)

        def xq_dv(v, pe_xb=pe_xb):
            for k in range(16):
                v.wait("pe", pe_xb + k + 1)
                v.inc(v.tensor_copy(SB[:, 2048 + k * 128:2048 + (k + 1) * 128],
                                    ps[6 + k % 2][:, 0:128]), "dv")
        b.dv(xq_dv)
        xqf_m = b.mil("dv")

        down_open = [False]

        for obi in range(6):
            obw, obo = OBW[obi], OBO[obi]
            def wb_sy(sy, obo=obo, obw=obw, dvz=b.mil("dv"), pez=b.mil("pe")):
                sy.wait("dv", dvz)
                sy.wait("pe", pez)
                for c in range(8):
                    sy.inc(sy.dma_start(out=SA[:, c * 512:c * 512 + obw],
                                        in_=scr["wgqT"][c * 128:(c + 1) * 128,
                                                        obo:obo + obw]), "wd", 16)
                    sy.inc(sy.dma_start(out=SA[:, 4096 + c * 512:4096 + c * 512 + obw],
                                        in_=scr["wuqT"][c * 128:(c + 1) * 128,
                                                        obo:obo + obw]), "wd", 16)
                for c in range(obw // 128):
                    sy.inc(sy.dma_start(out=SA[:, 8192 + c * 1024:8192 + (c + 1) * 1024],
                                        in_=scr["wdqT"][obo + c * 128:obo + (c + 1) * 128, :]),
                           "wd", 16)
            b.sy(wb_sy)
            wbm = b.mil("wd")

            for j in range(2):
                pe_gb = b.mil("pe")
                ac_gb = b.mil("ac")

                def gu_pe(t, j=j, obw=obw, wbm=wbm, xqf_m=xqf_m, pe_gb=pe_gb,
                          acz=b.mil("ac"), dvz=b.mil("dv")):
                    t.wait("wd", wbm)
                    t.wait("dv", max(xqf_m, dvz))
                    t.wait("ac", acz)
                    for gi, sab in ((0, 0), (1, 4096)):
                        bank = ps[gi]
                        for c in range(7):
                            t.matmul(bank[:, 0:obw],
                                     SB[:, 2048 + (j * 8 + c) * 128:
                                        2048 + (j * 8 + c + 1) * 128],
                                     SA[:, sab + c * 512:sab + c * 512 + obw],
                                     start=(c == 0), stop=False)
                        t.inc(t.matmul(bank[:, 0:obw],
                                       SB[:, 2048 + (j * 8 + 7) * 128:
                                          2048 + (j * 8 + 8) * 128],
                                       SA[:, sab + 7 * 512:sab + 7 * 512 + obw],
                                       start=False, stop=True), "pe")
                b.pe(gu_pe)
                pe_gm = b.mil("pe")

                def gu_ac(a, pe_gm=pe_gm, obw=obw):
                    a.wait("pe", pe_gm - 1)
                    a.link(a.activation(SB[:, 4096:4096 + obw], ps[0][:, 0:obw], AF.Silu))
                    a.wait("pe", pe_gm)
                    a.inc(a.activation(SB[:, 5120:5120 + obw], ps[1][:, 0:obw], AF.Copy),
                          "ac")
                b.ac(gu_ac)
                gu_acm = b.mil("ac")

                def h_dv(v, gu_acm=gu_acm, obw=obw):
                    v.wait("ac", gu_acm)
                    v.inc(v.tensor_tensor(SB[:, 4096:4096 + obw], SB[:, 4096:4096 + obw],
                                          SB[:, 5120:5120 + obw], ALU.mult), "dv")
                b.dv(h_dv)
                hq_m = _emit_qdq(h, SB[:, 4096:4096 + obw], SB[:, 6144:6144 + obw],
                                 obw, 112)

                nch = obw // 128
                pe_hb = b.mil("pe")
                dv_hb = b.mil("dv")

                def hq_pe(t, hq_m=hq_m, pe_hb=pe_hb, dv_hb=dv_hb, nch=nch):
                    t.wait("dv", hq_m)
                    for c in range(nch):
                        if c >= 2:
                            t.wait("dv", dv_hb + c - 1)
                        t.inc(t.transpose(ps[6 + c % 2][:, 0:128],
                                          SB[:, 6144 + c * 128:6144 + (c + 1) * 128],
                                          ident[:]), "pe")
                b.pe(hq_pe)

                def hq_dv(v, pe_hb=pe_hb, nch=nch):
                    for c in range(nch):
                        v.wait("pe", pe_hb + c + 1)
                        v.inc(v.tensor_copy(SB[:, 7168 + c * 128:7168 + (c + 1) * 128],
                                            ps[6 + c % 2][:, 0:128]), "dv")
                b.dv(hq_dv)
                hqf_m = b.mil("dv")

                def dn_pe(t, j=j, obi=obi, nch=nch, hqf_m=hqf_m):
                    t.wait("dv", hqf_m)
                    for oh in range(2):
                        bank = ps[2 + 2 * j + oh]
                        for c in range(nch):
                            ins = t.matmul(bank[:, 0:512],
                                           SB[:, 7168 + c * 128:7168 + (c + 1) * 128],
                                           SA[:, 8192 + c * 1024 + oh * 512:
                                              8192 + c * 1024 + (oh + 1) * 512],
                                           start=(obi == 0 and c == 0),
                                           stop=(obi == 5 and c == nch - 1))
                            if oh == 1 and c == nch - 1:
                                t.inc(ins, "pe")

                b.pe(dn_pe)
        pe_dn = b.mil("pe")

        def res_dv(v, t0=t0, t1=t1, pe_dn=pe_dn):
            v.wait("pe", pe_dn)
            for j, tt in enumerate((t0, t1)):
                for oh in range(2):
                    dst = xres[:, tt * D + oh * 512:tt * D + (oh + 1) * 512]
                    v.inc(v.scalar_tensor_tensor(dst, ps[2 + 2 * j + oh][:, 0:512],
                                                 0.5, dst, ALU.mult, ALU.add), "dv")
        b.dv(res_dv)

    b.x_dv = b.mil("dv")


def emit_l3attn(h, sub):
    b, dram = h.b, h.dram
    SA, SB, SC, SD, ps = h.SA, h.SB, h.SC, h.SD, h.ps
    xres, ident, onesr, onesc, smalls, scrA, stats = (
        h.xres, h.ident, h.onesr, h.onesc, h.smalls, h.scrA, h.stats)
    ent = {s: b.mil(s) for s in Bld.SEMS}

    m, rs, ln_m = h.ln_stats()

    def loads(sy):
        for s in ("pe", "dv", "ac"):
            sy.wait(s, ent[s])
        sy.wait("st", ent["st"])
        sy.inc(sy.dma_start(out=SD[0:1, 2144:2160],
                            in_=dram["L3_attn_temp"].rearrange("h o -> o h")), "wd", 16)
        sy.inc(sy.dma_start(out=SC[0:1, 3072:4096], in_=dram["L3_attn_bout"][:, :]), "wd", 16)
    b.sy(loads)
    t_m = b.mil("wd")

    dv0, pe0, ac0 = b.mil("dv"), b.mil("pe"), b.mil("ac")

    def dvA(v):
        v.wait("dv", ln_m)
        for i in range(NT):
            xt = SD[:, (i % 2) * 1024:(i % 2 + 1) * 1024]
            if i >= 2:
                v.wait("pe", pe0 + 8 * (i - 1))
            v.inc(v.tensor_scalar(xt, xres[:, i * D:(i + 1) * D],
                                  m[:, i:i + 1], rs[:, i:i + 1],
                                  ALU.subtract, ALU.mult), "dv")

    def peA(t):
        t.wait("ac", ent["ac"])
        for i in range(NT):
            xt = SD[:, (i % 2) * 1024:(i % 2 + 1) * 1024]
            t.wait("dv", dv0 + i + 1)
            for c in range(8):
                k = 8 * i + c
                if k >= 2:
                    t.wait("ac", ac0 + k - 1)
                t.inc(t.transpose(ps[k % 2][:, 0:128], xt[:, c * 128:(c + 1) * 128],
                                  ident[:]), "pe")

    def acA(a):
        a.wait("dv", ent["dv"])
        a.wait("pe", ent["pe"])
        for i in range(NT):
            for c in range(8):
                k = 8 * i + c
                a.wait("pe", pe0 + k + 1)
                a.inc(a.activation(SA[:, c * 2048 + i * 128:c * 2048 + (i + 1) * 128],
                                   ps[k % 2][:, 0:128], AF.Copy), "ac")
    b.dv(dvA)
    b.pe(peA)
    b.ac(acA)
    xlnf_m = b.mil("ac")

    pe_tb = b.mil("pe")

    def tb_pe(t, t_m=t_m, xlnf_m=xlnf_m):
        t.wait("wd", t_m)
        t.wait("ac", xlnf_m)
        t.inc(t.matmul(ps[2][:, 0:16], onesr[0:1, 0:128], SD[0:1, 2144:2160],
                       start=True, stop=True), "pe")
    b.pe(tb_pe)

    def tb_dv(v, pe_tb=pe_tb):
        v.wait("pe", pe_tb + 1)
        v.inc(v.tensor_copy(SD[:, 2128:2144], ps[2][:, 0:16]), "dv")
    b.dv(tb_dv)
    tb_m = b.mil("dv")

    def _stage_w(nm, p, col_slice, dst):
        def s_sy(sy, nm=nm, p=p, col_slice=col_slice, dvz=b.mil("dv"),
                 pez=b.mil("pe")):
            sy.wait("dv", dvz)
            sy.wait("pe", pez)
            if col_slice:
                w3 = dram[nm].rearrange("(c p) j -> p c j", p=128)
                sy.inc(sy.dma_start(out=SC[:, 0:1024],
                                    in_=w3[:, :, p * 128:(p + 1) * 128]), "wd", 16)
            else:
                sy.inc(sy.dma_start(out=SC[:, 0:1024],
                                    in_=dram[nm][p * 128:(p + 1) * 128, :]), "wd", 16)
        b.sy(s_sy)
        wdm = b.mil("wd")
        pe_b = b.mil("pe")
        dv_b = b.mil("dv")

        def s_pe(t, wdm=wdm, dv_b=dv_b):
            t.wait("wd", wdm)
            t.wait("dv", dv_b)
            for c in range(8):
                if c >= 2:
                    t.wait("dv", dv_b + c - 1)
                t.inc(t.transpose(ps[6 + c % 2][:, 0:128],
                                  SC[:, c * 128:(c + 1) * 128], ident[:]), "pe")
        b.pe(s_pe)

        def s_dv(v, pe_b=pe_b, dst=dst):
            for c in range(8):
                v.wait("pe", pe_b + c + 1)
                v.inc(v.tensor_copy(dst[:, c * 128:(c + 1) * 128],
                                    ps[6 + c % 2][:, 0:128]), "dv")
        b.dv(s_dv)
        return b.mil("dv")

    def _w_pair(p, wT_m):
        pe_b = b.mil("pe")
        ac_b = b.mil("ac")

        def w_pe(t, wT_m=wT_m, pe_b=pe_b, ac_b=ac_b, acz=b.mil("ac"),
                 xlnf_m=xlnf_m):
            t.wait("dv", wT_m)
            t.wait("ac", max(xlnf_m, acz))
            for tt in range(NT):
                if tt >= 2:
                    t.wait("ac", ac_b + 2 * (tt - 1))
                bank = ps[2 + tt % 2]
                for c in range(7):
                    t.matmul(bank[:, 0:128],
                             SA[:, c * 2048 + tt * 128:c * 2048 + (tt + 1) * 128],
                             SC[:, 1024 + c * 128:1024 + (c + 1) * 128],
                             start=(c == 0), stop=False)
                t.inc(t.matmul(bank[:, 0:128],
                               SA[:, 7 * 2048 + tt * 128:7 * 2048 + (tt + 1) * 128],
                               SC[:, 1024 + 7 * 128:1024 + 8 * 128],
                               start=False, stop=True), "pe")
        b.pe(w_pe)

        def w_ac(a, pe_b=pe_b):
            for tt in range(NT):
                a.wait("pe", pe_b + tt + 1)
                a.link(a.activation(SB[:, tt * 128:(tt + 1) * 128],
                                    ps[2 + tt % 2][:, 0:128], AF.Copy))
                a.inc(a.activation(SB[:, 2048 + tt * 128:2048 + (tt + 1) * 128],
                                   ps[2 + tt % 2][:, 0:128], AF.Square), "ac")
        b.ac(w_ac)
        return b.mil("ac")

    for p in range(8):
        wT_m = _stage_w("L3_attn_Wqkv", p, False, SC[:, 1024:2048])
        w_m = _w_pair(p, wT_m)

        pe_s = b.mil("pe")

        def n_pe(t, w_m=w_m, pe_s=pe_s, dvz=b.mil("dv")):
            t.wait("ac", w_m)
            t.wait("dv", dvz)
            ins = None
            for tt in range(NT):
                ins = t.matmul(ps[4][0:1, 0:128], onesc[:],
                               SB[:, 2048 + tt * 128:2048 + (tt + 1) * 128],
                               start=(tt == 0), stop=(tt == NT - 1))
            t.inc(ins, "pe")
        b.pe(n_pe)
        pe_sm = b.mil("pe")

        def n_ac(a, pe_sm=pe_sm, p=p):
            a.wait("pe", pe_sm)
            a.inc(a.activation(SD[0:1, 2304:2432], ps[4][0:1, 0:128], AF.Sqrt), "ac")
        b.ac(n_ac)
        n_acm = b.mil("ac")

        def n_dv(v, n_acm=n_acm, p=p):
            v.wait("ac", n_acm)
            v.link(v.tensor_scalar(SD[0:1, 2304:2432], SD[0:1, 2304:2432],
                                   float(np.float32(1e-12)), None, ALU.max))
            v.link(v.reciprocal(SD[0:1, 1024 + p * 128:1024 + (p + 1) * 128],
                                SD[0:1, 2304:2432]))
            v.inc(v.tensor_tensor(SD[0:1, 2304:2432],
                                  SD[0:1, 1024 + p * 128:1024 + (p + 1) * 128],
                                  SD[0:1, 1024 + p * 128:1024 + (p + 1) * 128],
                                  ALU.mult), "dv")
        b.dv(n_dv)
        n_dvm = b.mil("dv")

        pe_b2 = b.mil("pe")

        def i_pe(t, n_dvm=n_dvm):
            t.wait("dv", n_dvm)
            t.inc(t.matmul(ps[4][:, 0:128], onesr[0:1, 0:128], SD[0:1, 2304:2432],
                           start=True, stop=True), "pe")
        b.pe(i_pe)
        pe_b2m = b.mil("pe")

        def lg_dv(v, pe_b2m=pe_b2m, p=p, tb_m=tb_m):
            v.wait("pe", pe_b2m)
            v.wait("dv", tb_m)
            for tt in range(NT):
                v.link(v.tensor_tensor(scrA[:, 256:384], SB[:, 2048 + tt * 128:
                                                            2048 + (tt + 1) * 128],
                                       ps[4][:, 0:128], ALU.mult))
                src3 = AP(scrA[:, 256:384].tensor, scrA[:, 256:384].offset,
                          [[1024, 128], [64, 2], [1, 64]])
                ins = v.tensor_reduce(smalls[:, tt * 16 + 2 * p:tt * 16 + 2 * p + 2],
                                      src3, axis=AX.X, op=ALU.add)
                if tt == NT - 1:
                    v.inc(ins, "dv")
                else:
                    v.link(ins)
        b.dv(lg_dv)

    sm_dvb = b.mil("dv")

    def sm_dv(v, sm_dvb=sm_dvb):
        for tt in range(NT):
            sl = smalls[:, tt * 16:(tt + 1) * 16]
            v.link(v.tensor_tensor(sl, sl, SD[:, 2128:2144], ALU.mult))
            v.link(v.reduce_max(stats[:, tt:tt + 1], sl, axis=AX.X))
        v.inc(v.tensor_scalar(stats[:, 16:32], stats[:, 0:16], -1.0, None, ALU.mult),
              "dv")
    b.dv(sm_dv)
    sm_m1 = b.mil("dv")

    def sm_ac(a, sm_m1=sm_m1):
        a.wait("dv", sm_m1)
        ins = None
        for tt in range(NT):
            ins = a.activation(scrA[:, tt * 16:(tt + 1) * 16],
                               smalls[:, tt * 16:(tt + 1) * 16], AF.Exp,
                               bias=stats[:, 16 + tt:17 + tt])
        a.inc(ins, "ac")
    b.ac(sm_ac)
    sm_m2 = b.mil("ac")

    def sm_dv2(v, sm_m2=sm_m2):
        v.wait("ac", sm_m2)
        for tt in range(NT):
            v.link(v.reduce_sum(stats[:, 32 + tt:33 + tt],
                                scrA[:, tt * 16:(tt + 1) * 16], axis=AX.X))
        v.link(v.reciprocal(stats[:, 48:64], stats[:, 32:48]))
        ins = None
        for tt in range(NT):
            ins = v.tensor_scalar(scrA[:, tt * 16:(tt + 1) * 16],
                                  scrA[:, tt * 16:(tt + 1) * 16],
                                  stats[:, 48 + tt:49 + tt], None, ALU.mult)
            if tt == NT - 1:
                v.inc(ins, "dv")
            else:
                v.link(ins)
    b.dv(sm_dv2)
    pi_m = b.mil("dv")

    pe_pc = b.mil("pe")

    def pc_pe(t, pi_m=pi_m):
        t.wait("dv", pi_m)
        ins = None
        for tt in range(NT):
            ins = t.matmul(ps[5][0:1, 0:16], onesc[:], scrA[:, tt * 16:(tt + 1) * 16],
                           start=(tt == 0), stop=(tt == NT - 1))
        t.inc(ins, "pe")
    b.pe(pc_pe)
    pc_m = b.mil("pe")

    def pc_ac(a, pc_m=pc_m):
        a.wait("pe", pc_m)
        a.inc(a.activation(SD[0:1, 2048:2064], ps[5][0:1, 0:16], AF.Copy,
                           bias=1e-8), "ac")
    b.ac(pc_ac)
    pcs_m0 = b.mil("ac")

    def pc_dv(v, pcs_m0=pcs_m0):
        v.wait("ac", pcs_m0)
        v.inc(v.reciprocal(SD[0:1, 2160:2176], SD[0:1, 2048:2064]), "dv")
    b.dv(pc_dv)
    pcs_m = b.mil("dv")

    for p in range(8):
        wT_m = _stage_w("L3_attn_Wqkv", p, False, SC[:, 1024:2048])
        w_m = _w_pair(p, wT_m)

        for hh in range(2):
            head = 2 * p + hh
            base = 64 * hh

            def d_pe(t, w_m=w_m, base=base, head=head, dvz=b.mil("dv"),
                     acz=b.mil("ac")):
                t.wait("ac", max(w_m, acz))
                t.wait("dv", max(dvz, pcs_m))
                ins = None
                for tt in range(NT):
                    ins = t.matmul(ps[4][0:1, 0:64],
                                   scrA[:, tt * 16 + head:tt * 16 + head + 1],
                                   SB[:, 2048 + tt * 128 + base:
                                      2048 + tt * 128 + base + 64],
                                   start=(tt == 0), stop=(tt == NT - 1))
                t.inc(ins, "pe")
            b.pe(d_pe)
            d_m = b.mil("pe")

            def at_dv(v, d_m=d_m, head=head):
                v.wait("pe", d_m)
                v.link(v.tensor_scalar(SD[0:1, 2064:2128], ps[4][0:1, 0:64],
                                       SD[0:1, 2160 + head:2161 + head], None,
                                       ALU.mult))
                v.link(v.tensor_scalar(SD[0:1, 2064:2128], SD[0:1, 2064:2128],
                                       1.0, None, ALU.add))
                v.link(v.reciprocal(SD[0:1, 2240:2304], SD[0:1, 2064:2128]))
                v.inc(v.tensor_scalar(SD[0:1, 2240:2304], SD[0:1, 2240:2304],
                                      -1.0, None, ALU.mult), "dv")
            b.dv(at_dv)
            nat_m = b.mil("dv")

            def nb_pe(t, nat_m=nat_m):
                t.wait("dv", nat_m)
                t.inc(t.matmul(ps[5][:, 0:64], onesr[0:1, 0:128],
                               SD[0:1, 2240:2304], start=True, stop=True), "pe")
            b.pe(nb_pe)
            nb_m = b.mil("pe")

            def z_dv(v, nb_m=nb_m, base=base, head=head):
                v.wait("pe", nb_m)
                v.link(v.tensor_copy(SD[:, 2176:2240], ps[5][:, 0:64]))
                for tt in range(NT):
                    zsl = SB[:, 4096 + tt * 128 + base:4096 + tt * 128 + base + 64]
                    v.link(v.tensor_scalar(zsl,
                                           SB[:, tt * 128 + base:tt * 128 + base + 64],
                                           scrA[:, tt * 16 + head:tt * 16 + head + 1],
                                           None, ALU.mult))
                    ins = v.tensor_tensor(zsl, zsl, SD[:, 2176:2240], ALU.mult)
                    if tt == NT - 1:
                        v.inc(ins, "dv")
                    else:
                        v.link(ins)
            b.dv(z_dv)
        z_m = b.mil("dv")

        pe_zb = b.mil("pe")
        ac_zb = b.mil("ac")

        def zt_pe(t, z_m=z_m, pe_zb=pe_zb, ac_zb=ac_zb):
            t.wait("dv", z_m)
            for tt in range(NT):
                if tt >= 2:
                    t.wait("ac", ac_zb + tt - 1)
                t.inc(t.transpose(ps[2 + tt % 2][:, 0:128],
                                  SB[:, 4096 + tt * 128:4096 + (tt + 1) * 128],
                                  ident[:]), "pe")
        b.pe(zt_pe)

        def zt_ac(a, pe_zb=pe_zb):
            for tt in range(NT):
                a.wait("pe", pe_zb + tt + 1)
                a.inc(a.activation(SB[:, 6144 + tt * 128:6144 + (tt + 1) * 128],
                                   ps[2 + tt % 2][:, 0:128], AF.Copy), "ac")
        b.ac(zt_ac)
        zf_m = b.mil("ac")

        woT_m = _stage_w("L3_attn_Wout", p, True, SC[:, 4096:5120])
        pe_wb = b.mil("pe")
        dv_rb = b.mil("dv")

        def wo_pe(t, zf_m=zf_m, woT_m=woT_m, pe_wb=pe_wb, dv_rb=dv_rb, p=p,
                  t_m=t_m):
            t.wait("ac", zf_m)
            t.wait("dv", woT_m)
            t.wait("wd", t_m)
            k = 0
            for tt in range(NT):
                for oh in range(2):
                    if k >= 2:
                        t.wait("dv", dv_rb + k - 1)
                    bank = ps[0 + k % 2]
                    if p == 0:
                        t.matmul(bank[:, 0:512],
                                 SB[:, 6144 + tt * 128:6144 + (tt + 1) * 128],
                                 SC[:, 4096 + oh * 512:4096 + (oh + 1) * 512],
                                 start=True, stop=False)
                        t.inc(t.matmul(bank[:, 0:512], onesr[0:1, 0:128],
                                       SC[0:1, 3072 + oh * 512:3072 + (oh + 1) * 512],
                                       start=False, stop=True), "pe")
                    else:
                        t.inc(t.matmul(bank[:, 0:512],
                                       SB[:, 6144 + tt * 128:6144 + (tt + 1) * 128],
                                       SC[:, 4096 + oh * 512:4096 + (oh + 1) * 512],
                                       start=True, stop=True), "pe")
                    k += 1
        b.pe(wo_pe)
        pe_wm = b.mil("pe")

        def wo_dv(v, pe_wb=pe_wb):
            k = 0
            for tt in range(NT):
                for oh in range(2):
                    v.wait("pe", pe_wb + k + 1)
                    dst = xres[:, tt * D + oh * 512:tt * D + (oh + 1) * 512]
                    v.inc(v.tensor_tensor(dst, ps[0 + k % 2][:, 0:512], dst,
                                          ALU.add), "dv")
                    k += 1
        b.dv(wo_dv)

    b.x_dv = b.mil("dv")


def _bf_diag_vectors(up, dn):
    out = {}
    i = np.arange(4096)
    for tag, w in (("up", up), ("dn", dn)):
        dself = np.zeros((12, 4096), np.float32)
        dcross = np.zeros((12, 4096), np.float32)
        for s in range(12):
            s2 = 1 << s
            blk = i // (2 * s2)
            j = i % s2
            side = (i // s2) % 2
            wr = np.asarray(w[s], np.float32).reshape(4096 // (2 * s2), s2, 2, 2)
            dself[s] = wr[blk, j, side, side]
            dcross[s] = wr[blk, j, side, 1 - side]
        out[f"BF_{tag}_self"] = dself
        out[f"BF_{tag}_cross"] = dcross
    return out


def kernel(**inputs):
    cst = host_constants()
    nc = build_program(N_SUB)
    base = {k: np.ascontiguousarray(v) for k, v in cst.items()}
    base.update(_bf_diag_vectors(np.asarray(inputs["L1_ffn_up_weights"]),
                                 np.asarray(inputs["L1_ffn_down_weights"])))
    skip = {"L0_attn_x", "L1_ffn_up_weights", "L1_ffn_down_weights"}
    for nm, v in inputs.items():
        if nm in skip:
            continue
        a = np.ascontiguousarray(np.asarray(v, np.float32))
        if a.ndim == 1:
            a = a.reshape(1, -1)
        base[nm] = a
    x = np.ascontiguousarray(np.asarray(inputs["L0_attn_x"], np.float32))
    in_maps = [dict(base, x_in=np.ascontiguousarray(x[c])) for c in range(8)]
    res = run_bass_kernel_spmd(nc, in_maps, core_ids=list(range(8)))
    return np.stack([res.results[i]["y"] for i in range(8)], axis=0)
